# version 52
# baseline (speedup 1.0000x reference)
import contextlib

import numpy as np

import concourse.bass as bass
import concourse.mybir as mybir
from concourse.bass_utils import run_bass_kernel_spmd

F32 = mybir.dt.float32
BF16 = mybir.dt.bfloat16
AX = mybir.AxisListType.X
OP = mybir.AluOpType
ACT = mybir.ActivationFunctionType

T = 2048
D = 1024
NDC = 8
NTC = 4
TW = 512
NNT = 16
EPS = 1e-5
K1 = float(np.float32(1.0 - EPS / 128.0))
INV_D = 1.0 / D
HSCALE = float(32.0 ** -0.5)

DMA_SEMS = ("DX", "DX2", "DW", "DW2", "DR", "DR2", "DC", "DO", "DO2", "SC",
            "XA", "XB", "XC", "XD", "WQ", "WQ2")
ENG_SEM = {"v": "SV", "s": "SS", "t": "ST", "gp": "SG"}


class Plan:

    def __init__(self):
        self.q = {k: [] for k in ("sync", "gp", "v", "s", "t")}
        self.cnt = {s: 0 for s in ("SV", "SS", "ST", "SG") + DMA_SEMS}
        self.seen = {k: {} for k in self.q}
        self.reg = {}

    def _dep(self, qn, sem, val, waits):
        if sem in DMA_SEMS:
            val = self.cnt[sem]
        if val <= 0 or self.seen[qn].get(sem, 0) >= val:
            return
        self.seen[qn][sem] = val
        waits.append((sem, val))

    def emit(self, qn, fn, reads=(), writes=(), accum_writes=(), sem=None, inc=1):
        if sem is None:
            sem = ENG_SEM[qn]
        waits = []
        for r in reads:
            e = self.reg.get(r)
            if e:
                for s_, v_ in e["w"].items():
                    self._dep(qn, s_, v_, waits)
        for w in list(writes) + list(accum_writes):
            e = self.reg.get(w)
            if e:
                for s_, v_ in e["w"].items():
                    self._dep(qn, s_, v_, waits)
                for s_, v_ in e["r"].items():
                    self._dep(qn, s_, v_, waits)
        if sem in DMA_SEMS:
            self._dep(qn, sem, 1, waits)
        self.cnt[sem] += inc
        cnt = self.cnt[sem]
        self.q[qn].append((tuple(sorted(set(waits))), fn, sem, inc))
        for r in reads:
            e = self.reg.setdefault(r, {"w": {}, "r": {}})
            e["r"][sem] = cnt
        for w in writes:
            self.reg[w] = {"w": {sem: cnt}, "r": {}}
        for w in accum_writes:
            e = self.reg.setdefault(w, {"w": {}, "r": {}})
            e["w"][sem] = cnt


def _qk(prefix, a, b):
    return [f"{prefix}{i}" for i in range(NTC) if a < (i + 1) * TW and b > i * TW]


def build(debug=False, ncores=8):
    nc = bass.Bass("TRN2", target_bir_lowering=False)

    x_d = nc.dram_tensor("x", [T, D], F32, kind="ExternalInput").ap()
    wffn_d = nc.dram_tensor("wffn", [D, D], F32, kind="ExternalInput").ap()
    wqkv_d = nc.dram_tensor("wqkv", [D, 384], F32, kind="ExternalInput").ap()
    wout_d = nc.dram_tensor("wout", [128, D], F32, kind="ExternalInput").ap()
    vecs_d = nc.dram_tensor("vecs", [128, 40], F32, kind="ExternalInput").ap()
    y_d = nc.dram_tensor("y", [D, T], F32, kind="ExternalOutput").ap()

    cc_in = nc.dram_tensor("cc_in", [1, 16], F32, kind="Internal").ap()
    bc_d = nc.dram_tensor("bc_d", [1, 8], F32, kind="Internal").ap()
    cc_out = nc.dram_tensor("cc_out", [1, 16], F32, kind="Internal").ap()

    ident_d = nc.inline_tensor(np.eye(128, dtype=np.float32), name="ident_c").ap()
    lhsbf_np = np.zeros((128, 128), dtype=np.float32)
    lhsbf_np[[0, 1, 32, 33, 64, 65], :] = 1.0
    import ml_dtypes
    lhsbf_d = nc.inline_tensor(lhsbf_np.astype(ml_dtypes.bfloat16),
                               name="lhsbf_c").ap()
    ones_col_d = nc.inline_tensor(np.ones((128, 1), dtype=np.float32),
                                  name="onesc_c").ap()
    onesb_d = nc.inline_tensor(np.ones((128, 1), dtype=ml_dtypes.bfloat16),
                               name="onesb_c").ap()

    dbg = {}
    if debug:
        for name, shape in [("dbg_h1", [D, T]), ("dbg_h2", [D, T]),
                            ("dbg_q", [128, T]), ("dbg_k", [128, T]),
                            ("dbg_v", [128, T]), ("dbg_gamma", [1, 16])]:
            dbg[name] = nc.dram_tensor(name, shape, F32, kind="ExternalOutput").ap()

    es = contextlib.ExitStack()
    with es:
        sbt = lambda name, shape, dt=F32: es.enter_context(
            nc.sbuf_tensor(name, shape, dt))
        h = [sbt(f"h{i}", [128, T]) for i in range(NDC)]
        pA = [sbt(f"pA{i}", [128, T], BF16) for i in range(NDC)]
        pB = [sbt(f"pB{i}", [128, T], BF16) for i in range(NDC)]
        wq = [sbt(f"wq{i}", [128, D], BF16) for i in range(NDC)]
        wqgh = [sbt(f"wqgh{i}", [128, 384], BF16) for i in range(NDC)]
        wqgl = [sbt(f"wqgl{i}", [128, 384], BF16) for i in range(NDC)]
        wouth = sbt("wouth", [128, D], BF16)
        woutl = sbt("woutl", [128, D], BF16)
        scrq = sbt("scrq", [128, T])
        scrke = sbt("scrke", [128, T])
        scrv = sbt("scrv", [128, T])
        xstage = [sbt(f"xstage{i}", [128, D]) for i in range(2)]
        rowpool = sbt("rowpool", [128, T])
        s1s2 = sbt("s1s2", [2, 384])
        stats = sbt("stats", [128, 128])
        vecs = sbt("vecs_sb", [128, 40])
        mv = sbt("mv", [128, 22])
        gmaxrow = sbt("gmaxrow", [1, 16])
        srow = sbt("srow", [1, 16])
        ident = sbt("ident", [128, 128])
        lhsbf = sbt("lhsbf", [128, 128], BF16)
        rowbf = sbt("rowbf", [128, T], BF16)
        statsbf = sbt("statsbf", [128, 32], BF16)
        ones_col = sbt("ones_col", [128, 1])
        onesb = sbt("onesb", [128, 1], BF16)
        bdm = sbt("bdm", [128, 128])
        ps = [es.enter_context(nc.psum_tensor(f"ps{i}", [128, TW], F32))
              for i in range(8)]
        sems = {n: es.enter_context(nc.semaphore(n))
                for n in ("SV", "SS", "ST", "SG") + DMA_SEMS}
        block = es.enter_context(nc.Block())

        (ST_S, ST_Q, ST_M, ST_VAR, ST_RF, ST_IRF, ST_NMRF,
         ST_R2) = (slice(16 * i, 16 * (i + 1)) for i in range(8))
        ST_T0 = ST_NMRF
        ST_NMR2 = ST_Q
        ST_R2P1 = ST_S
        V_BOUT, V_GFF, V_BFF, V_GAT, V_BAT = (
            slice(8 * i, 8 * (i + 1)) for i in range(5))
        MV_AM = slice(0, 8)
        MV_LMAX = slice(8, 9)
        MV_THR = slice(9, 10)
        MV_NTHR = slice(10, 11)
        MV_SC = slice(11, 12)
        MV_CBC = slice(12, 13)
        MV_NCBC = slice(13, 14)
        MV_KNEG = slice(14, 15)
        MV_SSUM = slice(15, 16)
        MV_RS = slice(16, 17)
        MV_AWT = slice(17, 18)
        MV_CS = slice(18, 19)
        MV_WRS = slice(19, 20)
        MV_BS = slice(20, 21)

        P = Plan()
        hr = lambda dc, tc: f"h{dc}_{tc}"
        hq = lambda dc: [hr(dc, tc) for tc in range(NTC)]
        tsl = lambda tc: slice(tc * TW, (tc + 1) * TW)

        P.emit("sync", lambda e: e.dma_start(out=ident[:], in_=ident_d[:]),
               writes=["ident"], sem="DW", inc=16)
        P.emit("sync", lambda e: e.dma_start(out=lhsbf[:], in_=lhsbf_d[:]),
               writes=["lhsbf"], sem="DW2", inc=16)
        P.emit("sync", lambda e: e.dma_start(out=ones_col[:], in_=ones_col_d[:]),
               writes=["ones_col"], sem="DW", inc=16)
        P.emit("sync", lambda e: e.dma_start(out=onesb[:], in_=onesb_d[:]),
               writes=["onesb"], sem="DW2", inc=16)
        P.emit("sync", lambda e: e.dma_start(out=vecs[:], in_=vecs_d[:]),
               writes=["vecs"], sem="DW", inc=16)
        P.emit("sync", lambda e: e.dma_start(out=scrv[:, 0:D], in_=wout_d[:]),
               writes=["v0", "v1"], sem="DW", inc=16)
        P.emit("s", lambda e: e.copy(wouth[:], scrv[:, 0:D]),
               reads=["v0", "v1"], writes=["wouth"])
        P.emit("v", lambda e: e.tensor_tensor(woutl[:], scrv[:, 0:D], wouth[:],
                                              OP.subtract),
               reads=["v0", "v1", "wouth"], writes=["woutl"])
        P.emit("v", lambda e: e.tensor_reduce(mv[:, MV_WRS], scrv[:, 0:D], AX,
                                              OP.add),
               reads=["v0", "v1"], writes=["wrs"])
        P.emit("v", lambda e: e.memset(gmaxrow[0:1, :], 0.0), writes=["gmaxrow"])
        P.emit("sync", lambda e: e.dma_start(out=cc_in[:], in_=gmaxrow[0:1, :]),
               reads=["gmaxrow"], writes=["ccin"], sem="DC", inc=16)
        P.emit("gp", lambda e: e.collective_compute(
                   "AllReduce", OP.max, replica_groups=[list(range(ncores))],
                   ins=[cc_in], outs=[cc_out]),
               reads=["ccin"], writes=["ccout"], sem="SC", inc=1)
        P.emit("v", lambda e: e.memset(bdm[:], 0.0), writes=["bd"])

        def xbuf(nt):
            if nt < 8:
                i = nt % 2
                return xstage[i][:, :], [f"xst{i}"]
            idx = (nt - 8) % 4
            if idx == 0:
                return scrq[:, 0:D], ["q0", "q1"]
            if idx == 1:
                return scrq[:, D:2 * D], ["q2", "q3"]
            i = idx - 2
            return xstage[i][:, :], [f"xst{i}"]

        def xload(nt):
            buf, keys = xbuf(nt)
            if nt < 8:
                eng, sem = "gp", ("DX" if nt % 2 == 0 else "DX2")
            else:
                eng, sem = "sync", ("XA", "XB", "XC", "XD")[(nt - 8) % 4]
            P.emit(eng,
                   lambda e, nt=nt, buf=buf: e.dma_start(
                       out=buf, in_=x_d[nt * 128:(nt + 1) * 128, :]),
                   writes=keys, sem=sem, inc=16)
        xload(0)
        xload(1)

        def pass1_dc(dc):
            a = (dc % 2) * D
            keys = _qk("q", a, a + D)
            wpa = [f"pa{dc}_0", f"pa{dc}_1"]
            wpb = [f"pb{dc}_0", f"pb{dc}_1"]
            buf, rkeys = (scrke, ["e2", "e3"]) if dc % 2 == 0 \
                else (scrv, ["v2", "v3"])
            P.emit("sync",
                   lambda e, dc=dc, a=a: e.dma_start(
                       out=scrq[:, a:a + D], in_=wffn_d[dc * 128:(dc + 1) * 128, :]),
                   writes=keys, sem="DW" if dc % 2 == 0 else "DW2", inc=16)
            P.emit("s", lambda e, dc=dc, a=a: e.copy(pA[dc][:, 0:D],
                                                     scrq[:, a:a + D]),
                   reads=keys, writes=wpa)
            P.emit("v",
                   lambda e, dc=dc, a=a: e.tensor_reduce(
                       mv[:, MV_AM][:, dc:dc + 1], scrq[:, a:a + D], AX, OP.add,
                       apply_absolute_value=True),
                   reads=keys, writes=[f"am{dc}"])
            P.emit("v",
                   lambda e, dc=dc, a=a, buf=buf: e.tensor_tensor(
                       buf[:, D:2 * D], scrq[:, a:a + D], pA[dc][:, 0:D],
                       OP.subtract),
                   reads=keys + wpa, writes=rkeys)
            P.emit("s", lambda e, dc=dc, buf=buf: e.copy(
                       wq[dc][:, :], buf[:, D:2 * D]),
                   reads=rkeys, writes=[f"wq{dc}"])
            P.emit("v",
                   lambda e, dc=dc, buf=buf: e.tensor_tensor(
                       pB[dc][:, 0:D], buf[:, D:2 * D], wq[dc][:, :],
                       OP.subtract),
                   reads=rkeys + [f"wq{dc}"], writes=wpb)

        def beta_tail():
            P.emit("v", lambda e: e.tensor_reduce(mv[:, MV_AWT], mv[:, MV_AM], AX,
                                                  OP.add),
                   reads=[f"am{dc}" for dc in range(NDC)], writes=["awt"])
            P.emit("t", lambda e: e.transpose(ps[5][0:1, 0:128], mv[:, MV_AWT],
                                              ident[:]),
                   reads=["awt", "ident"], writes=["ps5"])
            P.emit("v", lambda e: e.tensor_reduce(srow[0:1, 0:1],
                                                  ps[5][0:1, 0:128], AX, OP.add),
                   reads=["ps5"], writes=["beta_s"])
            P.emit("v", lambda e: e.tensor_scalar(srow[0:1, 1:2], srow[0:1, 0:1],
                                                  1.0 / (D * D), None, OP.mult),
                   reads=["beta_s"], writes=["beta"])
            P.emit("v", lambda e: e.tensor_scalar(srow[0:1, 2:3], srow[0:1, 1:2],
                                                  EPS, 0.5, OP.add, OP.mult),
                   reads=["beta"], writes=["thr_s"])
            P.emit("v", lambda e: e.tensor_scalar(srow[0:1, 3:4], srow[0:1, 1:2],
                                                  0.5, None, OP.mult),
                   reads=["beta"], writes=["sc_s"])
            P.emit("v", lambda e: e.tensor_reduce(mv[:, MV_BS],
                                                  vecs[:, V_BOUT], AX, OP.add),
                   reads=["vecs"], writes=["bs_p"])
            P.emit("t", lambda e: e.transpose(ps[5][0:1, 0:128], mv[:, MV_BS],
                                              ident[:]),
                   reads=["bs_p", "ident"], writes=["ps5"])
            P.emit("v", lambda e: e.tensor_reduce(srow[0:1, 4:5],
                                                  ps[5][0:1, 0:128], AX, OP.add),
                   reads=["ps5"], writes=["bs_s"])
            P.emit("v", lambda e: e.tensor_scalar(srow[0:1, 4:5], srow[0:1, 4:5],
                                                  INV_D, None, OP.mult),
                   reads=["bs_s"], writes=["bs_s"])
            P.emit("sync", lambda e: e.dma_start(out=bc_d[0:1, 0:3],
                                                 in_=srow[0:1, 2:5]),
                   reads=["thr_s", "sc_s", "bs_s"], writes=["bc_d"],
                   sem="DC", inc=16)
            P.emit("sync", lambda e: e.dma_start(
                       out=mv[:, MV_BS],
                       in_=bc_d[0:1, 2:3].broadcast_to((128, 1))),
                   reads=["bc_d"], writes=["bs"], sem="DC", inc=16)
            P.emit("sync", lambda e: e.dma_start(
                       out=mv[:, MV_THR],
                       in_=bc_d[0:1, 0:1].broadcast_to((128, 1))),
                   reads=["bc_d"], writes=["thr"], sem="DC", inc=16)
            P.emit("sync", lambda e: e.dma_start(
                       out=mv[:, MV_SC],
                       in_=bc_d[0:1, 1:2].broadcast_to((128, 1))),
                   reads=["bc_d"], writes=["sc"], sem="DC", inc=16)
            P.emit("v", lambda e: e.tensor_scalar(mv[:, MV_NTHR], mv[:, MV_THR],
                                                  -1.0, None, OP.mult),
                   reads=["thr"], writes=["nthr"])

        def ternarize(dc):
            if True:
                buf, rkeys = (scrke, ["e2", "e3"]) if dc % 2 == 0 \
                    else (scrv, ["v2", "v3"])
                wpa = [f"pa{dc}_0", f"pa{dc}_1"]
                wpb = [f"pb{dc}_0", f"pb{dc}_1"]
                P.emit("gp",
                       lambda e, dc=dc, buf=buf: e.tensor_tensor(
                           buf[:, D:2 * D], pA[dc][:, 0:D], wq[dc][:, :],
                           OP.add),
                       reads=wpa + [f"wq{dc}"], writes=rkeys)
                P.emit("gp",
                       lambda e, dc=dc, buf=buf: e.tensor_tensor(
                           buf[:, D:2 * D], buf[:, D:2 * D], pB[dc][:, 0:D],
                           OP.add),
                       reads=rkeys + wpb, writes=rkeys)
                P.emit("v",
                       lambda e, dc=dc, buf=buf: e.tensor_scalar(
                           wq[dc][:, :], buf[:, D:2 * D], mv[:, MV_THR], -1.0,
                           OP.is_gt, OP.add),
                       reads=rkeys + ["thr", f"wq{dc}"], writes=[f"wq{dc}"])
                P.emit("v",
                       lambda e, dc=dc, buf=buf: e.scalar_tensor_tensor(
                           wq[dc][:, :], buf[:, D:2 * D], mv[:, MV_NTHR],
                           wq[dc][:, :], OP.is_gt, OP.add),
                       reads=rkeys + ["nthr", f"wq{dc}"], writes=[f"wq{dc}"])

        def wqkv_bundle(dc):
            a = (dc % 2) * 384
            akeys = _qk("v", a, a + 384)
            P.emit("sync",
                   lambda e, dc=dc, a=a: e.dma_start(
                       out=scrv[:, a:a + 384],
                       in_=wqkv_d[dc * 128:(dc + 1) * 128, :]),
                   writes=akeys, sem="WQ" if dc % 2 == 0 else "WQ2", inc=16)
            P.emit("s",
                   lambda e, dc=dc, a=a: e.activation(
                       rowpool[:, 0:384], scrv[:, a:a + 384], ACT.Copy,
                       scale=vecs[:, V_GAT][:, dc:dc + 1]),
                   reads=akeys + ["vecs", "wgf"], writes=["wgf"])
            P.emit("t",
                   lambda e, dc=dc: e.matmul(ps[6][0:1, 0:384], ones_col[:, 0:1],
                                             rowpool[:, 0:384],
                                             start=(dc == 0), stop=(dc == NDC - 1)),
                   reads=["wgf", "ones_col"],
                   writes=["ps6"] if dc == NDC - 1 else [],
                   accum_writes=["ps6"] if dc != NDC - 1 else [])
            P.emit("s", lambda e, dc=dc: e.copy(wqgh[dc][:, :],
                                                rowpool[:, 0:384]),
                   reads=["wgf"], writes=[f"wqgh{dc}"])
            P.emit("v", lambda e, dc=dc: e.tensor_tensor(
                       wqgl[dc][:, :], rowpool[:, 0:384], wqgh[dc][:, :],
                       OP.subtract),
                   reads=["wgf", f"wqgh{dc}"], writes=[f"wqgl{dc}"])
            P.emit("t",
                   lambda e, dc=dc, a=a: e.matmul(
                       ps[7][0:1, 0:384], vecs[:, V_BAT][:, dc:dc + 1],
                       scrv[:, a:a + 384], start=(dc == 0), stop=(dc == NDC - 1)),
                   reads=akeys + ["vecs"],
                   writes=["ps7"] if dc == NDC - 1 else [],
                   accum_writes=["ps7"] if dc != NDC - 1 else [])

        def wqkv_evacs():
            P.emit("v", lambda e: e.tensor_copy(s1s2[0:1, :], ps[6][0:1, 0:384]),
                   reads=["ps6"], writes=["s1"])
            P.emit("v", lambda e: e.tensor_copy(scrke[0:1, 1536:1920],
                                                ps[7][0:1, 0:384]),
                   reads=["ps7"], writes=["e3"])
            P.emit("gp", lambda e: e.dma_start(out=s1s2[1:2, :],
                                               in_=scrke[0:1, 1536:1920]),
                   reads=["e3"], writes=["s2"], sem="DR", inc=16)

        for nt in range(NNT):
            buf, keys = xbuf(nt)
            P.emit("v",
                   lambda e, nt=nt, buf=buf: e.tensor_reduce(
                       stats[:, ST_S][:, nt:nt + 1], buf, AX, OP.add),
                   reads=keys, accum_writes=["st_s"])
            P.emit("s",
                   lambda e, nt=nt, buf=buf: e.activation(
                       scrke[:, 0:D], buf, ACT.Square,
                       accum_out=stats[:, ST_Q][:, nt:nt + 1]),
                   reads=keys, writes=["e0", "e1"], accum_writes=["st_q"])
            for dc in range(NDC):
                slot = (nt * NDC + dc) % 6
                P.emit("t",
                       lambda e, buf=buf, dc=dc, slot=slot: e.transpose(
                           ps[slot][:, 0:128],
                           buf[:, dc * 128:(dc + 1) * 128], ident[:]),
                       reads=keys + ["ident"], writes=[f"ps{slot}"])
                if dc % 2 == 0:
                    P.emit("s",
                           lambda e, dc=dc, nt=nt, slot=slot: e.copy(
                               h[dc][:, nt::16], ps[slot][:, 0:128]),
                           reads=[f"ps{slot}"], accum_writes=hq(dc))
                else:
                    P.emit("v",
                           lambda e, dc=dc, nt=nt, slot=slot: e.tensor_copy(
                               h[dc][:, nt::16], ps[slot][:, 0:128]),
                           reads=[f"ps{slot}"], accum_writes=hq(dc))
            if nt + 2 < 8:
                xload(nt + 2)
            if nt < NDC:
                pass1_dc(nt)
            if nt == 7:
                beta_tail()
            if 2 <= nt < 2 + NDC:
                wqkv_bundle(nt - 2)
            if nt >= 8:
                ternarize(nt - 8)
            if 8 <= nt + 2 < NNT:
                xload(nt + 2)

        _dr_i = [0]

        def _dr():
            _dr_i[0] += 1
            return "DR" if _dr_i[0] % 2 == 0 else "DR2"

        def rowshuffle_out(pslot, field, fkey, rkey):
            P.emit("gp", lambda e: e.dma_start(
                       out=rowpool[pslot:pslot + 1, :].rearrange(
                           "o (p c) -> o p c", c=16),
                       in_=stats[:, field]),
                   reads=[fkey, "wgf"], writes=[rkey], sem=_dr(), inc=16)

        def rowshuffle_bf(pslot, field, bfield, fkey, rkey):
            b0 = slice(16 * bfield, 16 * bfield + 16)
            b1 = slice(16 * bfield + 16, 16 * bfield + 32)
            P.emit("v", lambda e: e.tensor_copy(statsbf[:, b0], stats[:, field]),
                   reads=[fkey], writes=[f"sbf{bfield}"])
            P.emit("v", lambda e: e.tensor_tensor(
                       statsbf[:, b1], stats[:, field], statsbf[:, b0],
                       OP.subtract),
                   reads=[fkey, f"sbf{bfield}"], writes=[f"sbf{bfield}l"])
            P.emit("gp", lambda e: e.dma_start(
                       out=rowbf[pslot:pslot + 1, :].rearrange(
                           "o (p c) -> o p c", c=16),
                       in_=statsbf[:, b0]),
                   reads=[f"sbf{bfield}"], writes=[rkey], sem=_dr(), inc=16)
            P.emit("gp", lambda e: e.dma_start(
                       out=rowbf[pslot + 1:pslot + 2, :].rearrange(
                           "o (p c) -> o p c", c=16),
                       in_=statsbf[:, b1]),
                   reads=[f"sbf{bfield}l"], accum_writes=[rkey], sem=_dr(), inc=16)

        def stats_math_common(mbias=False):
            if mbias:
                P.emit("v", lambda e: e.tensor_scalar(
                           stats[:, ST_M], stats[:, ST_S], INV_D,
                           mv[:, MV_BS], OP.mult, OP.add),
                       reads=["st_s", "bs"], writes=["st_m"])
            else:
                P.emit("v", lambda e: e.tensor_scalar(
                           stats[:, ST_M], stats[:, ST_S], INV_D, None,
                           OP.mult),
                       reads=["st_s"], writes=["st_m"])
            P.emit("v", lambda e: e.tensor_tensor(stats[:, ST_T0], stats[:, ST_M],
                                                  stats[:, ST_M], OP.mult),
                   reads=["st_m"], writes=["st_nmrf"])
            P.emit("v", lambda e: e.scalar_tensor_tensor(
                       stats[:, ST_VAR], stats[:, ST_Q], INV_D, stats[:, ST_T0],
                       OP.mult, OP.subtract),
                   reads=["st_q", "st_nmrf"], writes=["st_var"])

        def stats_math_ffn(mbias=False):
            stats_math_common(mbias)
            P.emit("v", lambda e: e.tensor_scalar(
                       stats[:, ST_IRF], stats[:, ST_VAR], 1.0 + 1e-3, 1e-8,
                       OP.mult, OP.add),
                   reads=["st_var"], writes=["st_irf"])
            P.emit("s", lambda e: e.activation(stats[:, ST_IRF], stats[:, ST_IRF],
                                               ACT.Ln),
                   reads=["st_irf"], writes=["st_irf"])
            P.emit("s", lambda e: e.activation(stats[:, ST_RF], stats[:, ST_IRF],
                                               ACT.Exp, scale=-0.5),
                   reads=["st_irf"], writes=["st_rf"])
            P.emit("v", lambda e: e.scalar_tensor_tensor(
                       stats[:, ST_NMRF], stats[:, ST_M], -1.0, stats[:, ST_RF],
                       OP.mult, OP.mult),
                   reads=["st_m", "st_rf"], writes=["st_nmrf"])
            rowshuffle_bf(32, ST_RF, 0, "st_rf", "row32")
            rowshuffle_bf(64, ST_NMRF, 0, "st_nmrf", "row64")

        def fm_stats(pairs=False, sum_skip=False, acc_sq=False):
            for dc in range(NDC):
                bb, bkeys = (scrq, [f"q{t}" for t in range(NTC)]) if dc % 2 == 0 \
                    else (scrke, [f"e{t}" for t in range(NTC)])
                if acc_sq:
                    pass
                elif dc in (5, 7):
                    P.emit("v", lambda e, dc=dc, bb=bb: e.tensor_tensor(
                               bb[:], h[dc][:], h[dc][:], OP.mult),
                           reads=hq(dc), writes=bkeys)
                elif dc in (4, 6):
                    P.emit("gp", lambda e, dc=dc, bb=bb: e.tensor_tensor(
                               bb[:], h[dc][:], h[dc][:], OP.mult),
                           reads=hq(dc), writes=bkeys)
                else:
                    P.emit("s", lambda e, dc=dc, bb=bb: e.activation(
                               bb[:], h[dc][:], ACT.Square),
                           reads=hq(dc), writes=bkeys)
                for tc in range(NTC):
                    if sum_skip:
                        pass
                    elif pairs:
                        def summ(e, dc=dc, tc=tc):
                            e.matmul(ps[tc][0:1, :], onesb[:, 0:1],
                                     pA[dc][:, tsl(tc)],
                                     start=(dc == 0), stop=False)
                            return e.matmul(ps[tc][0:1, :], onesb[:, 0:1],
                                            pB[dc][:, tsl(tc)],
                                            start=False, stop=(dc == NDC - 1))
                        P.emit("t", summ,
                               reads=[f"pa{dc}_{tc}", f"pb{dc}_{tc}", "onesb"],
                               writes=[f"ps{tc}"] if dc == NDC - 1 else [],
                               accum_writes=[f"ps{tc}"] if dc != NDC - 1 else [])
                    else:
                        P.emit("t",
                               lambda e, dc=dc, tc=tc: e.matmul(
                                   ps[tc][0:1, :], ones_col[:, 0:1],
                                   h[dc][:, tsl(tc)],
                                   start=(dc == 0), stop=(dc == NDC - 1)),
                               reads=[hr(dc, tc), "ones_col"],
                               writes=[f"ps{tc}"] if dc == NDC - 1 else [],
                               accum_writes=[f"ps{tc}"] if dc != NDC - 1 else [])
                    if not acc_sq:
                        P.emit("t",
                               lambda e, dc=dc, tc=tc, bb=bb: e.matmul(
                                   ps[4 + tc][0:1, :], ones_col[:, 0:1],
                                   bb[:, tsl(tc)], start=(dc == 0),
                                   stop=(dc == NDC - 1)),
                               reads=[bkeys[tc], "ones_col"],
                               writes=[f"ps{4 + tc}"] if dc == NDC - 1 else [],
                               accum_writes=[f"ps{4 + tc}"]
                               if dc != NDC - 1 else [])
            if acc_sq:
                for tc in range(NTC):
                    P.emit("t", lambda e, tc=tc: e.matmul(
                               ps[4 + tc][0:1, :], ones_col[:, 0:1],
                               scrq[:, tsl(tc)], start=True, stop=True),
                           reads=[f"q{tc}", "ones_col"],
                           writes=[f"ps{4 + tc}"])
            for tc in range(NTC):
                if not sum_skip:
                    P.emit("v", lambda e, tc=tc: e.tensor_copy(
                               rowpool[0:1, tsl(tc)], ps[tc][0:1, :]),
                           reads=[f"ps{tc}"], accum_writes=["row0"])
                P.emit("v", lambda e, tc=tc: e.tensor_copy(
                           rowpool[32:33, tsl(tc)], ps[4 + tc][0:1, :]),
                       reads=[f"ps{4 + tc}"], accum_writes=["row32"])
            P.emit("gp", lambda e: e.dma_start(
                       out=stats[:, ST_S],
                       in_=rowpool[0:1, :].rearrange("o (p c) -> o p c", c=16)),
                   reads=["row0"], writes=["st_s"], sem="DR2", inc=16)
            P.emit("gp", lambda e: e.dma_start(
                       out=stats[:, ST_Q],
                       in_=rowpool[32:33, :].rearrange("o (p c) -> o p c", c=16)),
                   reads=["row32"], writes=["st_q"], sem="DR", inc=16)

        def ffn(first, mid_cb=None, out_cb=None, split=False):
            for tc in range(NTC):
                P.emit("t", lambda e, tc=tc: e.matmul(
                           ps[tc][:, :], lhsbf[32:34, :], rowbf[32:34, tsl(tc)],
                           start=True, stop=True),
                       reads=["row32", "lhsbf"], writes=[f"ps{tc}"])
                bslot = 4 + (tc % 2)
                P.emit("t", lambda e, tc=tc, bslot=bslot: e.matmul(
                           ps[bslot][:, :], lhsbf[64:66, :], rowbf[64:66, tsl(tc)],
                           start=True, stop=True),
                       reads=["row64", "lhsbf"], writes=[f"ps{bslot}"])
                P.emit("v", lambda e, tc=tc, bslot=bslot: e.tensor_copy(
                           xstage[tc // 2][:, (tc % 2) * TW:(tc % 2) * TW + TW],
                           ps[bslot][:, :]),
                       reads=[f"ps{bslot}"], accum_writes=[f"xst{tc // 2}"])
            def prod_front(dc):
                bb, bkeys = (scrke, [f"e{t}" for t in range(NTC)]) if dc % 2 == 0 \
                    else (scrv, [f"v{t}" for t in range(NTC)])
                for tc in range(NTC):
                    P.emit("v", lambda e, dc=dc, tc=tc, bb=bb: e.tensor_tensor(
                               bb[:, tsl(tc)], h[dc][:, tsl(tc)],
                               ps[tc][:, :], OP.mult),
                           reads=[hr(dc, tc), f"ps{tc}"], writes=[bkeys[tc]])
                    if tc % 2 == 1:
                        P.emit("gp",
                               lambda e, tc=tc, bb=bb: e.tensor_tensor(
                                   bb[:, (tc - 1) * TW:(tc + 1) * TW],
                                   bb[:, (tc - 1) * TW:(tc + 1) * TW],
                                   xstage[tc // 2][:], OP.add),
                               reads=[bkeys[tc - 1], bkeys[tc], f"xst{tc // 2}"],
                               writes=[bkeys[tc - 1], bkeys[tc]])
            def prod_lo(dc):
                bb, bkeys = (scrke, [f"e{t}" for t in range(NTC)]) if dc % 2 == 0 \
                    else (scrv, [f"v{t}" for t in range(NTC)])
                P.emit("v",
                       lambda e, dc=dc, bb=bb: e.tensor_tensor(
                           pB[dc][:], bb[:], pA[dc][:], OP.subtract),
                       reads=bkeys + [f"pa{dc}_{t}" for t in range(NTC)],
                       writes=[f"pb{dc}_{t}" for t in range(NTC)])

            def prod_back(dc, with_lo=True):
                bb, bkeys = (scrke, [f"e{t}" for t in range(NTC)]) if dc % 2 == 0 \
                    else (scrv, [f"v{t}" for t in range(NTC)])
                P.emit("s", lambda e, dc=dc, bb=bb: e.mul(
                           bb[:], bb[:], vecs[:, V_GFF][:, dc:dc + 1]),
                       reads=bkeys + ["vecs"], writes=bkeys)
                P.emit("s", lambda e, dc=dc, bb=bb: e.add(
                           bb[:], bb[:], vecs[:, V_BFF][:, dc:dc + 1]),
                       reads=bkeys + ["vecs"], writes=bkeys)
                P.emit("v", lambda e, dc=dc, bb=bb: e.tensor_reduce(
                           mv[:, MV_AM][:, dc:dc + 1], bb[:], AX, OP.max,
                           apply_absolute_value=True),
                       reads=bkeys, writes=[f"am{dc}"])
                P.emit("s", lambda e, dc=dc, bb=bb: e.copy(pA[dc][:], bb[:]),
                       reads=bkeys, writes=[f"pa{dc}_{t}" for t in range(NTC)])
                if with_lo:
                    prod_lo(dc)

            for dc in range(NDC):
                prod_front(dc)
                if dc > 0:
                    prod_back(dc - 1)
            prod_back(NDC - 1, with_lo=False)
            def xr_slot(dc):
                if dc < 4:
                    return scrke, (dc % 4) * TW, f"e{dc % 4}"
                return scrv, (dc % 4) * TW, f"v{dc % 4}"

            def xr_stage(tc):
                for dc in range(NDC):
                    buf, xa, xrn = xr_slot(dc)
                    P.emit("gp", lambda e, dc=dc, tc=tc, buf=buf, xa=xa:
                           e.tensor_tensor(
                               buf[:, xa:xa + TW], pA[dc][:, tsl(tc)],
                               pB[dc][:, tsl(tc)], OP.add),
                           reads=[f"pa{dc}_{tc}", f"pb{dc}_{tc}"], writes=[xrn])

            def clip_stage(tc):
                for dc in range(NDC):
                    buf, xa, xrn = xr_slot(dc)
                    P.emit("v", lambda e, buf=buf, xa=xa: e.tensor_scalar(
                               buf[:, xa:xa + TW], buf[:, xa:xa + TW],
                               mv[:, MV_NCBC], mv[:, MV_CBC], OP.max, OP.min),
                           reads=[xrn, "ncbc", "cbc"], writes=[xrn])
                for dc in range(NDC):
                    buf, xa, xrn = xr_slot(dc)
                    P.emit("s", lambda e, dc=dc, tc=tc, buf=buf, xa=xa: e.copy(
                               pA[dc][:, tsl(tc)], buf[:, xa:xa + TW]),
                           reads=[xrn], writes=[f"pa{dc}_{tc}"])
                for dc in range(NDC):
                    buf, xa, xrn = xr_slot(dc)
                    P.emit("v", lambda e, dc=dc, tc=tc, buf=buf, xa=xa:
                           e.tensor_tensor(
                               pB[dc][:, tsl(tc)], buf[:, xa:xa + TW],
                               pA[dc][:, tsl(tc)], OP.subtract),
                           reads=[xrn, f"pa{dc}_{tc}"], writes=[f"pb{dc}_{tc}"])

            P.emit("v", lambda e: e.tensor_reduce(mv[:, MV_LMAX], mv[:, MV_AM],
                                                  AX, OP.max),
                   reads=[f"am{dc}" for dc in range(NDC)], writes=["lmax"])
            P.emit("t", lambda e: e.transpose(ps[0][0:1, 0:128], mv[:, MV_LMAX],
                                              ident[:]),
                   reads=["lmax", "ident"], writes=["ps0"])
            P.emit("v", lambda e: e.tensor_reduce(gmaxrow[0:1, 0:1],
                                                  ps[0][0:1, 0:128], AX, OP.max),
                   reads=["ps0"], writes=["gmaxrow"])
            P.emit("sync", lambda e: e.dma_start(out=cc_in[:], in_=gmaxrow[0:1, :]),
                   reads=["gmaxrow"], writes=["ccin"], sem="DC", inc=16)
            P.emit("gp", lambda e: e.collective_compute(
                       "AllReduce", OP.max, replica_groups=[list(range(ncores))],
                       ins=[cc_in], outs=[cc_out]),
                   reads=["ccin"], writes=["ccout"], sem="SC", inc=1)
            prod_lo(NDC - 1)
            xr_stage(0)
            if mid_cb is not None:
                mid_cb()
            if debug:
                P.emit("sync", lambda e: e.dma_start(out=gmaxrow[0:1, :],
                                                     in_=cc_out[:]),
                       reads=["ccout"], writes=["gmaxrow"], sem="DC", inc=16)
            if debug and first:
                P.emit("sync", lambda e: e.dma_start(out=dbg["dbg_gamma"],
                                                     in_=gmaxrow[0:1, :]),
                       reads=["gmaxrow"], sem="DO", inc=16)
            P.emit("sync", lambda e: e.dma_start(
                       out=mv[:, MV_CBC],
                       in_=cc_out[0:1, 0:1].broadcast_to((128, 1))),
                   reads=["ccout"], writes=["cbc"], sem="DC", inc=16)
            P.emit("v", lambda e: e.tensor_scalar(mv[:, MV_NCBC], mv[:, MV_CBC],
                                                  -K1, None, OP.mult),
                   reads=["cbc"], writes=["ncbc"])
            P.emit("v", lambda e: e.tensor_scalar(mv[:, MV_CBC], mv[:, MV_CBC],
                                                  K1, None, OP.mult),
                   reads=["cbc"], writes=["cbc"])

            clip_stage(0)
            for tc in range(NTC):
                for uc in range(NDC):
                    def mmgroup(e, uc=uc, tc=tc):
                        n = 0
                        last = None
                        for dc in range(NDC):
                            for pa in (pA, pB):
                                last = e.matmul(
                                    ps[uc][:, :],
                                    wq[dc][:, uc * 128:(uc + 1) * 128],
                                    pa[dc][:, tsl(tc)],
                                    start=(n == 0), stop=(n == 15))
                                n += 1
                        return last
                    P.emit("t", mmgroup,
                           reads=[f"pa{dc}_{tc}" for dc in range(NDC)] +
                                 [f"pb{dc}_{tc}" for dc in range(NDC)] +
                                 [f"wq{dc}" for dc in range(NDC)],
                           writes=[f"ps{uc}"])
                if tc + 1 < NTC:
                    xr_stage(tc + 1)
                    clip_stage(tc + 1)
                for uc in range(NDC):
                    P.emit("v", lambda e, uc=uc, tc=tc:
                           e.scalar_tensor_tensor(
                               h[uc][:, tsl(tc)], ps[uc][:, :], mv[:, MV_SC],
                               h[uc][:, tsl(tc)], OP.mult, OP.add),
                           reads=[f"ps{uc}", "sc", hr(uc, tc)],
                           writes=[hr(uc, tc)])
                    if out_cb is not None:
                        out_cb(uc, tc)
                    if split:
                        P.emit("s", lambda e, uc=uc, tc=tc: e.copy(
                                   pA[uc][:, tsl(tc)], h[uc][:, tsl(tc)]),
                               reads=[hr(uc, tc)], writes=[f"pa{uc}_{tc}"])
                        P.emit("v", lambda e, uc=uc, tc=tc: e.tensor_tensor(
                                   pB[uc][:, tsl(tc)], h[uc][:, tsl(tc)],
                                   pA[uc][:, tsl(tc)], OP.subtract),
                               reads=[hr(uc, tc), f"pa{uc}_{tc}"],
                               writes=[f"pb{uc}_{tc}"])
                        if uc == 0:
                            P.emit("s", lambda e, tc=tc: e.activation(
                                       scrq[:, tsl(tc)], h[0][:, tsl(tc)],
                                       ACT.Square),
                                   reads=[hr(0, tc)], writes=[f"q{tc}"])
                        else:
                            w = uc % 4
                            tmp = xstage[w // 2][:, (w % 2) * TW:
                                                 (w % 2) * TW + TW]
                            teng = ("v" if uc in (5, 7) else
                                    "gp" if uc in (2, 4, 6) else "s")
                            if teng == "s":
                                P.emit("s", lambda e, uc=uc, tc=tc, tmp=tmp:
                                       e.activation(tmp, h[uc][:, tsl(tc)],
                                                    ACT.Square),
                                       reads=[hr(uc, tc)],
                                       writes=[f"sqt{w}", f"xst{w // 2}"])
                            else:
                                P.emit(teng, lambda e, uc=uc, tc=tc, tmp=tmp:
                                       e.tensor_tensor(tmp, h[uc][:, tsl(tc)],
                                                       h[uc][:, tsl(tc)],
                                                       OP.mult),
                                       reads=[hr(uc, tc)],
                                       writes=[f"sqt{w}", f"xst{w // 2}"])
                            P.emit("v", lambda e, tc=tc, tmp=tmp:
                                   e.tensor_tensor(scrq[:, tsl(tc)],
                                                   scrq[:, tsl(tc)], tmp,
                                                   OP.add),
                                   reads=[f"q{tc}", f"sqt{w}"],
                                   writes=[f"q{tc}"])

        def stats_math_attn():
            stats_math_common()
            P.emit("v", lambda e: e.tensor_scalar(
                       stats[:, ST_R2], stats[:, ST_VAR], 1.0 + EPS, 1e-10,
                       OP.mult, OP.add),
                   reads=["st_var"], writes=["st_r2"])
            P.emit("s", lambda e: e.activation(stats[:, ST_R2], stats[:, ST_R2],
                                               ACT.Ln),
                   reads=["st_r2"], writes=["st_r2"])
            P.emit("s", lambda e: e.activation(stats[:, ST_IRF], stats[:, ST_R2],
                                               ACT.Exp, scale=0.5),
                   reads=["st_r2"], writes=["st_irf"])
            P.emit("s", lambda e: e.activation(stats[:, ST_RF], stats[:, ST_R2],
                                               ACT.Exp, scale=-0.5),
                   reads=["st_r2"], writes=["st_rf"])
            P.emit("v", lambda e: e.tensor_scalar(stats[:, ST_R2],
                                                  stats[:, ST_VAR], EPS, None,
                                                  OP.add),
                   reads=["st_var"], writes=["st_r2"])
            P.emit("s", lambda e: e.activation(stats[:, ST_R2], stats[:, ST_R2],
                                               ACT.Ln),
                   reads=["st_r2"], writes=["st_r2"])
            P.emit("s", lambda e: e.activation(stats[:, ST_R2], stats[:, ST_R2],
                                               ACT.Exp, scale=-0.5),
                   reads=["st_r2"], writes=["st_r2"])
            P.emit("v", lambda e: e.scalar_tensor_tensor(
                       stats[:, ST_NMR2], stats[:, ST_M], -1.0, stats[:, ST_R2],
                       OP.mult, OP.mult),
                   reads=["st_m", "st_r2"], writes=["st_nmr2"])
            P.emit("v", lambda e: e.tensor_scalar(stats[:, ST_R2P1],
                                                  stats[:, ST_R2], 1.0, None,
                                                  OP.add),
                   reads=["st_r2"], writes=["st_r2p1"])
            P.emit("v", lambda e: e.tensor_scalar(stats[:, ST_NMRF],
                                                  stats[:, ST_M], -1.0, None,
                                                  OP.mult),
                   reads=["st_m"], writes=["st_nmrf"])
            rowshuffle_out(0, ST_NMRF, "st_nmrf", "row0")
            rowshuffle_out(1, ST_IRF, "st_irf", "row1")
            rowshuffle_bf(32, ST_RF, 0, "st_rf", "row32")

        def attn():
            qbuf = {0: (scrq, "q"), 1: (scrke, "e"), 2: (scrv, "v")}
            slot_i = [0]
            for tc in range(NTC):
                bslot = 6 + (tc % 2)
                P.emit("t", lambda e, tc=tc, bslot=bslot: e.matmul(
                           ps[bslot][:, :], lhsbf[32:34, :],
                           rowbf[32:34, tsl(tc)], start=True, stop=True),
                       reads=["row32", "lhsbf"], writes=[f"ps{bslot}"])
                P.emit("v", lambda e, tc=tc, bslot=bslot: e.tensor_copy(
                           xstage[tc % 2][:, 0:TW], ps[bslot][:, :]),
                       reads=[f"ps{bslot}"], writes=[f"xst{tc % 2}"])
                for uc in range(3):
                    slot = slot_i[0] % 6
                    slot_i[0] += 1

                    def qkvgroup(e, uc=uc, tc=tc, slot=slot):
                        n = 0
                        for wg, pa in ((wqgh, pA), (wqgh, pB), (wqgl, pA)):
                            for dc in range(NDC):
                                e.matmul(ps[slot][:, :],
                                         wg[dc][:, uc * 128:(uc + 1) * 128],
                                         pa[dc][:, tsl(tc)],
                                         start=(n == 0), stop=False)
                                n += 1
                        return e.matmul(ps[slot][:, :],
                                        s1s2[0:2, uc * 128:(uc + 1) * 128],
                                        rowpool[0:2, tsl(tc)],
                                        start=False, stop=True)
                    P.emit("t", qkvgroup,
                           reads=[f"pa{dc}_{tc}" for dc in range(NDC)] +
                                 [f"pb{dc}_{tc}" for dc in range(NDC)] +
                                 [f"wqgh{dc}" for dc in range(NDC)] +
                                 [f"wqgl{dc}" for dc in range(NDC)] +
                                 ["s1", "s2", "row0", "row1"],
                           writes=[f"ps{slot}"])
                    buf, pre = qbuf[uc]
                    P.emit("v", lambda e, buf=buf, tc=tc, slot=slot:
                           e.tensor_tensor(buf[:, tsl(tc)], ps[slot][:, :],
                                           xstage[tc % 2][:, 0:TW], OP.mult),
                           reads=[f"ps{slot}", f"xst{tc % 2}"],
                           writes=[f"{pre}{tc}"])
            if debug:
                for pre, nmq, buf in (("q", "q", scrq), ("e", "k", scrke),
                                      ("v", "v", scrv)):
                    P.emit("sync", lambda e, nmq=nmq, buf=buf: e.dma_start(
                               out=dbg[f"dbg_{nmq}"], in_=buf[:]),
                           reads=[f"{pre}{tc}" for tc in range(NTC)],
                           sem="DO", inc=16)
            P.emit("v", lambda e: e.tensor_reduce(mv[:, MV_KNEG], scrke[:], AX,
                                                  OP.max, negate=True),
                   reads=[f"e{tc}" for tc in range(NTC)], writes=["kneg"])
            P.emit("s", lambda e: e.activation(scrke[:], scrke[:], ACT.Exp,
                                               bias=mv[:, MV_KNEG],
                                               accum_out=mv[:, MV_SSUM]),
                   reads=[f"e{tc}" for tc in range(NTC)] + ["kneg"],
                   writes=[f"e{tc}" for tc in range(NTC)] + ["ssum"])
            P.emit("v", lambda e: e.reciprocal(mv[:, MV_RS], mv[:, MV_SSUM]),
                   reads=["ssum"], writes=["rs"])
            P.emit("v", lambda e: e.tensor_scalar(mv[:, MV_CS], mv[:, MV_RS],
                                                  HSCALE, None, OP.mult),
                   reads=["rs"], writes=["cs"])
            for j in range(NNT):
                sl = slice(j * 128, (j + 1) * 128)
                tcj = (j * 128) // TW
                s0 = j % 4
                s1_ = 4 + (j % 2)
                P.emit("t", lambda e, sl=sl, s0=s0: e.transpose(
                           ps[s0][:, 0:128], scrke[:, sl], ident[:]),
                       reads=[f"e{tcj}", "ident"], writes=[f"ps{s0}"])
                P.emit("v", lambda e, sl=sl, s0=s0: e.tensor_copy(
                           scrke[:, sl], ps[s0][:, 0:128]),
                       reads=[f"ps{s0}"], accum_writes=[f"e{tcj}"])
                P.emit("t", lambda e, sl=sl, s1_=s1_: e.transpose(
                           ps[s1_][:, 0:128], scrv[:, sl], ident[:]),
                       reads=[f"v{tcj}", "ident"], writes=[f"ps{s1_}"])
                P.emit("s", lambda e, sl=sl, s1_=s1_: e.copy(
                           scrv[:, sl], ps[s1_][:, 0:128]),
                       reads=[f"ps{s1_}"], accum_writes=[f"v{tcj}"])

            def ctxgroup(e):
                last = None
                for j in range(NNT):
                    sl = slice(j * 128, (j + 1) * 128)
                    last = e.matmul(ps[6][:, 0:128], scrke[:, sl], scrv[:, sl],
                                    start=(j == 0), stop=(j == NNT - 1))
                return last
            P.emit("t", ctxgroup,
                   reads=[f"e{tc}" for tc in range(NTC)] +
                         [f"v{tc}" for tc in range(NTC)],
                   writes=["ps6"])
            for hd in range(4):
                b = 32 * hd
                P.emit("v", lambda e, b=b: e.tensor_scalar(
                           bdm[b:b + 32, b:b + 32], ps[6][b:b + 32, b:b + 32],
                           mv[:, MV_CS][b:b + 32, 0:1], None, OP.mult),
                       reads=["ps6", "cs"], accum_writes=["bd"])
            for tc in range(NTC):
                slot = tc % 2
                P.emit("t", lambda e, tc=tc, slot=slot: e.matmul(
                           ps[slot][:, :], bdm[:], scrq[:, tsl(tc)],
                           start=True, stop=True),
                       reads=["bd", f"q{tc}"], writes=[f"ps{slot}"])
                P.emit("v", lambda e, tc=tc, slot=slot: e.tensor_copy(
                           scrq[:, tsl(tc)], ps[slot][:, :]),
                       reads=[f"ps{slot}"], writes=[f"q{tc}"])
            for tc in range(NTC):
                P.emit("t", lambda e, tc=tc: e.matmul(
                           ps[0][0:1, :], mv[:, MV_WRS], scrq[:, tsl(tc)],
                           start=True, stop=True),
                       reads=["wrs", f"q{tc}"], writes=["ps0"])
                P.emit("v", lambda e, tc=tc: e.scalar_tensor_tensor(
                           rowpool[0:1, tsl(tc)], rowpool[0:1, tsl(tc)],
                           -float(D), ps[0][0:1, :], OP.mult, OP.add),
                       reads=["row0", "ps0"], writes=["row0"])
            rowshuffle_bf(32, ST_NMR2, 0, "st_nmr2", "row32")
            rowshuffle_bf(64, ST_R2P1, 0, "st_r2p1", "row64")
            for tc in range(NTC):
                slot = 2 + (tc % 2)
                P.emit("t", lambda e, tc=tc, slot=slot: e.matmul(
                           ps[slot][:, :], lhsbf[64:66, :],
                           rowbf[64:66, tsl(tc)], start=True, stop=True),
                       reads=["row64", "lhsbf"], writes=[f"ps{slot}"])
                P.emit("v", lambda e, tc=tc, slot=slot: e.tensor_copy(
                           scrke[:, tsl(tc)], ps[slot][:, :]),
                       reads=[f"ps{slot}"], writes=[f"e{tc}"])
            P.emit("s", lambda e: e.copy(pA[0][:], scrq[:]),
                   reads=[f"q{t}" for t in range(NTC)],
                   writes=[f"pa0_{t}" for t in range(NTC)])
            P.emit("v", lambda e: e.tensor_tensor(pB[0][:], scrq[:], pA[0][:],
                                                  OP.subtract),
                   reads=[f"q{t}" for t in range(NTC)] +
                         [f"pa0_{t}" for t in range(NTC)],
                   writes=[f"pb0_{t}" for t in range(NTC)])
            slot_i = [0]
            for tc in range(NTC):
                for uc in range(NDC):
                    slot = 4 + (slot_i[0] % 4)
                    slot_i[0] += 1

                    def opgroup(e, uc=uc, tc=tc, slot=slot):
                        sl = slice(uc * 128, (uc + 1) * 128)
                        e.matmul(ps[slot][:, :], wouth[:, sl],
                                 pA[0][:, tsl(tc)], start=True, stop=False)
                        e.matmul(ps[slot][:, :], wouth[:, sl],
                                 pB[0][:, tsl(tc)], start=False, stop=False)
                        e.matmul(ps[slot][:, :], woutl[:, sl],
                                 pA[0][:, tsl(tc)], start=False, stop=False)
                        return e.matmul(ps[slot][:, :], lhsbf[32:34, 0:128],
                                        rowbf[32:34, tsl(tc)],
                                        start=False, stop=True)
                    P.emit("t", opgroup,
                           reads=["wouth", "woutl", f"pa0_{tc}", f"pb0_{tc}",
                                  "row32", "lhsbf"],
                           writes=[f"ps{slot}"])
                    va = (uc % 4) * TW
                    P.emit("v", lambda e, uc=uc, tc=tc, va=va: e.tensor_tensor(
                               scrv[:, va:va + TW], h[uc][:, tsl(tc)],
                               scrke[:, tsl(tc)], OP.mult),
                           reads=[hr(uc, tc), f"e{tc}"], writes=[f"v{uc % 4}"])
                    P.emit("v", lambda e, uc=uc, tc=tc, slot=slot, va=va:
                           e.scalar_tensor_tensor(
                               h[uc][:, tsl(tc)], ps[slot][:, :],
                               vecs[:, V_BOUT][:, uc:uc + 1], scrv[:, va:va + TW],
                               OP.add, OP.add),
                           reads=[f"ps{slot}", "vecs", f"v{uc % 4}"],
                           writes=[hr(uc, tc)])
                    if uc == 0:
                        P.emit("s", lambda e, tc=tc: e.activation(
                                   scrq[:, tsl(tc)], h[0][:, tsl(tc)],
                                   ACT.Square),
                               reads=[hr(0, tc)], writes=[f"q{tc}"])
                    else:
                        w = uc % 4
                        tmp = xstage[w // 2][:, (w % 2) * TW:(w % 2) * TW + TW]
                        if uc in (2, 4, 6):
                            P.emit("gp", lambda e, uc=uc, tc=tc, tmp=tmp:
                                   e.tensor_tensor(tmp, h[uc][:, tsl(tc)],
                                                   h[uc][:, tsl(tc)], OP.mult),
                                   reads=[hr(uc, tc)],
                                   writes=[f"sqt{w}", f"xst{w // 2}"])
                        else:
                            P.emit("s", lambda e, uc=uc, tc=tc, tmp=tmp:
                                   e.activation(tmp, h[uc][:, tsl(tc)],
                                                ACT.Square),
                                   reads=[hr(uc, tc)],
                                   writes=[f"sqt{w}", f"xst{w // 2}"])
                        P.emit("v", lambda e, tc=tc, tmp=tmp:
                               e.tensor_tensor(scrq[:, tsl(tc)],
                                               scrq[:, tsl(tc)], tmp, OP.add),
                               reads=[f"q{tc}", f"sqt{w}"], writes=[f"q{tc}"])

        stats_math_ffn()
        wqkv_evacs()
        ffn(first=True, split=True)
        if debug:
            for dc in range(NDC):
                P.emit("sync", lambda e, dc=dc: e.dma_start(
                           out=dbg["dbg_h1"][dc * 128:(dc + 1) * 128, :],
                           in_=h[dc][:]),
                       reads=hq(dc), sem="DO", inc=16)
        fm_stats(pairs=True, acc_sq=True)
        stats_math_attn()
        attn()
        if debug:
            for dc in range(NDC):
                P.emit("sync", lambda e, dc=dc: e.dma_start(
                           out=dbg["dbg_h2"][dc * 128:(dc + 1) * 128, :],
                           in_=h[dc][:]),
                       reads=hq(dc), sem="DO", inc=16)
        fm_stats(sum_skip=True, acc_sq=True)
        stats_math_ffn(mbias=True)

        Y_SEMS = ("DO", "DO2", "XA", "XB")
        _yn = [0]

        def ydma(uc, tc):
            sem = Y_SEMS[_yn[0] % 4]
            _yn[0] += 1
            P.emit("sync", lambda e, uc=uc, tc=tc: e.dma_start(
                       out=y_d[uc * 128:(uc + 1) * 128, tc * TW:(tc + 1) * TW],
                       in_=h[uc][:, tsl(tc)]),
                   reads=[hr(uc, tc)], writes=[f"y{uc}_{tc}"], sem=sem, inc=16)

        ffn(first=False, out_cb=ydma)

        final_y = {s: P.cnt[s] for s in Y_SEMS}

        def make_runner(qname):
            items = P.q[qname]

            def run(eng):
                for waits, fn, sem, inc in items:
                    for s_, v_ in waits:
                        eng.wait_ge(sems[s_], v_)
                    last = fn(eng)
                    if last is not None:
                        last.then_inc(sems[sem], inc)
                if qname == "sync":
                    for s_, v_ in final_y.items():
                        eng.wait_ge(sems[s_], v_)
            return run

        block.sync(make_runner("sync"))
        block.gpsimd(make_runner("gp"))
        block.vector(make_runner("v"))
        block.scalar(make_runner("s"))
        block.tensor(make_runner("t"))

    return nc


_CACHE = {}


def _get_nc(debug=False, ncores=8):
    key = (debug, ncores)
    if key not in _CACHE:
        _CACHE[key] = build(debug, ncores)
    return _CACHE[key]


def _pack_vec(v):
    return np.ascontiguousarray(np.asarray(v, np.float32).reshape(8, 128).T)


def make_in_maps(L0_attn_x, L0_attn_gamma, L0_attn_beta, L0_attn_w_qkv,
                 L0_attn_w_out, L0_attn_b_out, L0_ffn_weight, L0_ffn_ln_gamma,
                 L0_ffn_ln_beta):
    x = np.asarray(L0_attn_x, np.float32)
    vecs = np.concatenate([
        _pack_vec(L0_attn_b_out), _pack_vec(L0_ffn_ln_gamma),
        _pack_vec(L0_ffn_ln_beta),
        _pack_vec(np.asarray(L0_attn_gamma).reshape(-1)),
        _pack_vec(np.asarray(L0_attn_beta).reshape(-1))], axis=1)
    vecs = np.ascontiguousarray(vecs)
    wffn = np.ascontiguousarray(np.asarray(L0_ffn_weight, np.float32))
    wqkv = np.ascontiguousarray(np.asarray(L0_attn_w_qkv, np.float32))
    wout = np.ascontiguousarray(np.asarray(L0_attn_w_out, np.float32))
    return [{"x": np.ascontiguousarray(x[c]), "wffn": wffn, "wqkv": wqkv,
             "wout": wout, "vecs": vecs} for c in range(8)]


def kernel(L0_attn_x, L0_attn_gamma, L0_attn_beta, L0_attn_w_qkv, L0_attn_w_out,
           L0_attn_b_out, L0_ffn_weight, L0_ffn_ln_gamma, L0_ffn_ln_beta):
    nc = _get_nc(False)
    in_maps = make_in_maps(L0_attn_x, L0_attn_gamma, L0_attn_beta, L0_attn_w_qkv,
                           L0_attn_w_out, L0_attn_b_out, L0_ffn_weight,
                           L0_ffn_ln_gamma, L0_ffn_ln_beta)
    res = run_bass_kernel_spmd(nc, in_maps, core_ids=list(range(8)))
    return np.stack(
        [np.asarray(r["y"]).reshape(D, 128, 16).transpose(2, 1, 0)
         .reshape(T, D) for r in res.results], axis=0)


# version 53 (final)
# speedup vs baseline: 1.0072x; 1.0072x over previous
import contextlib

import numpy as np

import concourse.bass as bass
import concourse.mybir as mybir
from concourse.bass_utils import run_bass_kernel_spmd

F32 = mybir.dt.float32
BF16 = mybir.dt.bfloat16
AX = mybir.AxisListType.X
OP = mybir.AluOpType
ACT = mybir.ActivationFunctionType

T = 2048
D = 1024
NDC = 8
NTC = 4
TW = 512
NNT = 16
EPS = 1e-5
K1 = float(np.float32(1.0 - EPS / 128.0))
INV_D = 1.0 / D
HSCALE = float(32.0 ** -0.5)

DMA_SEMS = ("DX", "DX2", "DW", "DW2", "DR", "DR2", "DC", "DO", "DO2", "SC",
            "XA", "XB", "XC", "XD", "WQ", "WQ2")
ENG_SEM = {"v": "SV", "s": "SS", "t": "ST", "gp": "SG"}


class Plan:

    def __init__(self):
        self.q = {k: [] for k in ("sync", "gp", "v", "s", "t")}
        self.cnt = {s: 0 for s in ("SV", "SS", "ST", "SG") + DMA_SEMS}
        self.seen = {k: {} for k in self.q}
        self.reg = {}

    def _dep(self, qn, sem, val, waits):
        if sem in DMA_SEMS:
            val = self.cnt[sem]
        if val <= 0 or self.seen[qn].get(sem, 0) >= val:
            return
        self.seen[qn][sem] = val
        waits.append((sem, val))

    def emit(self, qn, fn, reads=(), writes=(), accum_writes=(), sem=None, inc=1):
        if sem is None:
            sem = ENG_SEM[qn]
        waits = []
        for r in reads:
            e = self.reg.get(r)
            if e:
                for s_, v_ in e["w"].items():
                    self._dep(qn, s_, v_, waits)
        for w in list(writes) + list(accum_writes):
            e = self.reg.get(w)
            if e:
                for s_, v_ in e["w"].items():
                    self._dep(qn, s_, v_, waits)
                for s_, v_ in e["r"].items():
                    self._dep(qn, s_, v_, waits)
        if sem in DMA_SEMS:
            self._dep(qn, sem, 1, waits)
        self.cnt[sem] += inc
        cnt = self.cnt[sem]
        self.q[qn].append((tuple(sorted(set(waits))), fn, sem, inc))
        for r in reads:
            e = self.reg.setdefault(r, {"w": {}, "r": {}})
            e["r"][sem] = cnt
        for w in writes:
            self.reg[w] = {"w": {sem: cnt}, "r": {}}
        for w in accum_writes:
            e = self.reg.setdefault(w, {"w": {}, "r": {}})
            e["w"][sem] = cnt


def _qk(prefix, a, b):
    return [f"{prefix}{i}" for i in range(NTC) if a < (i + 1) * TW and b > i * TW]


def build(debug=False, ncores=8):
    nc = bass.Bass("TRN2", target_bir_lowering=False)

    x_d = nc.dram_tensor("x", [T, D], F32, kind="ExternalInput").ap()
    wffn_d = nc.dram_tensor("wffn", [D, D], F32, kind="ExternalInput").ap()
    wqkv_d = nc.dram_tensor("wqkv", [D, 384], F32, kind="ExternalInput").ap()
    wout_d = nc.dram_tensor("wout", [128, D], F32, kind="ExternalInput").ap()
    vecs_d = nc.dram_tensor("vecs", [128, 40], F32, kind="ExternalInput").ap()
    y_d = nc.dram_tensor("y", [D, T], F32, kind="ExternalOutput").ap()

    cc_in = nc.dram_tensor("cc_in", [1, 16], F32, kind="Internal").ap()
    bc_d = nc.dram_tensor("bc_d", [1, 8], F32, kind="Internal").ap()
    cc_out = nc.dram_tensor("cc_out", [1, 16], F32, kind="Internal").ap()

    ident_d = nc.inline_tensor(np.eye(128, dtype=np.float32), name="ident_c").ap()
    lhsbf_np = np.zeros((128, 128), dtype=np.float32)
    lhsbf_np[[0, 1, 32, 33, 64, 65], :] = 1.0
    import ml_dtypes
    lhsbf_d = nc.inline_tensor(lhsbf_np.astype(ml_dtypes.bfloat16),
                               name="lhsbf_c").ap()
    ones_col_d = nc.inline_tensor(np.ones((128, 1), dtype=np.float32),
                                  name="onesc_c").ap()
    onesb_d = nc.inline_tensor(np.ones((128, 1), dtype=ml_dtypes.bfloat16),
                               name="onesb_c").ap()

    dbg = {}
    if debug:
        for name, shape in [("dbg_h1", [D, T]), ("dbg_h2", [D, T]),
                            ("dbg_q", [128, T]), ("dbg_k", [128, T]),
                            ("dbg_v", [128, T]), ("dbg_gamma", [1, 16])]:
            dbg[name] = nc.dram_tensor(name, shape, F32, kind="ExternalOutput").ap()

    es = contextlib.ExitStack()
    with es:
        sbt = lambda name, shape, dt=F32: es.enter_context(
            nc.sbuf_tensor(name, shape, dt))
        h = [sbt(f"h{i}", [128, T]) for i in range(NDC)]
        pA = [sbt(f"pA{i}", [128, T], BF16) for i in range(NDC)]
        pB = [sbt(f"pB{i}", [128, T], BF16) for i in range(NDC)]
        wq = [sbt(f"wq{i}", [128, D], BF16) for i in range(NDC)]
        wqgh = [sbt(f"wqgh{i}", [128, 384], BF16) for i in range(NDC)]
        wqgl = [sbt(f"wqgl{i}", [128, 384], BF16) for i in range(NDC)]
        wouth = sbt("wouth", [128, D], BF16)
        woutl = sbt("woutl", [128, D], BF16)
        scrq = sbt("scrq", [128, T])
        scrke = sbt("scrke", [128, T])
        scrv = sbt("scrv", [128, T])
        xstage = [sbt(f"xstage{i}", [128, D]) for i in range(2)]
        rowpool = sbt("rowpool", [128, T])
        s1s2 = sbt("s1s2", [2, 384])
        stats = sbt("stats", [128, 128])
        vecs = sbt("vecs_sb", [128, 40])
        mv = sbt("mv", [128, 22])
        gmaxrow = sbt("gmaxrow", [1, 16])
        srow = sbt("srow", [1, 16])
        ident = sbt("ident", [128, 128])
        lhsbf = sbt("lhsbf", [128, 128], BF16)
        rowbf = sbt("rowbf", [128, T], BF16)
        statsbf = sbt("statsbf", [128, 32], BF16)
        ones_col = sbt("ones_col", [128, 1])
        onesb = sbt("onesb", [128, 1], BF16)
        bdm = sbt("bdm", [128, 128])
        ps = [es.enter_context(nc.psum_tensor(f"ps{i}", [128, TW], F32))
              for i in range(8)]
        sems = {n: es.enter_context(nc.semaphore(n))
                for n in ("SV", "SS", "ST", "SG") + DMA_SEMS}
        block = es.enter_context(nc.Block())

        (ST_S, ST_Q, ST_M, ST_VAR, ST_RF, ST_IRF, ST_NMRF,
         ST_R2) = (slice(16 * i, 16 * (i + 1)) for i in range(8))
        ST_T0 = ST_NMRF
        ST_NMR2 = ST_Q
        ST_R2P1 = ST_S
        V_BOUT, V_GFF, V_BFF, V_GAT, V_BAT = (
            slice(8 * i, 8 * (i + 1)) for i in range(5))
        MV_AM = slice(0, 8)
        MV_LMAX = slice(8, 9)
        MV_THR = slice(9, 10)
        MV_NTHR = slice(10, 11)
        MV_SC = slice(11, 12)
        MV_CBC = slice(12, 13)
        MV_NCBC = slice(13, 14)
        MV_KNEG = slice(14, 15)
        MV_SSUM = slice(15, 16)
        MV_RS = slice(16, 17)
        MV_AWT = slice(17, 18)
        MV_CS = slice(18, 19)
        MV_WRS = slice(19, 20)
        MV_BS = slice(20, 21)

        P = Plan()
        hr = lambda dc, tc: f"h{dc}_{tc}"
        hq = lambda dc: [hr(dc, tc) for tc in range(NTC)]
        tsl = lambda tc: slice(tc * TW, (tc + 1) * TW)

        P.emit("sync", lambda e: e.dma_start(out=ident[:], in_=ident_d[:]),
               writes=["ident"], sem="DW", inc=16)
        P.emit("sync", lambda e: e.dma_start(out=lhsbf[:], in_=lhsbf_d[:]),
               writes=["lhsbf"], sem="DW2", inc=16)
        P.emit("sync", lambda e: e.dma_start(out=ones_col[:], in_=ones_col_d[:]),
               writes=["ones_col"], sem="DW", inc=16)
        P.emit("sync", lambda e: e.dma_start(out=onesb[:], in_=onesb_d[:]),
               writes=["onesb"], sem="DW2", inc=16)
        P.emit("sync", lambda e: e.dma_start(out=vecs[:], in_=vecs_d[:]),
               writes=["vecs"], sem="DW", inc=16)
        P.emit("sync", lambda e: e.dma_start(out=scrv[:, 0:D], in_=wout_d[:]),
               writes=["v0", "v1"], sem="DW", inc=16)
        P.emit("s", lambda e: e.copy(wouth[:], scrv[:, 0:D]),
               reads=["v0", "v1"], writes=["wouth"])
        P.emit("v", lambda e: e.tensor_tensor(woutl[:], scrv[:, 0:D], wouth[:],
                                              OP.subtract),
               reads=["v0", "v1", "wouth"], writes=["woutl"])
        P.emit("v", lambda e: e.tensor_reduce(mv[:, MV_WRS], scrv[:, 0:D], AX,
                                              OP.add),
               reads=["v0", "v1"], writes=["wrs"])
        P.emit("v", lambda e: e.memset(gmaxrow[0:1, :], 0.0), writes=["gmaxrow"])
        P.emit("sync", lambda e: e.dma_start(out=cc_in[:], in_=gmaxrow[0:1, :]),
               reads=["gmaxrow"], writes=["ccin"], sem="DC", inc=16)
        P.emit("gp", lambda e: e.collective_compute(
                   "AllReduce", OP.max, replica_groups=[list(range(ncores))],
                   ins=[cc_in], outs=[cc_out]),
               reads=["ccin"], writes=["ccout"], sem="SC", inc=1)
        P.emit("v", lambda e: e.memset(bdm[:], 0.0), writes=["bd"])

        def xbuf(nt):
            if nt < 8:
                i = nt % 3
                if i == 2:
                    return rowpool[:, D:2 * D], ["xrp"]
                return xstage[i][:, :], [f"xst{i}"]
            idx = (nt - 8) % 4
            if idx == 0:
                return scrq[:, 0:D], ["q0", "q1"]
            if idx == 1:
                return scrq[:, D:2 * D], ["q2", "q3"]
            i = idx - 2
            return xstage[i][:, :], [f"xst{i}"]

        def xload(nt):
            buf, keys = xbuf(nt)
            if nt < 8:
                eng, sem = "gp", ("DX", "DX2", "DR")[nt % 3]
            else:
                eng, sem = "sync", ("XA", "XB", "XC", "XD")[(nt - 8) % 4]
            P.emit(eng,
                   lambda e, nt=nt, buf=buf: e.dma_start(
                       out=buf, in_=x_d[nt * 128:(nt + 1) * 128, :]),
                   writes=keys, sem=sem, inc=16)
        xload(0)
        xload(1)

        def pass1_dc(dc):
            a = (dc % 2) * D
            keys = _qk("q", a, a + D)
            wpa = [f"pa{dc}_0", f"pa{dc}_1"]
            wpb = [f"pb{dc}_0", f"pb{dc}_1"]
            buf, rkeys = (scrke, ["e2", "e3"]) if dc % 2 == 0 \
                else (scrv, ["v2", "v3"])
            P.emit("sync",
                   lambda e, dc=dc, a=a: e.dma_start(
                       out=scrq[:, a:a + D], in_=wffn_d[dc * 128:(dc + 1) * 128, :]),
                   writes=keys, sem="DW" if dc % 2 == 0 else "DW2", inc=16)
            P.emit("s", lambda e, dc=dc, a=a: e.copy(pA[dc][:, 0:D],
                                                     scrq[:, a:a + D]),
                   reads=keys, writes=wpa)
            P.emit("v",
                   lambda e, dc=dc, a=a: e.tensor_reduce(
                       mv[:, MV_AM][:, dc:dc + 1], scrq[:, a:a + D], AX, OP.add,
                       apply_absolute_value=True),
                   reads=keys, writes=[f"am{dc}"])
            P.emit("v",
                   lambda e, dc=dc, a=a, buf=buf: e.tensor_tensor(
                       buf[:, D:2 * D], scrq[:, a:a + D], pA[dc][:, 0:D],
                       OP.subtract),
                   reads=keys + wpa, writes=rkeys)
            P.emit("s", lambda e, dc=dc, buf=buf: e.copy(
                       wq[dc][:, :], buf[:, D:2 * D]),
                   reads=rkeys, writes=[f"wq{dc}"])
            P.emit("v",
                   lambda e, dc=dc, buf=buf: e.tensor_tensor(
                       pB[dc][:, 0:D], buf[:, D:2 * D], wq[dc][:, :],
                       OP.subtract),
                   reads=rkeys + [f"wq{dc}"], writes=wpb)

        def beta_tail():
            P.emit("v", lambda e: e.tensor_reduce(mv[:, MV_AWT], mv[:, MV_AM], AX,
                                                  OP.add),
                   reads=[f"am{dc}" for dc in range(NDC)], writes=["awt"])
            P.emit("t", lambda e: e.transpose(ps[6][0:1, 0:128], mv[:, MV_AWT],
                                              ident[:]),
                   reads=["awt", "ident"], writes=["ps6"])
            P.emit("v", lambda e: e.tensor_reduce(srow[0:1, 0:1],
                                                  ps[6][0:1, 0:128], AX, OP.add),
                   reads=["ps6"], writes=["beta_s"])
            P.emit("v", lambda e: e.tensor_scalar(srow[0:1, 1:2], srow[0:1, 0:1],
                                                  1.0 / (D * D), None, OP.mult),
                   reads=["beta_s"], writes=["beta"])
            P.emit("v", lambda e: e.tensor_scalar(srow[0:1, 2:3], srow[0:1, 1:2],
                                                  EPS, 0.5, OP.add, OP.mult),
                   reads=["beta"], writes=["thr_s"])
            P.emit("v", lambda e: e.tensor_scalar(srow[0:1, 3:4], srow[0:1, 1:2],
                                                  0.5, None, OP.mult),
                   reads=["beta"], writes=["sc_s"])
            P.emit("v", lambda e: e.tensor_reduce(mv[:, MV_BS],
                                                  vecs[:, V_BOUT], AX, OP.add),
                   reads=["vecs"], writes=["bs_p"])
            P.emit("t", lambda e: e.transpose(ps[6][0:1, 0:128], mv[:, MV_BS],
                                              ident[:]),
                   reads=["bs_p", "ident"], writes=["ps6"])
            P.emit("v", lambda e: e.tensor_reduce(srow[0:1, 4:5],
                                                  ps[6][0:1, 0:128], AX, OP.add),
                   reads=["ps6"], writes=["bs_s"])
            P.emit("v", lambda e: e.tensor_scalar(srow[0:1, 4:5], srow[0:1, 4:5],
                                                  INV_D, None, OP.mult),
                   reads=["bs_s"], writes=["bs_s"])
            P.emit("sync", lambda e: e.dma_start(out=bc_d[0:1, 0:3],
                                                 in_=srow[0:1, 2:5]),
                   reads=["thr_s", "sc_s", "bs_s"], writes=["bc_d"],
                   sem="DC", inc=16)
            P.emit("sync", lambda e: e.dma_start(
                       out=mv[:, MV_BS],
                       in_=bc_d[0:1, 2:3].broadcast_to((128, 1))),
                   reads=["bc_d"], writes=["bs"], sem="DC", inc=16)
            P.emit("sync", lambda e: e.dma_start(
                       out=mv[:, MV_THR],
                       in_=bc_d[0:1, 0:1].broadcast_to((128, 1))),
                   reads=["bc_d"], writes=["thr"], sem="DC", inc=16)
            P.emit("sync", lambda e: e.dma_start(
                       out=mv[:, MV_SC],
                       in_=bc_d[0:1, 1:2].broadcast_to((128, 1))),
                   reads=["bc_d"], writes=["sc"], sem="DC", inc=16)
            P.emit("v", lambda e: e.tensor_scalar(mv[:, MV_NTHR], mv[:, MV_THR],
                                                  -1.0, None, OP.mult),
                   reads=["thr"], writes=["nthr"])

        def ternarize(dc):
            if True:
                buf, rkeys = (scrke, ["e2", "e3"]) if dc % 2 == 0 \
                    else (scrv, ["v2", "v3"])
                wpa = [f"pa{dc}_0", f"pa{dc}_1"]
                wpb = [f"pb{dc}_0", f"pb{dc}_1"]
                P.emit("gp",
                       lambda e, dc=dc, buf=buf: e.tensor_tensor(
                           buf[:, D:2 * D], pA[dc][:, 0:D], wq[dc][:, :],
                           OP.add),
                       reads=wpa + [f"wq{dc}"], writes=rkeys)
                P.emit("gp",
                       lambda e, dc=dc, buf=buf: e.tensor_tensor(
                           buf[:, D:2 * D], buf[:, D:2 * D], pB[dc][:, 0:D],
                           OP.add),
                       reads=rkeys + wpb, writes=rkeys)
                P.emit("v",
                       lambda e, dc=dc, buf=buf: e.tensor_scalar(
                           wq[dc][:, :], buf[:, D:2 * D], mv[:, MV_THR], -1.0,
                           OP.is_gt, OP.add),
                       reads=rkeys + ["thr", f"wq{dc}"], writes=[f"wq{dc}"])
                P.emit("v",
                       lambda e, dc=dc, buf=buf: e.scalar_tensor_tensor(
                           wq[dc][:, :], buf[:, D:2 * D], mv[:, MV_NTHR],
                           wq[dc][:, :], OP.is_gt, OP.add),
                       reads=rkeys + ["nthr", f"wq{dc}"], writes=[f"wq{dc}"])

        def wqkv_bundle(dc):
            a = (dc % 2) * 384
            akeys = _qk("v", a, a + 384)
            P.emit("sync",
                   lambda e, dc=dc, a=a: e.dma_start(
                       out=scrv[:, a:a + 384],
                       in_=wqkv_d[dc * 128:(dc + 1) * 128, :]),
                   writes=akeys, sem="WQ" if dc % 2 == 0 else "WQ2", inc=16)
            P.emit("s",
                   lambda e, dc=dc, a=a: e.activation(
                       rowpool[:, 0:384], scrv[:, a:a + 384], ACT.Copy,
                       scale=vecs[:, V_GAT][:, dc:dc + 1]),
                   reads=akeys + ["vecs", "wgf"], writes=["wgf"])
            P.emit("t",
                   lambda e, dc=dc: e.matmul(ps[6][0:1, 0:384], ones_col[:, 0:1],
                                             rowpool[:, 0:384],
                                             start=(dc == 0), stop=(dc == NDC - 1)),
                   reads=["wgf", "ones_col"],
                   writes=["ps6"] if dc == NDC - 1 else [],
                   accum_writes=["ps6"] if dc != NDC - 1 else [])
            P.emit("s", lambda e, dc=dc: e.copy(wqgh[dc][:, :],
                                                rowpool[:, 0:384]),
                   reads=["wgf"], writes=[f"wqgh{dc}"])
            P.emit("v", lambda e, dc=dc: e.tensor_tensor(
                       wqgl[dc][:, :], rowpool[:, 0:384], wqgh[dc][:, :],
                       OP.subtract),
                   reads=["wgf", f"wqgh{dc}"], writes=[f"wqgl{dc}"])
            P.emit("t",
                   lambda e, dc=dc, a=a: e.matmul(
                       ps[7][0:1, 0:384], vecs[:, V_BAT][:, dc:dc + 1],
                       scrv[:, a:a + 384], start=(dc == 0), stop=(dc == NDC - 1)),
                   reads=akeys + ["vecs"],
                   writes=["ps7"] if dc == NDC - 1 else [],
                   accum_writes=["ps7"] if dc != NDC - 1 else [])

        def wqkv_evacs():
            P.emit("v", lambda e: e.tensor_copy(s1s2[0:1, :], ps[6][0:1, 0:384]),
                   reads=["ps6"], writes=["s1"])
            P.emit("v", lambda e: e.tensor_copy(scrke[0:1, 1536:1920],
                                                ps[7][0:1, 0:384]),
                   reads=["ps7"], writes=["e3"])
            P.emit("gp", lambda e: e.dma_start(out=s1s2[1:2, :],
                                               in_=scrke[0:1, 1536:1920]),
                   reads=["e3"], writes=["s2"], sem="DR", inc=16)

        for nt in range(NNT):
            buf, keys = xbuf(nt)
            P.emit("v",
                   lambda e, nt=nt, buf=buf: e.tensor_reduce(
                       stats[:, ST_S][:, nt:nt + 1], buf, AX, OP.add),
                   reads=keys, accum_writes=["st_s"])
            P.emit("s",
                   lambda e, nt=nt, buf=buf: e.activation(
                       scrke[:, 0:D], buf, ACT.Square,
                       accum_out=stats[:, ST_Q][:, nt:nt + 1]),
                   reads=keys, writes=["e0", "e1"], accum_writes=["st_q"])
            for dc in range(NDC):
                slot = (nt * NDC + dc) % 6
                P.emit("t",
                       lambda e, buf=buf, dc=dc, slot=slot: e.transpose(
                           ps[slot][:, 0:128],
                           buf[:, dc * 128:(dc + 1) * 128], ident[:]),
                       reads=keys + ["ident"], writes=[f"ps{slot}"])
                if dc % 2 == 0:
                    P.emit("s",
                           lambda e, dc=dc, nt=nt, slot=slot: e.copy(
                               h[dc][:, nt::16], ps[slot][:, 0:128]),
                           reads=[f"ps{slot}"], accum_writes=hq(dc))
                else:
                    P.emit("v",
                           lambda e, dc=dc, nt=nt, slot=slot: e.tensor_copy(
                               h[dc][:, nt::16], ps[slot][:, 0:128]),
                           reads=[f"ps{slot}"], accum_writes=hq(dc))
            if nt + 2 < 8:
                xload(nt + 2)
            if nt < NDC:
                pass1_dc(nt)
            if nt == 7:
                beta_tail()
            if nt >= 8:
                wqkv_bundle(nt - 8)
                ternarize(nt - 8)
            if 8 <= nt + 2 < NNT:
                xload(nt + 2)

        _dr_i = [0]

        def _dr():
            _dr_i[0] += 1
            return "DR" if _dr_i[0] % 2 == 0 else "DR2"

        def rowshuffle_out(pslot, field, fkey, rkey):
            P.emit("gp", lambda e: e.dma_start(
                       out=rowpool[pslot:pslot + 1, :].rearrange(
                           "o (p c) -> o p c", c=16),
                       in_=stats[:, field]),
                   reads=[fkey, "wgf", "xrp"], writes=[rkey], sem=_dr(), inc=16)

        def rowshuffle_bf(pslot, field, bfield, fkey, rkey):
            b0 = slice(16 * bfield, 16 * bfield + 16)
            b1 = slice(16 * bfield + 16, 16 * bfield + 32)
            P.emit("v", lambda e: e.tensor_copy(statsbf[:, b0], stats[:, field]),
                   reads=[fkey], writes=[f"sbf{bfield}"])
            P.emit("v", lambda e: e.tensor_tensor(
                       statsbf[:, b1], stats[:, field], statsbf[:, b0],
                       OP.subtract),
                   reads=[fkey, f"sbf{bfield}"], writes=[f"sbf{bfield}l"])
            P.emit("gp", lambda e: e.dma_start(
                       out=rowbf[pslot:pslot + 1, :].rearrange(
                           "o (p c) -> o p c", c=16),
                       in_=statsbf[:, b0]),
                   reads=[f"sbf{bfield}"], writes=[rkey], sem=_dr(), inc=16)
            P.emit("gp", lambda e: e.dma_start(
                       out=rowbf[pslot + 1:pslot + 2, :].rearrange(
                           "o (p c) -> o p c", c=16),
                       in_=statsbf[:, b1]),
                   reads=[f"sbf{bfield}l"], accum_writes=[rkey], sem=_dr(), inc=16)

        def stats_math_common(mbias=False):
            if mbias:
                P.emit("v", lambda e: e.tensor_scalar(
                           stats[:, ST_M], stats[:, ST_S], INV_D,
                           mv[:, MV_BS], OP.mult, OP.add),
                       reads=["st_s", "bs"], writes=["st_m"])
            else:
                P.emit("v", lambda e: e.tensor_scalar(
                           stats[:, ST_M], stats[:, ST_S], INV_D, None,
                           OP.mult),
                       reads=["st_s"], writes=["st_m"])
            P.emit("v", lambda e: e.tensor_tensor(stats[:, ST_T0], stats[:, ST_M],
                                                  stats[:, ST_M], OP.mult),
                   reads=["st_m"], writes=["st_nmrf"])
            P.emit("v", lambda e: e.scalar_tensor_tensor(
                       stats[:, ST_VAR], stats[:, ST_Q], INV_D, stats[:, ST_T0],
                       OP.mult, OP.subtract),
                   reads=["st_q", "st_nmrf"], writes=["st_var"])

        def stats_math_ffn(mbias=False):
            stats_math_common(mbias)
            P.emit("v", lambda e: e.tensor_scalar(
                       stats[:, ST_IRF], stats[:, ST_VAR], 1.0 + 1e-3, 1e-8,
                       OP.mult, OP.add),
                   reads=["st_var"], writes=["st_irf"])
            P.emit("s", lambda e: e.activation(stats[:, ST_IRF], stats[:, ST_IRF],
                                               ACT.Ln),
                   reads=["st_irf"], writes=["st_irf"])
            P.emit("s", lambda e: e.activation(stats[:, ST_RF], stats[:, ST_IRF],
                                               ACT.Exp, scale=-0.5),
                   reads=["st_irf"], writes=["st_rf"])
            P.emit("v", lambda e: e.scalar_tensor_tensor(
                       stats[:, ST_NMRF], stats[:, ST_M], -1.0, stats[:, ST_RF],
                       OP.mult, OP.mult),
                   reads=["st_m", "st_rf"], writes=["st_nmrf"])
            rowshuffle_bf(32, ST_RF, 0, "st_rf", "row32")
            rowshuffle_bf(64, ST_NMRF, 0, "st_nmrf", "row64")

        def fm_stats(pairs=False, sum_skip=False, acc_sq=False):
            for dc in range(NDC):
                bb, bkeys = (scrq, [f"q{t}" for t in range(NTC)]) if dc % 2 == 0 \
                    else (scrke, [f"e{t}" for t in range(NTC)])
                if acc_sq:
                    pass
                elif dc in (5, 7):
                    P.emit("v", lambda e, dc=dc, bb=bb: e.tensor_tensor(
                               bb[:], h[dc][:], h[dc][:], OP.mult),
                           reads=hq(dc), writes=bkeys)
                elif dc in (4, 6):
                    P.emit("gp", lambda e, dc=dc, bb=bb: e.tensor_tensor(
                               bb[:], h[dc][:], h[dc][:], OP.mult),
                           reads=hq(dc), writes=bkeys)
                else:
                    P.emit("s", lambda e, dc=dc, bb=bb: e.activation(
                               bb[:], h[dc][:], ACT.Square),
                           reads=hq(dc), writes=bkeys)
                for tc in range(NTC):
                    if sum_skip:
                        pass
                    elif pairs:
                        def summ(e, dc=dc, tc=tc):
                            e.matmul(ps[tc][0:1, :], onesb[:, 0:1],
                                     pA[dc][:, tsl(tc)],
                                     start=(dc == 0), stop=False)
                            return e.matmul(ps[tc][0:1, :], onesb[:, 0:1],
                                            pB[dc][:, tsl(tc)],
                                            start=False, stop=(dc == NDC - 1))
                        P.emit("t", summ,
                               reads=[f"pa{dc}_{tc}", f"pb{dc}_{tc}", "onesb"],
                               writes=[f"ps{tc}"] if dc == NDC - 1 else [],
                               accum_writes=[f"ps{tc}"] if dc != NDC - 1 else [])
                    else:
                        P.emit("t",
                               lambda e, dc=dc, tc=tc: e.matmul(
                                   ps[tc][0:1, :], ones_col[:, 0:1],
                                   h[dc][:, tsl(tc)],
                                   start=(dc == 0), stop=(dc == NDC - 1)),
                               reads=[hr(dc, tc), "ones_col"],
                               writes=[f"ps{tc}"] if dc == NDC - 1 else [],
                               accum_writes=[f"ps{tc}"] if dc != NDC - 1 else [])
                    if not acc_sq:
                        P.emit("t",
                               lambda e, dc=dc, tc=tc, bb=bb: e.matmul(
                                   ps[4 + tc][0:1, :], ones_col[:, 0:1],
                                   bb[:, tsl(tc)], start=(dc == 0),
                                   stop=(dc == NDC - 1)),
                               reads=[bkeys[tc], "ones_col"],
                               writes=[f"ps{4 + tc}"] if dc == NDC - 1 else [],
                               accum_writes=[f"ps{4 + tc}"]
                               if dc != NDC - 1 else [])
            if acc_sq:
                for tc in range(NTC):
                    P.emit("t", lambda e, tc=tc: e.matmul(
                               ps[4 + tc][0:1, :], ones_col[:, 0:1],
                               scrq[:, tsl(tc)], start=True, stop=True),
                           reads=[f"q{tc}", "ones_col"],
                           writes=[f"ps{4 + tc}"])
            for tc in range(NTC):
                if not sum_skip:
                    P.emit("v", lambda e, tc=tc: e.tensor_copy(
                               rowpool[0:1, tsl(tc)], ps[tc][0:1, :]),
                           reads=[f"ps{tc}"], accum_writes=["row0", "xrp"])
                P.emit("v", lambda e, tc=tc: e.tensor_copy(
                           rowpool[32:33, tsl(tc)], ps[4 + tc][0:1, :]),
                       reads=[f"ps{4 + tc}"], accum_writes=["row32", "xrp"])
            P.emit("gp", lambda e: e.dma_start(
                       out=stats[:, ST_S],
                       in_=rowpool[0:1, :].rearrange("o (p c) -> o p c", c=16)),
                   reads=["row0"], writes=["st_s"], sem="DR2", inc=16)
            P.emit("gp", lambda e: e.dma_start(
                       out=stats[:, ST_Q],
                       in_=rowpool[32:33, :].rearrange("o (p c) -> o p c", c=16)),
                   reads=["row32"], writes=["st_q"], sem="DR", inc=16)

        def ffn(first, mid_cb=None, out_cb=None, split=False):
            for tc in range(NTC):
                P.emit("t", lambda e, tc=tc: e.matmul(
                           ps[tc][:, :], lhsbf[32:34, :], rowbf[32:34, tsl(tc)],
                           start=True, stop=True),
                       reads=["row32", "lhsbf"], writes=[f"ps{tc}"])
                bslot = 4 + (tc % 2)
                P.emit("t", lambda e, tc=tc, bslot=bslot: e.matmul(
                           ps[bslot][:, :], lhsbf[64:66, :], rowbf[64:66, tsl(tc)],
                           start=True, stop=True),
                       reads=["row64", "lhsbf"], writes=[f"ps{bslot}"])
                P.emit("v", lambda e, tc=tc, bslot=bslot: e.tensor_copy(
                           xstage[tc // 2][:, (tc % 2) * TW:(tc % 2) * TW + TW],
                           ps[bslot][:, :]),
                       reads=[f"ps{bslot}"], accum_writes=[f"xst{tc // 2}"])
            def prod_front(dc):
                bb, bkeys = (scrke, [f"e{t}" for t in range(NTC)]) if dc % 2 == 0 \
                    else (scrv, [f"v{t}" for t in range(NTC)])
                for tc in range(NTC):
                    P.emit("v", lambda e, dc=dc, tc=tc, bb=bb: e.tensor_tensor(
                               bb[:, tsl(tc)], h[dc][:, tsl(tc)],
                               ps[tc][:, :], OP.mult),
                           reads=[hr(dc, tc), f"ps{tc}"], writes=[bkeys[tc]])
                    if tc % 2 == 1:
                        P.emit("gp",
                               lambda e, tc=tc, bb=bb: e.tensor_tensor(
                                   bb[:, (tc - 1) * TW:(tc + 1) * TW],
                                   bb[:, (tc - 1) * TW:(tc + 1) * TW],
                                   xstage[tc // 2][:], OP.add),
                               reads=[bkeys[tc - 1], bkeys[tc], f"xst{tc // 2}"],
                               writes=[bkeys[tc - 1], bkeys[tc]])
            def prod_lo(dc):
                bb, bkeys = (scrke, [f"e{t}" for t in range(NTC)]) if dc % 2 == 0 \
                    else (scrv, [f"v{t}" for t in range(NTC)])
                P.emit("v",
                       lambda e, dc=dc, bb=bb: e.tensor_tensor(
                           pB[dc][:], bb[:], pA[dc][:], OP.subtract),
                       reads=bkeys + [f"pa{dc}_{t}" for t in range(NTC)],
                       writes=[f"pb{dc}_{t}" for t in range(NTC)])

            def prod_back(dc, with_lo=True):
                bb, bkeys = (scrke, [f"e{t}" for t in range(NTC)]) if dc % 2 == 0 \
                    else (scrv, [f"v{t}" for t in range(NTC)])
                P.emit("s", lambda e, dc=dc, bb=bb: e.mul(
                           bb[:], bb[:], vecs[:, V_GFF][:, dc:dc + 1]),
                       reads=bkeys + ["vecs"], writes=bkeys)
                P.emit("s", lambda e, dc=dc, bb=bb: e.add(
                           bb[:], bb[:], vecs[:, V_BFF][:, dc:dc + 1]),
                       reads=bkeys + ["vecs"], writes=bkeys)
                P.emit("v", lambda e, dc=dc, bb=bb: e.tensor_reduce(
                           mv[:, MV_AM][:, dc:dc + 1], bb[:], AX, OP.max,
                           apply_absolute_value=True),
                       reads=bkeys, writes=[f"am{dc}"])
                P.emit("s", lambda e, dc=dc, bb=bb: e.copy(pA[dc][:], bb[:]),
                       reads=bkeys, writes=[f"pa{dc}_{t}" for t in range(NTC)])
                if with_lo:
                    prod_lo(dc)

            for dc in range(NDC):
                prod_front(dc)
                if dc > 0:
                    prod_back(dc - 1)
            prod_back(NDC - 1, with_lo=False)
            def xr_slot(dc):
                if dc < 4:
                    return scrke, (dc % 4) * TW, f"e{dc % 4}"
                return scrv, (dc % 4) * TW, f"v{dc % 4}"

            def xr_stage(tc):
                for dc in range(NDC):
                    buf, xa, xrn = xr_slot(dc)
                    P.emit("gp", lambda e, dc=dc, tc=tc, buf=buf, xa=xa:
                           e.tensor_tensor(
                               buf[:, xa:xa + TW], pA[dc][:, tsl(tc)],
                               pB[dc][:, tsl(tc)], OP.add),
                           reads=[f"pa{dc}_{tc}", f"pb{dc}_{tc}"], writes=[xrn])

            def clip_stage(tc):
                for dc in range(NDC):
                    buf, xa, xrn = xr_slot(dc)
                    P.emit("v", lambda e, buf=buf, xa=xa: e.tensor_scalar(
                               buf[:, xa:xa + TW], buf[:, xa:xa + TW],
                               mv[:, MV_NCBC], mv[:, MV_CBC], OP.max, OP.min),
                           reads=[xrn, "ncbc", "cbc"], writes=[xrn])
                for dc in range(NDC):
                    buf, xa, xrn = xr_slot(dc)
                    P.emit("s", lambda e, dc=dc, tc=tc, buf=buf, xa=xa: e.copy(
                               pA[dc][:, tsl(tc)], buf[:, xa:xa + TW]),
                           reads=[xrn], writes=[f"pa{dc}_{tc}"])
                for dc in range(NDC):
                    buf, xa, xrn = xr_slot(dc)
                    P.emit("v", lambda e, dc=dc, tc=tc, buf=buf, xa=xa:
                           e.tensor_tensor(
                               pB[dc][:, tsl(tc)], buf[:, xa:xa + TW],
                               pA[dc][:, tsl(tc)], OP.subtract),
                           reads=[xrn, f"pa{dc}_{tc}"], writes=[f"pb{dc}_{tc}"])

            P.emit("v", lambda e: e.tensor_reduce(mv[:, MV_LMAX], mv[:, MV_AM],
                                                  AX, OP.max),
                   reads=[f"am{dc}" for dc in range(NDC)], writes=["lmax"])
            P.emit("t", lambda e: e.transpose(ps[0][0:1, 0:128], mv[:, MV_LMAX],
                                              ident[:]),
                   reads=["lmax", "ident"], writes=["ps0"])
            P.emit("v", lambda e: e.tensor_reduce(gmaxrow[0:1, 0:1],
                                                  ps[0][0:1, 0:128], AX, OP.max),
                   reads=["ps0"], writes=["gmaxrow"])
            P.emit("sync", lambda e: e.dma_start(out=cc_in[:], in_=gmaxrow[0:1, :]),
                   reads=["gmaxrow"], writes=["ccin"], sem="DC", inc=16)
            P.emit("gp", lambda e: e.collective_compute(
                       "AllReduce", OP.max, replica_groups=[list(range(ncores))],
                       ins=[cc_in], outs=[cc_out]),
                   reads=["ccin"], writes=["ccout"], sem="SC", inc=1)
            prod_lo(NDC - 1)
            xr_stage(0)
            if mid_cb is not None:
                mid_cb()
            if debug:
                P.emit("sync", lambda e: e.dma_start(out=gmaxrow[0:1, :],
                                                     in_=cc_out[:]),
                       reads=["ccout"], writes=["gmaxrow"], sem="DC", inc=16)
            if debug and first:
                P.emit("sync", lambda e: e.dma_start(out=dbg["dbg_gamma"],
                                                     in_=gmaxrow[0:1, :]),
                       reads=["gmaxrow"], sem="DO", inc=16)
            P.emit("sync", lambda e: e.dma_start(
                       out=mv[:, MV_CBC],
                       in_=cc_out[0:1, 0:1].broadcast_to((128, 1))),
                   reads=["ccout"], writes=["cbc"], sem="DC", inc=16)
            P.emit("v", lambda e: e.tensor_scalar(mv[:, MV_NCBC], mv[:, MV_CBC],
                                                  -K1, None, OP.mult),
                   reads=["cbc"], writes=["ncbc"])
            P.emit("v", lambda e: e.tensor_scalar(mv[:, MV_CBC], mv[:, MV_CBC],
                                                  K1, None, OP.mult),
                   reads=["cbc"], writes=["cbc"])

            clip_stage(0)
            for tc in range(NTC):
                for uc in range(NDC):
                    def mmgroup(e, uc=uc, tc=tc):
                        n = 0
                        last = None
                        for dc in range(NDC):
                            for pa in (pA, pB):
                                last = e.matmul(
                                    ps[uc][:, :],
                                    wq[dc][:, uc * 128:(uc + 1) * 128],
                                    pa[dc][:, tsl(tc)],
                                    start=(n == 0), stop=(n == 15))
                                n += 1
                        return last
                    P.emit("t", mmgroup,
                           reads=[f"pa{dc}_{tc}" for dc in range(NDC)] +
                                 [f"pb{dc}_{tc}" for dc in range(NDC)] +
                                 [f"wq{dc}" for dc in range(NDC)],
                           writes=[f"ps{uc}"])
                if tc + 1 < NTC:
                    xr_stage(tc + 1)
                    clip_stage(tc + 1)
                for uc in range(NDC):
                    P.emit("v", lambda e, uc=uc, tc=tc:
                           e.scalar_tensor_tensor(
                               h[uc][:, tsl(tc)], ps[uc][:, :], mv[:, MV_SC],
                               h[uc][:, tsl(tc)], OP.mult, OP.add),
                           reads=[f"ps{uc}", "sc", hr(uc, tc)],
                           writes=[hr(uc, tc)])
                    if out_cb is not None:
                        out_cb(uc, tc)
                    if split:
                        P.emit("s", lambda e, uc=uc, tc=tc: e.copy(
                                   pA[uc][:, tsl(tc)], h[uc][:, tsl(tc)]),
                               reads=[hr(uc, tc)], writes=[f"pa{uc}_{tc}"])
                        P.emit("v", lambda e, uc=uc, tc=tc: e.tensor_tensor(
                                   pB[uc][:, tsl(tc)], h[uc][:, tsl(tc)],
                                   pA[uc][:, tsl(tc)], OP.subtract),
                               reads=[hr(uc, tc), f"pa{uc}_{tc}"],
                               writes=[f"pb{uc}_{tc}"])
                        if uc == 0:
                            P.emit("s", lambda e, tc=tc: e.activation(
                                       scrq[:, tsl(tc)], h[0][:, tsl(tc)],
                                       ACT.Square),
                                   reads=[hr(0, tc)], writes=[f"q{tc}"])
                        else:
                            w = uc % 4
                            tmp = xstage[w // 2][:, (w % 2) * TW:
                                                 (w % 2) * TW + TW]
                            teng = ("v" if uc in (5, 7) else
                                    "gp" if uc in (2, 4, 6) else "s")
                            if teng == "s":
                                P.emit("s", lambda e, uc=uc, tc=tc, tmp=tmp:
                                       e.activation(tmp, h[uc][:, tsl(tc)],
                                                    ACT.Square),
                                       reads=[hr(uc, tc)],
                                       writes=[f"sqt{w}", f"xst{w // 2}"])
                            else:
                                P.emit(teng, lambda e, uc=uc, tc=tc, tmp=tmp:
                                       e.tensor_tensor(tmp, h[uc][:, tsl(tc)],
                                                       h[uc][:, tsl(tc)],
                                                       OP.mult),
                                       reads=[hr(uc, tc)],
                                       writes=[f"sqt{w}", f"xst{w // 2}"])
                            P.emit("v", lambda e, tc=tc, tmp=tmp:
                                   e.tensor_tensor(scrq[:, tsl(tc)],
                                                   scrq[:, tsl(tc)], tmp,
                                                   OP.add),
                                   reads=[f"q{tc}", f"sqt{w}"],
                                   writes=[f"q{tc}"])

        def stats_math_attn():
            stats_math_common()
            P.emit("v", lambda e: e.tensor_scalar(
                       stats[:, ST_R2], stats[:, ST_VAR], 1.0 + EPS, 1e-10,
                       OP.mult, OP.add),
                   reads=["st_var"], writes=["st_r2"])
            P.emit("s", lambda e: e.activation(stats[:, ST_R2], stats[:, ST_R2],
                                               ACT.Ln),
                   reads=["st_r2"], writes=["st_r2"])
            P.emit("s", lambda e: e.activation(stats[:, ST_IRF], stats[:, ST_R2],
                                               ACT.Exp, scale=0.5),
                   reads=["st_r2"], writes=["st_irf"])
            P.emit("s", lambda e: e.activation(stats[:, ST_RF], stats[:, ST_R2],
                                               ACT.Exp, scale=-0.5),
                   reads=["st_r2"], writes=["st_rf"])
            P.emit("v", lambda e: e.tensor_scalar(stats[:, ST_R2],
                                                  stats[:, ST_VAR], EPS, None,
                                                  OP.add),
                   reads=["st_var"], writes=["st_r2"])
            P.emit("s", lambda e: e.activation(stats[:, ST_R2], stats[:, ST_R2],
                                               ACT.Ln),
                   reads=["st_r2"], writes=["st_r2"])
            P.emit("s", lambda e: e.activation(stats[:, ST_R2], stats[:, ST_R2],
                                               ACT.Exp, scale=-0.5),
                   reads=["st_r2"], writes=["st_r2"])
            P.emit("v", lambda e: e.scalar_tensor_tensor(
                       stats[:, ST_NMR2], stats[:, ST_M], -1.0, stats[:, ST_R2],
                       OP.mult, OP.mult),
                   reads=["st_m", "st_r2"], writes=["st_nmr2"])
            P.emit("v", lambda e: e.tensor_scalar(stats[:, ST_R2P1],
                                                  stats[:, ST_R2], 1.0, None,
                                                  OP.add),
                   reads=["st_r2"], writes=["st_r2p1"])
            P.emit("v", lambda e: e.tensor_scalar(stats[:, ST_NMRF],
                                                  stats[:, ST_M], -1.0, None,
                                                  OP.mult),
                   reads=["st_m"], writes=["st_nmrf"])
            rowshuffle_out(0, ST_NMRF, "st_nmrf", "row0")
            rowshuffle_out(1, ST_IRF, "st_irf", "row1")
            rowshuffle_bf(32, ST_RF, 0, "st_rf", "row32")

        def attn():
            qbuf = {0: (scrq, "q"), 1: (scrke, "e"), 2: (scrv, "v")}
            slot_i = [0]
            for tc in range(NTC):
                bslot = 6 + (tc % 2)
                P.emit("t", lambda e, tc=tc, bslot=bslot: e.matmul(
                           ps[bslot][:, :], lhsbf[32:34, :],
                           rowbf[32:34, tsl(tc)], start=True, stop=True),
                       reads=["row32", "lhsbf"], writes=[f"ps{bslot}"])
                P.emit("v", lambda e, tc=tc, bslot=bslot: e.tensor_copy(
                           xstage[tc % 2][:, 0:TW], ps[bslot][:, :]),
                       reads=[f"ps{bslot}"], writes=[f"xst{tc % 2}"])
                for uc in range(3):
                    slot = slot_i[0] % 6
                    slot_i[0] += 1

                    def qkvgroup(e, uc=uc, tc=tc, slot=slot):
                        n = 0
                        for wg, pa in ((wqgh, pA), (wqgh, pB), (wqgl, pA)):
                            for dc in range(NDC):
                                e.matmul(ps[slot][:, :],
                                         wg[dc][:, uc * 128:(uc + 1) * 128],
                                         pa[dc][:, tsl(tc)],
                                         start=(n == 0), stop=False)
                                n += 1
                        return e.matmul(ps[slot][:, :],
                                        s1s2[0:2, uc * 128:(uc + 1) * 128],
                                        rowpool[0:2, tsl(tc)],
                                        start=False, stop=True)
                    P.emit("t", qkvgroup,
                           reads=[f"pa{dc}_{tc}" for dc in range(NDC)] +
                                 [f"pb{dc}_{tc}" for dc in range(NDC)] +
                                 [f"wqgh{dc}" for dc in range(NDC)] +
                                 [f"wqgl{dc}" for dc in range(NDC)] +
                                 ["s1", "s2", "row0", "row1"],
                           writes=[f"ps{slot}"])
                    buf, pre = qbuf[uc]
                    P.emit("v", lambda e, buf=buf, tc=tc, slot=slot:
                           e.tensor_tensor(buf[:, tsl(tc)], ps[slot][:, :],
                                           xstage[tc % 2][:, 0:TW], OP.mult),
                           reads=[f"ps{slot}", f"xst{tc % 2}"],
                           writes=[f"{pre}{tc}"])
            if debug:
                for pre, nmq, buf in (("q", "q", scrq), ("e", "k", scrke),
                                      ("v", "v", scrv)):
                    P.emit("sync", lambda e, nmq=nmq, buf=buf: e.dma_start(
                               out=dbg[f"dbg_{nmq}"], in_=buf[:]),
                           reads=[f"{pre}{tc}" for tc in range(NTC)],
                           sem="DO", inc=16)
            P.emit("v", lambda e: e.tensor_reduce(mv[:, MV_KNEG], scrke[:], AX,
                                                  OP.max, negate=True),
                   reads=[f"e{tc}" for tc in range(NTC)], writes=["kneg"])
            P.emit("s", lambda e: e.activation(scrke[:], scrke[:], ACT.Exp,
                                               bias=mv[:, MV_KNEG],
                                               accum_out=mv[:, MV_SSUM]),
                   reads=[f"e{tc}" for tc in range(NTC)] + ["kneg"],
                   writes=[f"e{tc}" for tc in range(NTC)] + ["ssum"])
            P.emit("v", lambda e: e.reciprocal(mv[:, MV_RS], mv[:, MV_SSUM]),
                   reads=["ssum"], writes=["rs"])
            P.emit("v", lambda e: e.tensor_scalar(mv[:, MV_CS], mv[:, MV_RS],
                                                  HSCALE, None, OP.mult),
                   reads=["rs"], writes=["cs"])
            for j in range(NNT):
                sl = slice(j * 128, (j + 1) * 128)
                tcj = (j * 128) // TW
                s0 = j % 4
                s1_ = 4 + (j % 2)
                P.emit("t", lambda e, sl=sl, s0=s0: e.transpose(
                           ps[s0][:, 0:128], scrke[:, sl], ident[:]),
                       reads=[f"e{tcj}", "ident"], writes=[f"ps{s0}"])
                P.emit("v", lambda e, sl=sl, s0=s0: e.tensor_copy(
                           scrke[:, sl], ps[s0][:, 0:128]),
                       reads=[f"ps{s0}"], accum_writes=[f"e{tcj}"])
                P.emit("t", lambda e, sl=sl, s1_=s1_: e.transpose(
                           ps[s1_][:, 0:128], scrv[:, sl], ident[:]),
                       reads=[f"v{tcj}", "ident"], writes=[f"ps{s1_}"])
                P.emit("s", lambda e, sl=sl, s1_=s1_: e.copy(
                           scrv[:, sl], ps[s1_][:, 0:128]),
                       reads=[f"ps{s1_}"], accum_writes=[f"v{tcj}"])

            def ctxgroup(e):
                last = None
                for j in range(NNT):
                    sl = slice(j * 128, (j + 1) * 128)
                    last = e.matmul(ps[6][:, 0:128], scrke[:, sl], scrv[:, sl],
                                    start=(j == 0), stop=(j == NNT - 1))
                return last
            P.emit("t", ctxgroup,
                   reads=[f"e{tc}" for tc in range(NTC)] +
                         [f"v{tc}" for tc in range(NTC)],
                   writes=["ps6"])
            for hd in range(4):
                b = 32 * hd
                P.emit("v", lambda e, b=b: e.tensor_scalar(
                           bdm[b:b + 32, b:b + 32], ps[6][b:b + 32, b:b + 32],
                           mv[:, MV_CS][b:b + 32, 0:1], None, OP.mult),
                       reads=["ps6", "cs"], accum_writes=["bd"])
            for tc in range(NTC):
                slot = tc % 2
                P.emit("t", lambda e, tc=tc, slot=slot: e.matmul(
                           ps[slot][:, :], bdm[:], scrq[:, tsl(tc)],
                           start=True, stop=True),
                       reads=["bd", f"q{tc}"], writes=[f"ps{slot}"])
                P.emit("v", lambda e, tc=tc, slot=slot: e.tensor_copy(
                           scrq[:, tsl(tc)], ps[slot][:, :]),
                       reads=[f"ps{slot}"], writes=[f"q{tc}"])
            for tc in range(NTC):
                P.emit("t", lambda e, tc=tc: e.matmul(
                           ps[0][0:1, :], mv[:, MV_WRS], scrq[:, tsl(tc)],
                           start=True, stop=True),
                       reads=["wrs", f"q{tc}"], writes=["ps0"])
                P.emit("v", lambda e, tc=tc: e.scalar_tensor_tensor(
                           rowpool[0:1, tsl(tc)], rowpool[0:1, tsl(tc)],
                           -float(D), ps[0][0:1, :], OP.mult, OP.add),
                       reads=["row0", "ps0"], writes=["row0"])
            rowshuffle_bf(32, ST_NMR2, 0, "st_nmr2", "row32")
            rowshuffle_bf(64, ST_R2P1, 0, "st_r2p1", "row64")
            for tc in range(NTC):
                slot = 2 + (tc % 2)
                P.emit("t", lambda e, tc=tc, slot=slot: e.matmul(
                           ps[slot][:, :], lhsbf[64:66, :],
                           rowbf[64:66, tsl(tc)], start=True, stop=True),
                       reads=["row64", "lhsbf"], writes=[f"ps{slot}"])
                P.emit("v", lambda e, tc=tc, slot=slot: e.tensor_copy(
                           scrke[:, tsl(tc)], ps[slot][:, :]),
                       reads=[f"ps{slot}"], writes=[f"e{tc}"])
            P.emit("s", lambda e: e.copy(pA[0][:], scrq[:]),
                   reads=[f"q{t}" for t in range(NTC)],
                   writes=[f"pa0_{t}" for t in range(NTC)])
            P.emit("v", lambda e: e.tensor_tensor(pB[0][:], scrq[:], pA[0][:],
                                                  OP.subtract),
                   reads=[f"q{t}" for t in range(NTC)] +
                         [f"pa0_{t}" for t in range(NTC)],
                   writes=[f"pb0_{t}" for t in range(NTC)])
            slot_i = [0]
            for tc in range(NTC):
                for uc in range(NDC):
                    slot = 4 + (slot_i[0] % 4)
                    slot_i[0] += 1

                    def opgroup(e, uc=uc, tc=tc, slot=slot):
                        sl = slice(uc * 128, (uc + 1) * 128)
                        e.matmul(ps[slot][:, :], wouth[:, sl],
                                 pA[0][:, tsl(tc)], start=True, stop=False)
                        e.matmul(ps[slot][:, :], wouth[:, sl],
                                 pB[0][:, tsl(tc)], start=False, stop=False)
                        e.matmul(ps[slot][:, :], woutl[:, sl],
                                 pA[0][:, tsl(tc)], start=False, stop=False)
                        return e.matmul(ps[slot][:, :], lhsbf[32:34, 0:128],
                                        rowbf[32:34, tsl(tc)],
                                        start=False, stop=True)
                    P.emit("t", opgroup,
                           reads=["wouth", "woutl", f"pa0_{tc}", f"pb0_{tc}",
                                  "row32", "lhsbf"],
                           writes=[f"ps{slot}"])
                    va = (uc % 4) * TW
                    P.emit("v", lambda e, uc=uc, tc=tc, va=va: e.tensor_tensor(
                               scrv[:, va:va + TW], h[uc][:, tsl(tc)],
                               scrke[:, tsl(tc)], OP.mult),
                           reads=[hr(uc, tc), f"e{tc}"], writes=[f"v{uc % 4}"])
                    P.emit("v", lambda e, uc=uc, tc=tc, slot=slot, va=va:
                           e.scalar_tensor_tensor(
                               h[uc][:, tsl(tc)], ps[slot][:, :],
                               vecs[:, V_BOUT][:, uc:uc + 1], scrv[:, va:va + TW],
                               OP.add, OP.add),
                           reads=[f"ps{slot}", "vecs", f"v{uc % 4}"],
                           writes=[hr(uc, tc)])
                    if uc == 0:
                        P.emit("s", lambda e, tc=tc: e.activation(
                                   scrq[:, tsl(tc)], h[0][:, tsl(tc)],
                                   ACT.Square),
                               reads=[hr(0, tc)], writes=[f"q{tc}"])
                    else:
                        w = uc % 4
                        tmp = xstage[w // 2][:, (w % 2) * TW:(w % 2) * TW + TW]
                        if uc in (2, 4, 6):
                            P.emit("gp", lambda e, uc=uc, tc=tc, tmp=tmp:
                                   e.tensor_tensor(tmp, h[uc][:, tsl(tc)],
                                                   h[uc][:, tsl(tc)], OP.mult),
                                   reads=[hr(uc, tc)],
                                   writes=[f"sqt{w}", f"xst{w // 2}"])
                        else:
                            P.emit("s", lambda e, uc=uc, tc=tc, tmp=tmp:
                                   e.activation(tmp, h[uc][:, tsl(tc)],
                                                ACT.Square),
                                   reads=[hr(uc, tc)],
                                   writes=[f"sqt{w}", f"xst{w // 2}"])
                        P.emit("v", lambda e, tc=tc, tmp=tmp:
                               e.tensor_tensor(scrq[:, tsl(tc)],
                                               scrq[:, tsl(tc)], tmp, OP.add),
                               reads=[f"q{tc}", f"sqt{w}"], writes=[f"q{tc}"])

        stats_math_ffn()
        wqkv_evacs()
        ffn(first=True, split=True)
        if debug:
            for dc in range(NDC):
                P.emit("sync", lambda e, dc=dc: e.dma_start(
                           out=dbg["dbg_h1"][dc * 128:(dc + 1) * 128, :],
                           in_=h[dc][:]),
                       reads=hq(dc), sem="DO", inc=16)
        fm_stats(pairs=True, acc_sq=True)
        stats_math_attn()
        attn()
        if debug:
            for dc in range(NDC):
                P.emit("sync", lambda e, dc=dc: e.dma_start(
                           out=dbg["dbg_h2"][dc * 128:(dc + 1) * 128, :],
                           in_=h[dc][:]),
                       reads=hq(dc), sem="DO", inc=16)
        fm_stats(sum_skip=True, acc_sq=True)
        stats_math_ffn(mbias=True)

        Y_SEMS = ("DO", "DO2", "XA", "XB")
        _yn = [0]

        def ydma(uc, tc):
            sem = Y_SEMS[_yn[0] % 4]
            _yn[0] += 1
            P.emit("sync", lambda e, uc=uc, tc=tc: e.dma_start(
                       out=y_d[uc * 128:(uc + 1) * 128, tc * TW:(tc + 1) * TW],
                       in_=h[uc][:, tsl(tc)]),
                   reads=[hr(uc, tc)], writes=[f"y{uc}_{tc}"], sem=sem, inc=16)

        ffn(first=False, out_cb=ydma)

        final_y = {s: P.cnt[s] for s in Y_SEMS}

        def make_runner(qname):
            items = P.q[qname]

            def run(eng):
                for waits, fn, sem, inc in items:
                    for s_, v_ in waits:
                        eng.wait_ge(sems[s_], v_)
                    last = fn(eng)
                    if last is not None:
                        last.then_inc(sems[sem], inc)
                if qname == "sync":
                    for s_, v_ in final_y.items():
                        eng.wait_ge(sems[s_], v_)
            return run

        block.sync(make_runner("sync"))
        block.gpsimd(make_runner("gp"))
        block.vector(make_runner("v"))
        block.scalar(make_runner("s"))
        block.tensor(make_runner("t"))

    return nc


_CACHE = {}


def _get_nc(debug=False, ncores=8):
    key = (debug, ncores)
    if key not in _CACHE:
        _CACHE[key] = build(debug, ncores)
    return _CACHE[key]


def _pack_vec(v):
    return np.ascontiguousarray(np.asarray(v, np.float32).reshape(8, 128).T)


def make_in_maps(L0_attn_x, L0_attn_gamma, L0_attn_beta, L0_attn_w_qkv,
                 L0_attn_w_out, L0_attn_b_out, L0_ffn_weight, L0_ffn_ln_gamma,
                 L0_ffn_ln_beta):
    x = np.asarray(L0_attn_x, np.float32)
    vecs = np.concatenate([
        _pack_vec(L0_attn_b_out), _pack_vec(L0_ffn_ln_gamma),
        _pack_vec(L0_ffn_ln_beta),
        _pack_vec(np.asarray(L0_attn_gamma).reshape(-1)),
        _pack_vec(np.asarray(L0_attn_beta).reshape(-1))], axis=1)
    vecs = np.ascontiguousarray(vecs)
    wffn = np.ascontiguousarray(np.asarray(L0_ffn_weight, np.float32))
    wqkv = np.ascontiguousarray(np.asarray(L0_attn_w_qkv, np.float32))
    wout = np.ascontiguousarray(np.asarray(L0_attn_w_out, np.float32))
    return [{"x": np.ascontiguousarray(x[c]), "wffn": wffn, "wqkv": wqkv,
             "wout": wout, "vecs": vecs} for c in range(8)]


def kernel(L0_attn_x, L0_attn_gamma, L0_attn_beta, L0_attn_w_qkv, L0_attn_w_out,
           L0_attn_b_out, L0_ffn_weight, L0_ffn_ln_gamma, L0_ffn_ln_beta):
    nc = _get_nc(False)
    in_maps = make_in_maps(L0_attn_x, L0_attn_gamma, L0_attn_beta, L0_attn_w_qkv,
                           L0_attn_w_out, L0_attn_b_out, L0_ffn_weight,
                           L0_ffn_ln_gamma, L0_ffn_ln_beta)
    res = run_bass_kernel_spmd(nc, in_maps, core_ids=list(range(8)))
    return np.stack(
        [np.asarray(r["y"]).reshape(D, 128, 16).transpose(2, 1, 0)
         .reshape(T, D) for r in res.results], axis=0)


# version 54
# speedup vs baseline: 1.0089x; 1.0017x over previous
import contextlib

import numpy as np

import concourse.bass as bass
import concourse.mybir as mybir
from concourse.bass_utils import run_bass_kernel_spmd

F32 = mybir.dt.float32
BF16 = mybir.dt.bfloat16
AX = mybir.AxisListType.X
OP = mybir.AluOpType
ACT = mybir.ActivationFunctionType

T = 2048
D = 1024
NDC = 8
NTC = 4
TW = 512
NNT = 16
EPS = 1e-5
K1 = float(np.float32(1.0 - EPS / 128.0))
INV_D = 1.0 / D
HSCALE = float(32.0 ** -0.5)

DMA_SEMS = ("DX", "DX2", "DW", "DW2", "DR", "DR2", "DC", "DO", "DO2", "SC",
            "XA", "XB", "XC", "XD", "WQ", "WQ2")
ENG_SEM = {"v": "SV", "s": "SS", "t": "ST", "gp": "SG"}


class Plan:

    def __init__(self):
        self.q = {k: [] for k in ("sync", "gp", "v", "s", "t")}
        self.cnt = {s: 0 for s in ("SV", "SS", "ST", "SG") + DMA_SEMS}
        self.seen = {k: {} for k in self.q}
        self.reg = {}

    def _dep(self, qn, sem, val, waits):
        if sem in DMA_SEMS:
            val = self.cnt[sem]
        if val <= 0 or self.seen[qn].get(sem, 0) >= val:
            return
        self.seen[qn][sem] = val
        waits.append((sem, val))

    def emit(self, qn, fn, reads=(), writes=(), accum_writes=(), sem=None, inc=1):
        if sem is None:
            sem = ENG_SEM[qn]
        waits = []
        for r in reads:
            e = self.reg.get(r)
            if e:
                for s_, v_ in e["w"].items():
                    self._dep(qn, s_, v_, waits)
        for w in list(writes) + list(accum_writes):
            e = self.reg.get(w)
            if e:
                for s_, v_ in e["w"].items():
                    self._dep(qn, s_, v_, waits)
                for s_, v_ in e["r"].items():
                    self._dep(qn, s_, v_, waits)
        if sem in DMA_SEMS:
            self._dep(qn, sem, 1, waits)
        self.cnt[sem] += inc
        cnt = self.cnt[sem]
        self.q[qn].append((tuple(sorted(set(waits))), fn, sem, inc))
        for r in reads:
            e = self.reg.setdefault(r, {"w": {}, "r": {}})
            e["r"][sem] = cnt
        for w in writes:
            self.reg[w] = {"w": {sem: cnt}, "r": {}}
        for w in accum_writes:
            e = self.reg.setdefault(w, {"w": {}, "r": {}})
            e["w"][sem] = cnt


def _qk(prefix, a, b):
    return [f"{prefix}{i}" for i in range(NTC) if a < (i + 1) * TW and b > i * TW]


def build(debug=False, ncores=8):
    nc = bass.Bass("TRN2", target_bir_lowering=False)

    x_d = nc.dram_tensor("x", [T, D], F32, kind="ExternalInput").ap()
    wffn_d = nc.dram_tensor("wffn", [D, D], F32, kind="ExternalInput").ap()
    wqkv_d = nc.dram_tensor("wqkv", [D, 384], F32, kind="ExternalInput").ap()
    wout_d = nc.dram_tensor("wout", [128, D], F32, kind="ExternalInput").ap()
    vecs_d = nc.dram_tensor("vecs", [128, 40], F32, kind="ExternalInput").ap()
    y_d = nc.dram_tensor("y", [D, T], F32, kind="ExternalOutput").ap()

    cc_in = nc.dram_tensor("cc_in", [1, 16], F32, kind="Internal").ap()
    bc_d = nc.dram_tensor("bc_d", [1, 8], F32, kind="Internal").ap()
    cc_out = nc.dram_tensor("cc_out", [1, 16], F32, kind="Internal").ap()

    ident_d = nc.inline_tensor(np.eye(128, dtype=np.float32), name="ident_c").ap()
    lhsbf_np = np.zeros((128, 128), dtype=np.float32)
    lhsbf_np[[0, 1, 32, 33, 64, 65], :] = 1.0
    import ml_dtypes
    lhsbf_d = nc.inline_tensor(lhsbf_np.astype(ml_dtypes.bfloat16),
                               name="lhsbf_c").ap()
    ones_col_d = nc.inline_tensor(np.ones((128, 1), dtype=np.float32),
                                  name="onesc_c").ap()
    onesb_d = nc.inline_tensor(np.ones((128, 1), dtype=ml_dtypes.bfloat16),
                               name="onesb_c").ap()

    dbg = {}
    if debug:
        for name, shape in [("dbg_h1", [D, T]), ("dbg_h2", [D, T]),
                            ("dbg_q", [128, T]), ("dbg_k", [128, T]),
                            ("dbg_v", [128, T]), ("dbg_gamma", [1, 16])]:
            dbg[name] = nc.dram_tensor(name, shape, F32, kind="ExternalOutput").ap()

    es = contextlib.ExitStack()
    with es:
        sbt = lambda name, shape, dt=F32: es.enter_context(
            nc.sbuf_tensor(name, shape, dt))
        h = [sbt(f"h{i}", [128, T]) for i in range(NDC)]
        pA = [sbt(f"pA{i}", [128, T], BF16) for i in range(NDC)]
        pB = [sbt(f"pB{i}", [128, T], BF16) for i in range(NDC)]
        wq = [sbt(f"wq{i}", [128, D], BF16) for i in range(NDC)]
        wqgh = [sbt(f"wqgh{i}", [128, 384], BF16) for i in range(NDC)]
        wqgl = [sbt(f"wqgl{i}", [128, 384], BF16) for i in range(NDC)]
        wouth = sbt("wouth", [128, D], BF16)
        woutl = sbt("woutl", [128, D], BF16)
        scrq = sbt("scrq", [128, T])
        scrke = sbt("scrke", [128, T])
        scrv = sbt("scrv", [128, T])
        xstage = [sbt(f"xstage{i}", [128, D]) for i in range(2)]
        rowpool = sbt("rowpool", [128, T])
        s1s2 = sbt("s1s2", [2, 384])
        stats = sbt("stats", [128, 128])
        vecs = sbt("vecs_sb", [128, 40])
        mv = sbt("mv", [128, 22])
        gmaxrow = sbt("gmaxrow", [1, 16])
        srow = sbt("srow", [1, 16])
        ident = sbt("ident", [128, 128])
        lhsbf = sbt("lhsbf", [128, 128], BF16)
        rowbf = sbt("rowbf", [128, T], BF16)
        statsbf = sbt("statsbf", [128, 32], BF16)
        ones_col = sbt("ones_col", [128, 1])
        onesb = sbt("onesb", [128, 1], BF16)
        bdm = sbt("bdm", [128, 128])
        ps = [es.enter_context(nc.psum_tensor(f"ps{i}", [128, TW], F32))
              for i in range(8)]
        sems = {n: es.enter_context(nc.semaphore(n))
                for n in ("SV", "SS", "ST", "SG") + DMA_SEMS}
        block = es.enter_context(nc.Block())

        (ST_S, ST_Q, ST_M, ST_VAR, ST_RF, ST_IRF, ST_NMRF,
         ST_R2) = (slice(16 * i, 16 * (i + 1)) for i in range(8))
        ST_T0 = ST_NMRF
        ST_NMR2 = ST_Q
        ST_R2P1 = ST_S
        V_BOUT, V_GFF, V_BFF, V_GAT, V_BAT = (
            slice(8 * i, 8 * (i + 1)) for i in range(5))
        MV_AM = slice(0, 8)
        MV_LMAX = slice(8, 9)
        MV_THR = slice(9, 10)
        MV_NTHR = slice(10, 11)
        MV_SC = slice(11, 12)
        MV_CBC = slice(12, 13)
        MV_NCBC = slice(13, 14)
        MV_KNEG = slice(14, 15)
        MV_SSUM = slice(15, 16)
        MV_RS = slice(16, 17)
        MV_AWT = slice(17, 18)
        MV_CS = slice(18, 19)
        MV_WRS = slice(19, 20)
        MV_BS = slice(20, 21)

        P = Plan()
        hr = lambda dc, tc: f"h{dc}_{tc}"
        hq = lambda dc: [hr(dc, tc) for tc in range(NTC)]
        tsl = lambda tc: slice(tc * TW, (tc + 1) * TW)

        P.emit("sync", lambda e: e.dma_start(out=ident[:], in_=ident_d[:]),
               writes=["ident"], sem="DW", inc=16)
        P.emit("sync", lambda e: e.dma_start(out=lhsbf[:], in_=lhsbf_d[:]),
               writes=["lhsbf"], sem="DW2", inc=16)
        P.emit("sync", lambda e: e.dma_start(out=ones_col[:], in_=ones_col_d[:]),
               writes=["ones_col"], sem="DW", inc=16)
        P.emit("sync", lambda e: e.dma_start(out=onesb[:], in_=onesb_d[:]),
               writes=["onesb"], sem="DW2", inc=16)
        P.emit("sync", lambda e: e.dma_start(out=vecs[:], in_=vecs_d[:]),
               writes=["vecs"], sem="DW", inc=16)
        P.emit("sync", lambda e: e.dma_start(out=scrv[:, 0:D], in_=wout_d[:]),
               writes=["v0", "v1"], sem="DW", inc=16)
        P.emit("s", lambda e: e.copy(wouth[:], scrv[:, 0:D]),
               reads=["v0", "v1"], writes=["wouth"])
        P.emit("v", lambda e: e.tensor_tensor(woutl[:], scrv[:, 0:D], wouth[:],
                                              OP.subtract),
               reads=["v0", "v1", "wouth"], writes=["woutl"])
        P.emit("v", lambda e: e.tensor_reduce(mv[:, MV_WRS], scrv[:, 0:D], AX,
                                              OP.add),
               reads=["v0", "v1"], writes=["wrs"])
        P.emit("v", lambda e: e.memset(gmaxrow[0:1, :], 0.0), writes=["gmaxrow"])
        P.emit("sync", lambda e: e.dma_start(out=cc_in[:], in_=gmaxrow[0:1, :]),
               reads=["gmaxrow"], writes=["ccin"], sem="DC", inc=16)
        P.emit("gp", lambda e: e.collective_compute(
                   "AllReduce", OP.max, replica_groups=[list(range(ncores))],
                   ins=[cc_in], outs=[cc_out]),
               reads=["ccin"], writes=["ccout"], sem="SC", inc=1)
        P.emit("v", lambda e: e.memset(bdm[:], 0.0), writes=["bd"])

        def xbuf(nt):
            if nt < 8:
                i = nt % 4
                if i == 2:
                    return rowpool[:, D:2 * D], ["xrp"]
                if i == 3:
                    return rowpool[:, 0:D], ["xrp2"]
                return xstage[i][:, :], [f"xst{i}"]
            idx = (nt - 8) % 4
            if idx == 0:
                return scrq[:, 0:D], ["q0", "q1"]
            if idx == 1:
                return scrq[:, D:2 * D], ["q2", "q3"]
            i = idx - 2
            return xstage[i][:, :], [f"xst{i}"]

        def xload(nt):
            buf, keys = xbuf(nt)
            if nt < 8:
                eng, sem = "gp", ("DX", "DX2", "DR", "DR2")[nt % 4]
            else:
                eng, sem = "sync", ("XA", "XB", "XC", "XD")[(nt - 8) % 4]
            P.emit(eng,
                   lambda e, nt=nt, buf=buf: e.dma_start(
                       out=buf, in_=x_d[nt * 128:(nt + 1) * 128, :]),
                   writes=keys, sem=sem, inc=16)
        xload(0)
        xload(1)

        def pass1_dc(dc):
            a = (dc % 2) * D
            keys = _qk("q", a, a + D)
            wpa = [f"pa{dc}_0", f"pa{dc}_1"]
            wpb = [f"pb{dc}_0", f"pb{dc}_1"]
            buf, rkeys = (scrke, ["e2", "e3"]) if dc % 2 == 0 \
                else (scrv, ["v2", "v3"])
            P.emit("sync",
                   lambda e, dc=dc, a=a: e.dma_start(
                       out=scrq[:, a:a + D], in_=wffn_d[dc * 128:(dc + 1) * 128, :]),
                   writes=keys, sem="DW" if dc % 2 == 0 else "DW2", inc=16)
            P.emit("s", lambda e, dc=dc, a=a: e.copy(pA[dc][:, 0:D],
                                                     scrq[:, a:a + D]),
                   reads=keys, writes=wpa)
            P.emit("v",
                   lambda e, dc=dc, a=a: e.tensor_reduce(
                       mv[:, MV_AM][:, dc:dc + 1], scrq[:, a:a + D], AX, OP.add,
                       apply_absolute_value=True),
                   reads=keys, writes=[f"am{dc}"])
            P.emit("v",
                   lambda e, dc=dc, a=a, buf=buf: e.tensor_tensor(
                       buf[:, D:2 * D], scrq[:, a:a + D], pA[dc][:, 0:D],
                       OP.subtract),
                   reads=keys + wpa, writes=rkeys)
            P.emit("s", lambda e, dc=dc, buf=buf: e.copy(
                       wq[dc][:, :], buf[:, D:2 * D]),
                   reads=rkeys, writes=[f"wq{dc}"])
            P.emit("v",
                   lambda e, dc=dc, buf=buf: e.tensor_tensor(
                       pB[dc][:, 0:D], buf[:, D:2 * D], wq[dc][:, :],
                       OP.subtract),
                   reads=rkeys + [f"wq{dc}"], writes=wpb)

        def beta_tail():
            P.emit("v", lambda e: e.tensor_reduce(mv[:, MV_AWT], mv[:, MV_AM], AX,
                                                  OP.add),
                   reads=[f"am{dc}" for dc in range(NDC)], writes=["awt"])
            P.emit("t", lambda e: e.transpose(ps[6][0:1, 0:128], mv[:, MV_AWT],
                                              ident[:]),
                   reads=["awt", "ident"], writes=["ps6"])
            P.emit("v", lambda e: e.tensor_reduce(srow[0:1, 0:1],
                                                  ps[6][0:1, 0:128], AX, OP.add),
                   reads=["ps6"], writes=["beta_s"])
            P.emit("v", lambda e: e.tensor_scalar(srow[0:1, 1:2], srow[0:1, 0:1],
                                                  1.0 / (D * D), None, OP.mult),
                   reads=["beta_s"], writes=["beta"])
            P.emit("v", lambda e: e.tensor_scalar(srow[0:1, 2:3], srow[0:1, 1:2],
                                                  EPS, 0.5, OP.add, OP.mult),
                   reads=["beta"], writes=["thr_s"])
            P.emit("v", lambda e: e.tensor_scalar(srow[0:1, 3:4], srow[0:1, 1:2],
                                                  0.5, None, OP.mult),
                   reads=["beta"], writes=["sc_s"])
            P.emit("v", lambda e: e.tensor_reduce(mv[:, MV_BS],
                                                  vecs[:, V_BOUT], AX, OP.add),
                   reads=["vecs"], writes=["bs_p"])
            P.emit("t", lambda e: e.transpose(ps[6][0:1, 0:128], mv[:, MV_BS],
                                              ident[:]),
                   reads=["bs_p", "ident"], writes=["ps6"])
            P.emit("v", lambda e: e.tensor_reduce(srow[0:1, 4:5],
                                                  ps[6][0:1, 0:128], AX, OP.add),
                   reads=["ps6"], writes=["bs_s"])
            P.emit("v", lambda e: e.tensor_scalar(srow[0:1, 4:5], srow[0:1, 4:5],
                                                  INV_D, None, OP.mult),
                   reads=["bs_s"], writes=["bs_s"])
            P.emit("sync", lambda e: e.dma_start(out=bc_d[0:1, 0:3],
                                                 in_=srow[0:1, 2:5]),
                   reads=["thr_s", "sc_s", "bs_s"], writes=["bc_d"],
                   sem="DC", inc=16)
            P.emit("sync", lambda e: e.dma_start(
                       out=mv[:, MV_BS],
                       in_=bc_d[0:1, 2:3].broadcast_to((128, 1))),
                   reads=["bc_d"], writes=["bs"], sem="DC", inc=16)
            P.emit("sync", lambda e: e.dma_start(
                       out=mv[:, MV_THR],
                       in_=bc_d[0:1, 0:1].broadcast_to((128, 1))),
                   reads=["bc_d"], writes=["thr"], sem="DC", inc=16)
            P.emit("sync", lambda e: e.dma_start(
                       out=mv[:, MV_SC],
                       in_=bc_d[0:1, 1:2].broadcast_to((128, 1))),
                   reads=["bc_d"], writes=["sc"], sem="DC", inc=16)
            P.emit("v", lambda e: e.tensor_scalar(mv[:, MV_NTHR], mv[:, MV_THR],
                                                  -1.0, None, OP.mult),
                   reads=["thr"], writes=["nthr"])

        def ternarize(dc):
            if True:
                buf, rkeys = (scrke, ["e2", "e3"]) if dc % 2 == 0 \
                    else (scrv, ["v2", "v3"])
                wpa = [f"pa{dc}_0", f"pa{dc}_1"]
                wpb = [f"pb{dc}_0", f"pb{dc}_1"]
                P.emit("gp",
                       lambda e, dc=dc, buf=buf: e.tensor_tensor(
                           buf[:, D:2 * D], pA[dc][:, 0:D], wq[dc][:, :],
                           OP.add),
                       reads=wpa + [f"wq{dc}"], writes=rkeys)
                P.emit("gp",
                       lambda e, dc=dc, buf=buf: e.tensor_tensor(
                           buf[:, D:2 * D], buf[:, D:2 * D], pB[dc][:, 0:D],
                           OP.add),
                       reads=rkeys + wpb, writes=rkeys)
                P.emit("v",
                       lambda e, dc=dc, buf=buf: e.tensor_scalar(
                           wq[dc][:, :], buf[:, D:2 * D], mv[:, MV_THR], -1.0,
                           OP.is_gt, OP.add),
                       reads=rkeys + ["thr", f"wq{dc}"], writes=[f"wq{dc}"])
                P.emit("v",
                       lambda e, dc=dc, buf=buf: e.scalar_tensor_tensor(
                           wq[dc][:, :], buf[:, D:2 * D], mv[:, MV_NTHR],
                           wq[dc][:, :], OP.is_gt, OP.add),
                       reads=rkeys + ["nthr", f"wq{dc}"], writes=[f"wq{dc}"])

        def wqkv_bundle(dc):
            a = (dc % 2) * 384
            akeys = _qk("v", a, a + 384)
            P.emit("sync",
                   lambda e, dc=dc, a=a: e.dma_start(
                       out=scrv[:, a:a + 384],
                       in_=wqkv_d[dc * 128:(dc + 1) * 128, :]),
                   writes=akeys, sem="WQ" if dc % 2 == 0 else "WQ2", inc=16)
            P.emit("s",
                   lambda e, dc=dc, a=a: e.activation(
                       rowpool[:, 0:384], scrv[:, a:a + 384], ACT.Copy,
                       scale=vecs[:, V_GAT][:, dc:dc + 1]),
                   reads=akeys + ["vecs", "wgf"], writes=["wgf", "xrp2"])
            P.emit("t",
                   lambda e, dc=dc: e.matmul(ps[6][0:1, 0:384], ones_col[:, 0:1],
                                             rowpool[:, 0:384],
                                             start=(dc == 0), stop=(dc == NDC - 1)),
                   reads=["wgf", "ones_col"],
                   writes=["ps6"] if dc == NDC - 1 else [],
                   accum_writes=["ps6"] if dc != NDC - 1 else [])
            P.emit("s", lambda e, dc=dc: e.copy(wqgh[dc][:, :],
                                                rowpool[:, 0:384]),
                   reads=["wgf"], writes=[f"wqgh{dc}"])
            P.emit("v", lambda e, dc=dc: e.tensor_tensor(
                       wqgl[dc][:, :], rowpool[:, 0:384], wqgh[dc][:, :],
                       OP.subtract),
                   reads=["wgf", f"wqgh{dc}"], writes=[f"wqgl{dc}"])
            P.emit("t",
                   lambda e, dc=dc, a=a: e.matmul(
                       ps[7][0:1, 0:384], vecs[:, V_BAT][:, dc:dc + 1],
                       scrv[:, a:a + 384], start=(dc == 0), stop=(dc == NDC - 1)),
                   reads=akeys + ["vecs"],
                   writes=["ps7"] if dc == NDC - 1 else [],
                   accum_writes=["ps7"] if dc != NDC - 1 else [])

        def wqkv_evacs():
            P.emit("v", lambda e: e.tensor_copy(s1s2[0:1, :], ps[6][0:1, 0:384]),
                   reads=["ps6"], writes=["s1"])
            P.emit("v", lambda e: e.tensor_copy(scrke[0:1, 1536:1920],
                                                ps[7][0:1, 0:384]),
                   reads=["ps7"], writes=["e3"])
            P.emit("gp", lambda e: e.dma_start(out=s1s2[1:2, :],
                                               in_=scrke[0:1, 1536:1920]),
                   reads=["e3"], writes=["s2"], sem="DR", inc=16)

        for nt in range(NNT):
            buf, keys = xbuf(nt)
            P.emit("v",
                   lambda e, nt=nt, buf=buf: e.tensor_reduce(
                       stats[:, ST_S][:, nt:nt + 1], buf, AX, OP.add),
                   reads=keys, accum_writes=["st_s"])
            P.emit("s",
                   lambda e, nt=nt, buf=buf: e.activation(
                       scrke[:, 0:D], buf, ACT.Square,
                       accum_out=stats[:, ST_Q][:, nt:nt + 1]),
                   reads=keys, writes=["e0", "e1"], accum_writes=["st_q"])
            for dc in range(NDC):
                slot = (nt * NDC + dc) % 6
                P.emit("t",
                       lambda e, buf=buf, dc=dc, slot=slot: e.transpose(
                           ps[slot][:, 0:128],
                           buf[:, dc * 128:(dc + 1) * 128], ident[:]),
                       reads=keys + ["ident"], writes=[f"ps{slot}"])
                if dc % 2 == 0:
                    P.emit("s",
                           lambda e, dc=dc, nt=nt, slot=slot: e.copy(
                               h[dc][:, nt::16], ps[slot][:, 0:128]),
                           reads=[f"ps{slot}"], accum_writes=hq(dc))
                else:
                    P.emit("v",
                           lambda e, dc=dc, nt=nt, slot=slot: e.tensor_copy(
                               h[dc][:, nt::16], ps[slot][:, 0:128]),
                           reads=[f"ps{slot}"], accum_writes=hq(dc))
            if nt + 2 < 8:
                xload(nt + 2)
            if nt < NDC:
                pass1_dc(nt)
            if nt == 7:
                beta_tail()
            if nt >= 8:
                wqkv_bundle(nt - 8)
                ternarize(nt - 8)
            if 8 <= nt + 2 < NNT:
                xload(nt + 2)

        _dr_i = [0]

        def _dr():
            _dr_i[0] += 1
            return "DR" if _dr_i[0] % 2 == 0 else "DR2"

        def rowshuffle_out(pslot, field, fkey, rkey):
            P.emit("gp", lambda e: e.dma_start(
                       out=rowpool[pslot:pslot + 1, :].rearrange(
                           "o (p c) -> o p c", c=16),
                       in_=stats[:, field]),
                   reads=[fkey, "wgf", "xrp", "xrp2"], writes=[rkey], sem=_dr(), inc=16)

        def rowshuffle_bf(pslot, field, bfield, fkey, rkey):
            b0 = slice(16 * bfield, 16 * bfield + 16)
            b1 = slice(16 * bfield + 16, 16 * bfield + 32)
            P.emit("v", lambda e: e.tensor_copy(statsbf[:, b0], stats[:, field]),
                   reads=[fkey], writes=[f"sbf{bfield}"])
            P.emit("v", lambda e: e.tensor_tensor(
                       statsbf[:, b1], stats[:, field], statsbf[:, b0],
                       OP.subtract),
                   reads=[fkey, f"sbf{bfield}"], writes=[f"sbf{bfield}l"])
            P.emit("gp", lambda e: e.dma_start(
                       out=rowbf[pslot:pslot + 1, :].rearrange(
                           "o (p c) -> o p c", c=16),
                       in_=statsbf[:, b0]),
                   reads=[f"sbf{bfield}"], writes=[rkey], sem=_dr(), inc=16)
            P.emit("gp", lambda e: e.dma_start(
                       out=rowbf[pslot + 1:pslot + 2, :].rearrange(
                           "o (p c) -> o p c", c=16),
                       in_=statsbf[:, b1]),
                   reads=[f"sbf{bfield}l"], accum_writes=[rkey], sem=_dr(), inc=16)

        def stats_math_common(mbias=False):
            if mbias:
                P.emit("v", lambda e: e.tensor_scalar(
                           stats[:, ST_M], stats[:, ST_S], INV_D,
                           mv[:, MV_BS], OP.mult, OP.add),
                       reads=["st_s", "bs"], writes=["st_m"])
            else:
                P.emit("v", lambda e: e.tensor_scalar(
                           stats[:, ST_M], stats[:, ST_S], INV_D, None,
                           OP.mult),
                       reads=["st_s"], writes=["st_m"])
            P.emit("v", lambda e: e.tensor_tensor(stats[:, ST_T0], stats[:, ST_M],
                                                  stats[:, ST_M], OP.mult),
                   reads=["st_m"], writes=["st_nmrf"])
            P.emit("v", lambda e: e.scalar_tensor_tensor(
                       stats[:, ST_VAR], stats[:, ST_Q], INV_D, stats[:, ST_T0],
                       OP.mult, OP.subtract),
                   reads=["st_q", "st_nmrf"], writes=["st_var"])

        def stats_math_ffn(mbias=False):
            stats_math_common(mbias)
            P.emit("v", lambda e: e.tensor_scalar(
                       stats[:, ST_IRF], stats[:, ST_VAR], 1.0 + 1e-3, 1e-8,
                       OP.mult, OP.add),
                   reads=["st_var"], writes=["st_irf"])
            P.emit("s", lambda e: e.activation(stats[:, ST_IRF], stats[:, ST_IRF],
                                               ACT.Ln),
                   reads=["st_irf"], writes=["st_irf"])
            P.emit("s", lambda e: e.activation(stats[:, ST_RF], stats[:, ST_IRF],
                                               ACT.Exp, scale=-0.5),
                   reads=["st_irf"], writes=["st_rf"])
            P.emit("v", lambda e: e.scalar_tensor_tensor(
                       stats[:, ST_NMRF], stats[:, ST_M], -1.0, stats[:, ST_RF],
                       OP.mult, OP.mult),
                   reads=["st_m", "st_rf"], writes=["st_nmrf"])
            rowshuffle_bf(32, ST_RF, 0, "st_rf", "row32")
            rowshuffle_bf(64, ST_NMRF, 0, "st_nmrf", "row64")

        def fm_stats(pairs=False, sum_skip=False, acc_sq=False):
            for dc in range(NDC):
                bb, bkeys = (scrq, [f"q{t}" for t in range(NTC)]) if dc % 2 == 0 \
                    else (scrke, [f"e{t}" for t in range(NTC)])
                if acc_sq:
                    pass
                elif dc in (5, 7):
                    P.emit("v", lambda e, dc=dc, bb=bb: e.tensor_tensor(
                               bb[:], h[dc][:], h[dc][:], OP.mult),
                           reads=hq(dc), writes=bkeys)
                elif dc in (4, 6):
                    P.emit("gp", lambda e, dc=dc, bb=bb: e.tensor_tensor(
                               bb[:], h[dc][:], h[dc][:], OP.mult),
                           reads=hq(dc), writes=bkeys)
                else:
                    P.emit("s", lambda e, dc=dc, bb=bb: e.activation(
                               bb[:], h[dc][:], ACT.Square),
                           reads=hq(dc), writes=bkeys)
                for tc in range(NTC):
                    if sum_skip:
                        pass
                    elif pairs:
                        def summ(e, dc=dc, tc=tc):
                            e.matmul(ps[tc][0:1, :], onesb[:, 0:1],
                                     pA[dc][:, tsl(tc)],
                                     start=(dc == 0), stop=False)
                            return e.matmul(ps[tc][0:1, :], onesb[:, 0:1],
                                            pB[dc][:, tsl(tc)],
                                            start=False, stop=(dc == NDC - 1))
                        P.emit("t", summ,
                               reads=[f"pa{dc}_{tc}", f"pb{dc}_{tc}", "onesb"],
                               writes=[f"ps{tc}"] if dc == NDC - 1 else [],
                               accum_writes=[f"ps{tc}"] if dc != NDC - 1 else [])
                    else:
                        P.emit("t",
                               lambda e, dc=dc, tc=tc: e.matmul(
                                   ps[tc][0:1, :], ones_col[:, 0:1],
                                   h[dc][:, tsl(tc)],
                                   start=(dc == 0), stop=(dc == NDC - 1)),
                               reads=[hr(dc, tc), "ones_col"],
                               writes=[f"ps{tc}"] if dc == NDC - 1 else [],
                               accum_writes=[f"ps{tc}"] if dc != NDC - 1 else [])
                    if not acc_sq:
                        P.emit("t",
                               lambda e, dc=dc, tc=tc, bb=bb: e.matmul(
                                   ps[4 + tc][0:1, :], ones_col[:, 0:1],
                                   bb[:, tsl(tc)], start=(dc == 0),
                                   stop=(dc == NDC - 1)),
                               reads=[bkeys[tc], "ones_col"],
                               writes=[f"ps{4 + tc}"] if dc == NDC - 1 else [],
                               accum_writes=[f"ps{4 + tc}"]
                               if dc != NDC - 1 else [])
            if acc_sq:
                for tc in range(NTC):
                    P.emit("t", lambda e, tc=tc: e.matmul(
                               ps[4 + tc][0:1, :], ones_col[:, 0:1],
                               scrq[:, tsl(tc)], start=True, stop=True),
                           reads=[f"q{tc}", "ones_col"],
                           writes=[f"ps{4 + tc}"])
            for tc in range(NTC):
                if not sum_skip:
                    P.emit("v", lambda e, tc=tc: e.tensor_copy(
                               rowpool[0:1, tsl(tc)], ps[tc][0:1, :]),
                           reads=[f"ps{tc}"], accum_writes=["row0", "xrp", "xrp2"])
                P.emit("v", lambda e, tc=tc: e.tensor_copy(
                           rowpool[32:33, tsl(tc)], ps[4 + tc][0:1, :]),
                       reads=[f"ps{4 + tc}"], accum_writes=["row32", "xrp", "xrp2"])
            P.emit("gp", lambda e: e.dma_start(
                       out=stats[:, ST_S],
                       in_=rowpool[0:1, :].rearrange("o (p c) -> o p c", c=16)),
                   reads=["row0"], writes=["st_s"], sem="DR2", inc=16)
            P.emit("gp", lambda e: e.dma_start(
                       out=stats[:, ST_Q],
                       in_=rowpool[32:33, :].rearrange("o (p c) -> o p c", c=16)),
                   reads=["row32"], writes=["st_q"], sem="DR", inc=16)

        def ffn(first, mid_cb=None, out_cb=None, split=False):
            for tc in range(NTC):
                P.emit("t", lambda e, tc=tc: e.matmul(
                           ps[tc][:, :], lhsbf[32:34, :], rowbf[32:34, tsl(tc)],
                           start=True, stop=True),
                       reads=["row32", "lhsbf"], writes=[f"ps{tc}"])
                bslot = 4 + (tc % 2)
                P.emit("t", lambda e, tc=tc, bslot=bslot: e.matmul(
                           ps[bslot][:, :], lhsbf[64:66, :], rowbf[64:66, tsl(tc)],
                           start=True, stop=True),
                       reads=["row64", "lhsbf"], writes=[f"ps{bslot}"])
                P.emit("v", lambda e, tc=tc, bslot=bslot: e.tensor_copy(
                           xstage[tc // 2][:, (tc % 2) * TW:(tc % 2) * TW + TW],
                           ps[bslot][:, :]),
                       reads=[f"ps{bslot}"], accum_writes=[f"xst{tc // 2}"])
            def prod_front(dc):
                bb, bkeys = (scrke, [f"e{t}" for t in range(NTC)]) if dc % 2 == 0 \
                    else (scrv, [f"v{t}" for t in range(NTC)])
                for tc in range(NTC):
                    P.emit("v", lambda e, dc=dc, tc=tc, bb=bb: e.tensor_tensor(
                               bb[:, tsl(tc)], h[dc][:, tsl(tc)],
                               ps[tc][:, :], OP.mult),
                           reads=[hr(dc, tc), f"ps{tc}"], writes=[bkeys[tc]])
                    if tc % 2 == 1:
                        P.emit("gp",
                               lambda e, tc=tc, bb=bb: e.tensor_tensor(
                                   bb[:, (tc - 1) * TW:(tc + 1) * TW],
                                   bb[:, (tc - 1) * TW:(tc + 1) * TW],
                                   xstage[tc // 2][:], OP.add),
                               reads=[bkeys[tc - 1], bkeys[tc], f"xst{tc // 2}"],
                               writes=[bkeys[tc - 1], bkeys[tc]])
            def prod_lo(dc):
                bb, bkeys = (scrke, [f"e{t}" for t in range(NTC)]) if dc % 2 == 0 \
                    else (scrv, [f"v{t}" for t in range(NTC)])
                P.emit("v",
                       lambda e, dc=dc, bb=bb: e.tensor_tensor(
                           pB[dc][:], bb[:], pA[dc][:], OP.subtract),
                       reads=bkeys + [f"pa{dc}_{t}" for t in range(NTC)],
                       writes=[f"pb{dc}_{t}" for t in range(NTC)])

            def prod_back(dc, with_lo=True):
                bb, bkeys = (scrke, [f"e{t}" for t in range(NTC)]) if dc % 2 == 0 \
                    else (scrv, [f"v{t}" for t in range(NTC)])
                P.emit("s", lambda e, dc=dc, bb=bb: e.mul(
                           bb[:], bb[:], vecs[:, V_GFF][:, dc:dc + 1]),
                       reads=bkeys + ["vecs"], writes=bkeys)
                P.emit("s", lambda e, dc=dc, bb=bb: e.add(
                           bb[:], bb[:], vecs[:, V_BFF][:, dc:dc + 1]),
                       reads=bkeys + ["vecs"], writes=bkeys)
                P.emit("v", lambda e, dc=dc, bb=bb: e.tensor_reduce(
                           mv[:, MV_AM][:, dc:dc + 1], bb[:], AX, OP.max,
                           apply_absolute_value=True),
                       reads=bkeys, writes=[f"am{dc}"])
                P.emit("s", lambda e, dc=dc, bb=bb: e.copy(pA[dc][:], bb[:]),
                       reads=bkeys, writes=[f"pa{dc}_{t}" for t in range(NTC)])
                if with_lo:
                    prod_lo(dc)

            for dc in range(NDC):
                prod_front(dc)
                if dc > 0:
                    prod_back(dc - 1)
            prod_back(NDC - 1, with_lo=False)
            def xr_slot(dc):
                if dc < 4:
                    return scrke, (dc % 4) * TW, f"e{dc % 4}"
                return scrv, (dc % 4) * TW, f"v{dc % 4}"

            def xr_stage(tc):
                for dc in range(NDC):
                    buf, xa, xrn = xr_slot(dc)
                    P.emit("gp", lambda e, dc=dc, tc=tc, buf=buf, xa=xa:
                           e.tensor_tensor(
                               buf[:, xa:xa + TW], pA[dc][:, tsl(tc)],
                               pB[dc][:, tsl(tc)], OP.add),
                           reads=[f"pa{dc}_{tc}", f"pb{dc}_{tc}"], writes=[xrn])

            def clip_stage(tc):
                for dc in range(NDC):
                    buf, xa, xrn = xr_slot(dc)
                    P.emit("v", lambda e, buf=buf, xa=xa: e.tensor_scalar(
                               buf[:, xa:xa + TW], buf[:, xa:xa + TW],
                               mv[:, MV_NCBC], mv[:, MV_CBC], OP.max, OP.min),
                           reads=[xrn, "ncbc", "cbc"], writes=[xrn])
                for dc in range(NDC):
                    buf, xa, xrn = xr_slot(dc)
                    P.emit("s", lambda e, dc=dc, tc=tc, buf=buf, xa=xa: e.copy(
                               pA[dc][:, tsl(tc)], buf[:, xa:xa + TW]),
                           reads=[xrn], writes=[f"pa{dc}_{tc}"])
                for dc in range(NDC):
                    buf, xa, xrn = xr_slot(dc)
                    P.emit("v", lambda e, dc=dc, tc=tc, buf=buf, xa=xa:
                           e.tensor_tensor(
                               pB[dc][:, tsl(tc)], buf[:, xa:xa + TW],
                               pA[dc][:, tsl(tc)], OP.subtract),
                           reads=[xrn, f"pa{dc}_{tc}"], writes=[f"pb{dc}_{tc}"])

            P.emit("v", lambda e: e.tensor_reduce(mv[:, MV_LMAX], mv[:, MV_AM],
                                                  AX, OP.max),
                   reads=[f"am{dc}" for dc in range(NDC)], writes=["lmax"])
            P.emit("t", lambda e: e.transpose(ps[0][0:1, 0:128], mv[:, MV_LMAX],
                                              ident[:]),
                   reads=["lmax", "ident"], writes=["ps0"])
            P.emit("v", lambda e: e.tensor_reduce(gmaxrow[0:1, 0:1],
                                                  ps[0][0:1, 0:128], AX, OP.max),
                   reads=["ps0"], writes=["gmaxrow"])
            P.emit("sync", lambda e: e.dma_start(out=cc_in[:], in_=gmaxrow[0:1, :]),
                   reads=["gmaxrow"], writes=["ccin"], sem="DC", inc=16)
            P.emit("gp", lambda e: e.collective_compute(
                       "AllReduce", OP.max, replica_groups=[list(range(ncores))],
                       ins=[cc_in], outs=[cc_out]),
                   reads=["ccin"], writes=["ccout"], sem="SC", inc=1)
            prod_lo(NDC - 1)
            xr_stage(0)
            if mid_cb is not None:
                mid_cb()
            if debug:
                P.emit("sync", lambda e: e.dma_start(out=gmaxrow[0:1, :],
                                                     in_=cc_out[:]),
                       reads=["ccout"], writes=["gmaxrow"], sem="DC", inc=16)
            if debug and first:
                P.emit("sync", lambda e: e.dma_start(out=dbg["dbg_gamma"],
                                                     in_=gmaxrow[0:1, :]),
                       reads=["gmaxrow"], sem="DO", inc=16)
            P.emit("sync", lambda e: e.dma_start(
                       out=mv[:, MV_CBC],
                       in_=cc_out[0:1, 0:1].broadcast_to((128, 1))),
                   reads=["ccout"], writes=["cbc"], sem="DC", inc=16)
            P.emit("v", lambda e: e.tensor_scalar(mv[:, MV_NCBC], mv[:, MV_CBC],
                                                  -K1, None, OP.mult),
                   reads=["cbc"], writes=["ncbc"])
            P.emit("v", lambda e: e.tensor_scalar(mv[:, MV_CBC], mv[:, MV_CBC],
                                                  K1, None, OP.mult),
                   reads=["cbc"], writes=["cbc"])

            clip_stage(0)
            for tc in range(NTC):
                for uc in range(NDC):
                    def mmgroup(e, uc=uc, tc=tc):
                        n = 0
                        last = None
                        for dc in range(NDC):
                            for pa in (pA, pB):
                                last = e.matmul(
                                    ps[uc][:, :],
                                    wq[dc][:, uc * 128:(uc + 1) * 128],
                                    pa[dc][:, tsl(tc)],
                                    start=(n == 0), stop=(n == 15))
                                n += 1
                        return last
                    P.emit("t", mmgroup,
                           reads=[f"pa{dc}_{tc}" for dc in range(NDC)] +
                                 [f"pb{dc}_{tc}" for dc in range(NDC)] +
                                 [f"wq{dc}" for dc in range(NDC)],
                           writes=[f"ps{uc}"])
                if tc + 1 < NTC:
                    xr_stage(tc + 1)
                    clip_stage(tc + 1)
                for uc in range(NDC):
                    P.emit("v", lambda e, uc=uc, tc=tc:
                           e.scalar_tensor_tensor(
                               h[uc][:, tsl(tc)], ps[uc][:, :], mv[:, MV_SC],
                               h[uc][:, tsl(tc)], OP.mult, OP.add),
                           reads=[f"ps{uc}", "sc", hr(uc, tc)],
                           writes=[hr(uc, tc)])
                    if out_cb is not None:
                        out_cb(uc, tc)
                    if split:
                        P.emit("s", lambda e, uc=uc, tc=tc: e.copy(
                                   pA[uc][:, tsl(tc)], h[uc][:, tsl(tc)]),
                               reads=[hr(uc, tc)], writes=[f"pa{uc}_{tc}"])
                        P.emit("v", lambda e, uc=uc, tc=tc: e.tensor_tensor(
                                   pB[uc][:, tsl(tc)], h[uc][:, tsl(tc)],
                                   pA[uc][:, tsl(tc)], OP.subtract),
                               reads=[hr(uc, tc), f"pa{uc}_{tc}"],
                               writes=[f"pb{uc}_{tc}"])
                        if uc == 0:
                            P.emit("s", lambda e, tc=tc: e.activation(
                                       scrq[:, tsl(tc)], h[0][:, tsl(tc)],
                                       ACT.Square),
                                   reads=[hr(0, tc)], writes=[f"q{tc}"])
                        else:
                            w = uc % 4
                            tmp = xstage[w // 2][:, (w % 2) * TW:
                                                 (w % 2) * TW + TW]
                            teng = ("v" if uc in (5, 7) else
                                    "gp" if uc in (2, 4, 6) else "s")
                            if teng == "s":
                                P.emit("s", lambda e, uc=uc, tc=tc, tmp=tmp:
                                       e.activation(tmp, h[uc][:, tsl(tc)],
                                                    ACT.Square),
                                       reads=[hr(uc, tc)],
                                       writes=[f"sqt{w}", f"xst{w // 2}"])
                            else:
                                P.emit(teng, lambda e, uc=uc, tc=tc, tmp=tmp:
                                       e.tensor_tensor(tmp, h[uc][:, tsl(tc)],
                                                       h[uc][:, tsl(tc)],
                                                       OP.mult),
                                       reads=[hr(uc, tc)],
                                       writes=[f"sqt{w}", f"xst{w // 2}"])
                            P.emit("v", lambda e, tc=tc, tmp=tmp:
                                   e.tensor_tensor(scrq[:, tsl(tc)],
                                                   scrq[:, tsl(tc)], tmp,
                                                   OP.add),
                                   reads=[f"q{tc}", f"sqt{w}"],
                                   writes=[f"q{tc}"])

        def stats_math_attn():
            stats_math_common()
            P.emit("v", lambda e: e.tensor_scalar(
                       stats[:, ST_R2], stats[:, ST_VAR], 1.0 + EPS, 1e-10,
                       OP.mult, OP.add),
                   reads=["st_var"], writes=["st_r2"])
            P.emit("s", lambda e: e.activation(stats[:, ST_R2], stats[:, ST_R2],
                                               ACT.Ln),
                   reads=["st_r2"], writes=["st_r2"])
            P.emit("s", lambda e: e.activation(stats[:, ST_IRF], stats[:, ST_R2],
                                               ACT.Exp, scale=0.5),
                   reads=["st_r2"], writes=["st_irf"])
            P.emit("s", lambda e: e.activation(stats[:, ST_RF], stats[:, ST_R2],
                                               ACT.Exp, scale=-0.5),
                   reads=["st_r2"], writes=["st_rf"])
            P.emit("v", lambda e: e.tensor_scalar(stats[:, ST_R2],
                                                  stats[:, ST_VAR], EPS, None,
                                                  OP.add),
                   reads=["st_var"], writes=["st_r2"])
            P.emit("s", lambda e: e.activation(stats[:, ST_R2], stats[:, ST_R2],
                                               ACT.Ln),
                   reads=["st_r2"], writes=["st_r2"])
            P.emit("s", lambda e: e.activation(stats[:, ST_R2], stats[:, ST_R2],
                                               ACT.Exp, scale=-0.5),
                   reads=["st_r2"], writes=["st_r2"])
            P.emit("v", lambda e: e.scalar_tensor_tensor(
                       stats[:, ST_NMR2], stats[:, ST_M], -1.0, stats[:, ST_R2],
                       OP.mult, OP.mult),
                   reads=["st_m", "st_r2"], writes=["st_nmr2"])
            P.emit("v", lambda e: e.tensor_scalar(stats[:, ST_R2P1],
                                                  stats[:, ST_R2], 1.0, None,
                                                  OP.add),
                   reads=["st_r2"], writes=["st_r2p1"])
            P.emit("v", lambda e: e.tensor_scalar(stats[:, ST_NMRF],
                                                  stats[:, ST_M], -1.0, None,
                                                  OP.mult),
                   reads=["st_m"], writes=["st_nmrf"])
            rowshuffle_out(0, ST_NMRF, "st_nmrf", "row0")
            rowshuffle_out(1, ST_IRF, "st_irf", "row1")
            rowshuffle_bf(32, ST_RF, 0, "st_rf", "row32")

        def attn():
            qbuf = {0: (scrq, "q"), 1: (scrke, "e"), 2: (scrv, "v")}
            slot_i = [0]
            for tc in range(NTC):
                bslot = 6 + (tc % 2)
                P.emit("t", lambda e, tc=tc, bslot=bslot: e.matmul(
                           ps[bslot][:, :], lhsbf[32:34, :],
                           rowbf[32:34, tsl(tc)], start=True, stop=True),
                       reads=["row32", "lhsbf"], writes=[f"ps{bslot}"])
                P.emit("v", lambda e, tc=tc, bslot=bslot: e.tensor_copy(
                           xstage[tc % 2][:, 0:TW], ps[bslot][:, :]),
                       reads=[f"ps{bslot}"], writes=[f"xst{tc % 2}"])
                for uc in range(3):
                    slot = slot_i[0] % 6
                    slot_i[0] += 1

                    def qkvgroup(e, uc=uc, tc=tc, slot=slot):
                        n = 0
                        for wg, pa in ((wqgh, pA), (wqgh, pB), (wqgl, pA)):
                            for dc in range(NDC):
                                e.matmul(ps[slot][:, :],
                                         wg[dc][:, uc * 128:(uc + 1) * 128],
                                         pa[dc][:, tsl(tc)],
                                         start=(n == 0), stop=False)
                                n += 1
                        return e.matmul(ps[slot][:, :],
                                        s1s2[0:2, uc * 128:(uc + 1) * 128],
                                        rowpool[0:2, tsl(tc)],
                                        start=False, stop=True)
                    P.emit("t", qkvgroup,
                           reads=[f"pa{dc}_{tc}" for dc in range(NDC)] +
                                 [f"pb{dc}_{tc}" for dc in range(NDC)] +
                                 [f"wqgh{dc}" for dc in range(NDC)] +
                                 [f"wqgl{dc}" for dc in range(NDC)] +
                                 ["s1", "s2", "row0", "row1"],
                           writes=[f"ps{slot}"])
                    buf, pre = qbuf[uc]
                    P.emit("v", lambda e, buf=buf, tc=tc, slot=slot:
                           e.tensor_tensor(buf[:, tsl(tc)], ps[slot][:, :],
                                           xstage[tc % 2][:, 0:TW], OP.mult),
                           reads=[f"ps{slot}", f"xst{tc % 2}"],
                           writes=[f"{pre}{tc}"])
            if debug:
                for pre, nmq, buf in (("q", "q", scrq), ("e", "k", scrke),
                                      ("v", "v", scrv)):
                    P.emit("sync", lambda e, nmq=nmq, buf=buf: e.dma_start(
                               out=dbg[f"dbg_{nmq}"], in_=buf[:]),
                           reads=[f"{pre}{tc}" for tc in range(NTC)],
                           sem="DO", inc=16)
            P.emit("v", lambda e: e.tensor_reduce(mv[:, MV_KNEG], scrke[:], AX,
                                                  OP.max, negate=True),
                   reads=[f"e{tc}" for tc in range(NTC)], writes=["kneg"])
            P.emit("s", lambda e: e.activation(scrke[:], scrke[:], ACT.Exp,
                                               bias=mv[:, MV_KNEG],
                                               accum_out=mv[:, MV_SSUM]),
                   reads=[f"e{tc}" for tc in range(NTC)] + ["kneg"],
                   writes=[f"e{tc}" for tc in range(NTC)] + ["ssum"])
            P.emit("v", lambda e: e.reciprocal(mv[:, MV_RS], mv[:, MV_SSUM]),
                   reads=["ssum"], writes=["rs"])
            P.emit("v", lambda e: e.tensor_scalar(mv[:, MV_CS], mv[:, MV_RS],
                                                  HSCALE, None, OP.mult),
                   reads=["rs"], writes=["cs"])
            for j in range(NNT):
                sl = slice(j * 128, (j + 1) * 128)
                tcj = (j * 128) // TW
                s0 = j % 4
                s1_ = 4 + (j % 2)
                P.emit("t", lambda e, sl=sl, s0=s0: e.transpose(
                           ps[s0][:, 0:128], scrke[:, sl], ident[:]),
                       reads=[f"e{tcj}", "ident"], writes=[f"ps{s0}"])
                P.emit("v", lambda e, sl=sl, s0=s0: e.tensor_copy(
                           scrke[:, sl], ps[s0][:, 0:128]),
                       reads=[f"ps{s0}"], accum_writes=[f"e{tcj}"])
                P.emit("t", lambda e, sl=sl, s1_=s1_: e.transpose(
                           ps[s1_][:, 0:128], scrv[:, sl], ident[:]),
                       reads=[f"v{tcj}", "ident"], writes=[f"ps{s1_}"])
                P.emit("s", lambda e, sl=sl, s1_=s1_: e.copy(
                           scrv[:, sl], ps[s1_][:, 0:128]),
                       reads=[f"ps{s1_}"], accum_writes=[f"v{tcj}"])

            def ctxgroup(e):
                last = None
                for j in range(NNT):
                    sl = slice(j * 128, (j + 1) * 128)
                    last = e.matmul(ps[6][:, 0:128], scrke[:, sl], scrv[:, sl],
                                    start=(j == 0), stop=(j == NNT - 1))
                return last
            P.emit("t", ctxgroup,
                   reads=[f"e{tc}" for tc in range(NTC)] +
                         [f"v{tc}" for tc in range(NTC)],
                   writes=["ps6"])
            for hd in range(4):
                b = 32 * hd
                P.emit("v", lambda e, b=b: e.tensor_scalar(
                           bdm[b:b + 32, b:b + 32], ps[6][b:b + 32, b:b + 32],
                           mv[:, MV_CS][b:b + 32, 0:1], None, OP.mult),
                       reads=["ps6", "cs"], accum_writes=["bd"])
            for tc in range(NTC):
                slot = tc % 2
                P.emit("t", lambda e, tc=tc, slot=slot: e.matmul(
                           ps[slot][:, :], bdm[:], scrq[:, tsl(tc)],
                           start=True, stop=True),
                       reads=["bd", f"q{tc}"], writes=[f"ps{slot}"])
                P.emit("v", lambda e, tc=tc, slot=slot: e.tensor_copy(
                           scrq[:, tsl(tc)], ps[slot][:, :]),
                       reads=[f"ps{slot}"], writes=[f"q{tc}"])
            for tc in range(NTC):
                P.emit("t", lambda e, tc=tc: e.matmul(
                           ps[0][0:1, :], mv[:, MV_WRS], scrq[:, tsl(tc)],
                           start=True, stop=True),
                       reads=["wrs", f"q{tc}"], writes=["ps0"])
                P.emit("v", lambda e, tc=tc: e.scalar_tensor_tensor(
                           rowpool[0:1, tsl(tc)], rowpool[0:1, tsl(tc)],
                           -float(D), ps[0][0:1, :], OP.mult, OP.add),
                       reads=["row0", "ps0"], writes=["row0"])
            rowshuffle_bf(32, ST_NMR2, 0, "st_nmr2", "row32")
            rowshuffle_bf(64, ST_R2P1, 0, "st_r2p1", "row64")
            for tc in range(NTC):
                slot = 2 + (tc % 2)
                P.emit("t", lambda e, tc=tc, slot=slot: e.matmul(
                           ps[slot][:, :], lhsbf[64:66, :],
                           rowbf[64:66, tsl(tc)], start=True, stop=True),
                       reads=["row64", "lhsbf"], writes=[f"ps{slot}"])
                P.emit("v", lambda e, tc=tc, slot=slot: e.tensor_copy(
                           scrke[:, tsl(tc)], ps[slot][:, :]),
                       reads=[f"ps{slot}"], writes=[f"e{tc}"])
            P.emit("s", lambda e: e.copy(pA[0][:], scrq[:]),
                   reads=[f"q{t}" for t in range(NTC)],
                   writes=[f"pa0_{t}" for t in range(NTC)])
            P.emit("v", lambda e: e.tensor_tensor(pB[0][:], scrq[:], pA[0][:],
                                                  OP.subtract),
                   reads=[f"q{t}" for t in range(NTC)] +
                         [f"pa0_{t}" for t in range(NTC)],
                   writes=[f"pb0_{t}" for t in range(NTC)])
            slot_i = [0]
            for tc in range(NTC):
                for uc in range(NDC):
                    slot = 4 + (slot_i[0] % 4)
                    slot_i[0] += 1

                    def opgroup(e, uc=uc, tc=tc, slot=slot):
                        sl = slice(uc * 128, (uc + 1) * 128)
                        e.matmul(ps[slot][:, :], wouth[:, sl],
                                 pA[0][:, tsl(tc)], start=True, stop=False)
                        e.matmul(ps[slot][:, :], wouth[:, sl],
                                 pB[0][:, tsl(tc)], start=False, stop=False)
                        e.matmul(ps[slot][:, :], woutl[:, sl],
                                 pA[0][:, tsl(tc)], start=False, stop=False)
                        return e.matmul(ps[slot][:, :], lhsbf[32:34, 0:128],
                                        rowbf[32:34, tsl(tc)],
                                        start=False, stop=True)
                    P.emit("t", opgroup,
                           reads=["wouth", "woutl", f"pa0_{tc}", f"pb0_{tc}",
                                  "row32", "lhsbf"],
                           writes=[f"ps{slot}"])
                    va = (uc % 4) * TW
                    P.emit("v", lambda e, uc=uc, tc=tc, va=va: e.tensor_tensor(
                               scrv[:, va:va + TW], h[uc][:, tsl(tc)],
                               scrke[:, tsl(tc)], OP.mult),
                           reads=[hr(uc, tc), f"e{tc}"], writes=[f"v{uc % 4}"])
                    P.emit("v", lambda e, uc=uc, tc=tc, slot=slot, va=va:
                           e.scalar_tensor_tensor(
                               h[uc][:, tsl(tc)], ps[slot][:, :],
                               vecs[:, V_BOUT][:, uc:uc + 1], scrv[:, va:va + TW],
                               OP.add, OP.add),
                           reads=[f"ps{slot}", "vecs", f"v{uc % 4}"],
                           writes=[hr(uc, tc)])
                    if uc == 0:
                        P.emit("s", lambda e, tc=tc: e.activation(
                                   scrq[:, tsl(tc)], h[0][:, tsl(tc)],
                                   ACT.Square),
                               reads=[hr(0, tc)], writes=[f"q{tc}"])
                    else:
                        w = uc % 4
                        tmp = xstage[w // 2][:, (w % 2) * TW:(w % 2) * TW + TW]
                        if uc in (2, 4, 6):
                            P.emit("gp", lambda e, uc=uc, tc=tc, tmp=tmp:
                                   e.tensor_tensor(tmp, h[uc][:, tsl(tc)],
                                                   h[uc][:, tsl(tc)], OP.mult),
                                   reads=[hr(uc, tc)],
                                   writes=[f"sqt{w}", f"xst{w // 2}"])
                        else:
                            P.emit("s", lambda e, uc=uc, tc=tc, tmp=tmp:
                                   e.activation(tmp, h[uc][:, tsl(tc)],
                                                ACT.Square),
                                   reads=[hr(uc, tc)],
                                   writes=[f"sqt{w}", f"xst{w // 2}"])
                        P.emit("v", lambda e, tc=tc, tmp=tmp:
                               e.tensor_tensor(scrq[:, tsl(tc)],
                                               scrq[:, tsl(tc)], tmp, OP.add),
                               reads=[f"q{tc}", f"sqt{w}"], writes=[f"q{tc}"])

        stats_math_ffn()
        wqkv_evacs()
        ffn(first=True, split=True)
        if debug:
            for dc in range(NDC):
                P.emit("sync", lambda e, dc=dc: e.dma_start(
                           out=dbg["dbg_h1"][dc * 128:(dc + 1) * 128, :],
                           in_=h[dc][:]),
                       reads=hq(dc), sem="DO", inc=16)
        fm_stats(pairs=True, acc_sq=True)
        stats_math_attn()
        attn()
        if debug:
            for dc in range(NDC):
                P.emit("sync", lambda e, dc=dc: e.dma_start(
                           out=dbg["dbg_h2"][dc * 128:(dc + 1) * 128, :],
                           in_=h[dc][:]),
                       reads=hq(dc), sem="DO", inc=16)
        fm_stats(sum_skip=True, acc_sq=True)
        stats_math_ffn(mbias=True)

        Y_SEMS = ("DO", "DO2", "XA", "XB")
        _yn = [0]

        def ydma(uc, tc):
            sem = Y_SEMS[_yn[0] % 4]
            _yn[0] += 1
            P.emit("sync", lambda e, uc=uc, tc=tc: e.dma_start(
                       out=y_d[uc * 128:(uc + 1) * 128, tc * TW:(tc + 1) * TW],
                       in_=h[uc][:, tsl(tc)]),
                   reads=[hr(uc, tc)], writes=[f"y{uc}_{tc}"], sem=sem, inc=16)

        ffn(first=False, out_cb=ydma)

        final_y = {s: P.cnt[s] for s in Y_SEMS}

        def make_runner(qname):
            items = P.q[qname]

            def run(eng):
                for waits, fn, sem, inc in items:
                    for s_, v_ in waits:
                        eng.wait_ge(sems[s_], v_)
                    last = fn(eng)
                    if last is not None:
                        last.then_inc(sems[sem], inc)
                if qname == "sync":
                    for s_, v_ in final_y.items():
                        eng.wait_ge(sems[s_], v_)
            return run

        block.sync(make_runner("sync"))
        block.gpsimd(make_runner("gp"))
        block.vector(make_runner("v"))
        block.scalar(make_runner("s"))
        block.tensor(make_runner("t"))

    return nc


_CACHE = {}


def _get_nc(debug=False, ncores=8):
    key = (debug, ncores)
    if key not in _CACHE:
        _CACHE[key] = build(debug, ncores)
    return _CACHE[key]


def _pack_vec(v):
    return np.ascontiguousarray(np.asarray(v, np.float32).reshape(8, 128).T)


def make_in_maps(L0_attn_x, L0_attn_gamma, L0_attn_beta, L0_attn_w_qkv,
                 L0_attn_w_out, L0_attn_b_out, L0_ffn_weight, L0_ffn_ln_gamma,
                 L0_ffn_ln_beta):
    x = np.asarray(L0_attn_x, np.float32)
    vecs = np.concatenate([
        _pack_vec(L0_attn_b_out), _pack_vec(L0_ffn_ln_gamma),
        _pack_vec(L0_ffn_ln_beta),
        _pack_vec(np.asarray(L0_attn_gamma).reshape(-1)),
        _pack_vec(np.asarray(L0_attn_beta).reshape(-1))], axis=1)
    vecs = np.ascontiguousarray(vecs)
    wffn = np.ascontiguousarray(np.asarray(L0_ffn_weight, np.float32))
    wqkv = np.ascontiguousarray(np.asarray(L0_attn_w_qkv, np.float32))
    wout = np.ascontiguousarray(np.asarray(L0_attn_w_out, np.float32))
    return [{"x": np.ascontiguousarray(x[c]), "wffn": wffn, "wqkv": wqkv,
             "wout": wout, "vecs": vecs} for c in range(8)]


def kernel(L0_attn_x, L0_attn_gamma, L0_attn_beta, L0_attn_w_qkv, L0_attn_w_out,
           L0_attn_b_out, L0_ffn_weight, L0_ffn_ln_gamma, L0_ffn_ln_beta):
    nc = _get_nc(False)
    in_maps = make_in_maps(L0_attn_x, L0_attn_gamma, L0_attn_beta, L0_attn_w_qkv,
                           L0_attn_w_out, L0_attn_b_out, L0_ffn_weight,
                           L0_ffn_ln_gamma, L0_ffn_ln_beta)
    res = run_bass_kernel_spmd(nc, in_maps, core_ids=list(range(8)))
    return np.stack(
        [np.asarray(r["y"]).reshape(D, 128, 16).transpose(2, 1, 0)
         .reshape(T, D) for r in res.results], axis=0)


# version 55
# speedup vs baseline: 1.0182x; 1.0092x over previous
import contextlib

import numpy as np

import concourse.bass as bass
import concourse.mybir as mybir
from concourse.bass_utils import run_bass_kernel_spmd

F32 = mybir.dt.float32
BF16 = mybir.dt.bfloat16
AX = mybir.AxisListType.X
OP = mybir.AluOpType
ACT = mybir.ActivationFunctionType

T = 2048
D = 1024
NDC = 8
NTC = 4
TW = 512
NNT = 16
EPS = 1e-5
K1 = float(np.float32(1.0 - EPS / 128.0))
INV_D = 1.0 / D
HSCALE = float(32.0 ** -0.5)

DMA_SEMS = ("DX", "DX2", "DW", "DW2", "DR", "DR2", "DC", "DO", "DO2", "SC",
            "XA", "XB", "XC", "XD", "WQ", "WQ2")
ENG_SEM = {"v": "SV", "s": "SS", "t": "ST", "gp": "SG"}


class Plan:

    def __init__(self):
        self.q = {k: [] for k in ("sync", "gp", "v", "s", "t")}
        self.cnt = {s: 0 for s in ("SV", "SS", "ST", "SG") + DMA_SEMS}
        self.seen = {k: {} for k in self.q}
        self.reg = {}

    def _dep(self, qn, sem, val, waits):
        if sem in DMA_SEMS:
            val = self.cnt[sem]
        if val <= 0 or self.seen[qn].get(sem, 0) >= val:
            return
        self.seen[qn][sem] = val
        waits.append((sem, val))

    def emit(self, qn, fn, reads=(), writes=(), accum_writes=(), sem=None, inc=1):
        if sem is None:
            sem = ENG_SEM[qn]
        waits = []
        for r in reads:
            e = self.reg.get(r)
            if e:
                for s_, v_ in e["w"].items():
                    self._dep(qn, s_, v_, waits)
        for w in list(writes) + list(accum_writes):
            e = self.reg.get(w)
            if e:
                for s_, v_ in e["w"].items():
                    self._dep(qn, s_, v_, waits)
                for s_, v_ in e["r"].items():
                    self._dep(qn, s_, v_, waits)
        if sem in DMA_SEMS:
            self._dep(qn, sem, 1, waits)
        self.cnt[sem] += inc
        cnt = self.cnt[sem]
        self.q[qn].append((tuple(sorted(set(waits))), fn, sem, inc))
        for r in reads:
            e = self.reg.setdefault(r, {"w": {}, "r": {}})
            e["r"][sem] = cnt
        for w in writes:
            self.reg[w] = {"w": {sem: cnt}, "r": {}}
        for w in accum_writes:
            e = self.reg.setdefault(w, {"w": {}, "r": {}})
            e["w"][sem] = cnt


def _qk(prefix, a, b):
    return [f"{prefix}{i}" for i in range(NTC) if a < (i + 1) * TW and b > i * TW]


def build(debug=False, ncores=8):
    nc = bass.Bass("TRN2", target_bir_lowering=False)

    x_d = nc.dram_tensor("x", [T, D], F32, kind="ExternalInput").ap()
    wffn_d = nc.dram_tensor("wffn", [D, D], F32, kind="ExternalInput").ap()
    wqkv_d = nc.dram_tensor("wqkv", [D, 384], F32, kind="ExternalInput").ap()
    wout_d = nc.dram_tensor("wout", [128, D], F32, kind="ExternalInput").ap()
    vecs_d = nc.dram_tensor("vecs", [128, 40], F32, kind="ExternalInput").ap()
    y_d = nc.dram_tensor("y", [D, T], F32, kind="ExternalOutput").ap()

    cc_in = nc.dram_tensor("cc_in", [1, 16], F32, kind="Internal").ap()
    bc_d = nc.dram_tensor("bc_d", [1, 8], F32, kind="Internal").ap()
    cc_out = nc.dram_tensor("cc_out", [1, 16], F32, kind="Internal").ap()

    ident_d = nc.inline_tensor(np.eye(128, dtype=np.float32), name="ident_c").ap()
    lhsbf_np = np.zeros((128, 128), dtype=np.float32)
    lhsbf_np[[0, 1, 32, 33, 64, 65], :] = 1.0
    import ml_dtypes
    lhsbf_d = nc.inline_tensor(lhsbf_np.astype(ml_dtypes.bfloat16),
                               name="lhsbf_c").ap()
    ones_col_d = nc.inline_tensor(np.ones((128, 1), dtype=np.float32),
                                  name="onesc_c").ap()
    onesb_d = nc.inline_tensor(np.ones((128, 1), dtype=ml_dtypes.bfloat16),
                               name="onesb_c").ap()

    dbg = {}
    if debug:
        for name, shape in [("dbg_h1", [D, T]), ("dbg_h2", [D, T]),
                            ("dbg_q", [128, T]), ("dbg_k", [128, T]),
                            ("dbg_v", [128, T]), ("dbg_gamma", [1, 16])]:
            dbg[name] = nc.dram_tensor(name, shape, F32, kind="ExternalOutput").ap()

    es = contextlib.ExitStack()
    with es:
        sbt = lambda name, shape, dt=F32: es.enter_context(
            nc.sbuf_tensor(name, shape, dt))
        h = [sbt(f"h{i}", [128, T]) for i in range(NDC)]
        pA = [sbt(f"pA{i}", [128, T], BF16) for i in range(NDC)]
        pB = [sbt(f"pB{i}", [128, T], BF16) for i in range(NDC)]
        wq = [sbt(f"wq{i}", [128, D], BF16) for i in range(NDC)]
        wqgh = [sbt(f"wqgh{i}", [128, 384], BF16) for i in range(NDC)]
        wqgl = [sbt(f"wqgl{i}", [128, 384], BF16) for i in range(NDC)]
        wouth = sbt("wouth", [128, D], BF16)
        woutl = sbt("woutl", [128, D], BF16)
        scrq = sbt("scrq", [128, T])
        scrke = sbt("scrke", [128, T])
        scrv = sbt("scrv", [128, T])
        xstage = [sbt(f"xstage{i}", [128, D]) for i in range(2)]
        rowpool = sbt("rowpool", [128, T])
        s1s2 = sbt("s1s2", [2, 384])
        stats = sbt("stats", [128, 128])
        vecs = sbt("vecs_sb", [128, 40])
        mv = sbt("mv", [128, 22])
        gmaxrow = sbt("gmaxrow", [1, 16])
        srow = sbt("srow", [1, 16])
        ident = sbt("ident", [128, 128])
        lhsbf = sbt("lhsbf", [128, 128], BF16)
        rowbf = sbt("rowbf", [128, T], BF16)
        statsbf = sbt("statsbf", [128, 32], BF16)
        ones_col = sbt("ones_col", [128, 1])
        onesb = sbt("onesb", [128, 1], BF16)
        bdm = sbt("bdm", [128, 128])
        ps = [es.enter_context(nc.psum_tensor(f"ps{i}", [128, TW], F32))
              for i in range(8)]
        sems = {n: es.enter_context(nc.semaphore(n))
                for n in ("SV", "SS", "ST", "SG") + DMA_SEMS}
        block = es.enter_context(nc.Block())

        (ST_S, ST_Q, ST_M, ST_VAR, ST_RF, ST_IRF, ST_NMRF,
         ST_R2) = (slice(16 * i, 16 * (i + 1)) for i in range(8))
        ST_T0 = ST_NMRF
        ST_NMR2 = ST_Q
        ST_R2P1 = ST_S
        V_BOUT, V_GFF, V_BFF, V_GAT, V_BAT = (
            slice(8 * i, 8 * (i + 1)) for i in range(5))
        MV_AM = slice(0, 8)
        MV_LMAX = slice(8, 9)
        MV_THR = slice(9, 10)
        MV_NTHR = slice(10, 11)
        MV_SC = slice(11, 12)
        MV_CBC = slice(12, 13)
        MV_NCBC = slice(13, 14)
        MV_KNEG = slice(14, 15)
        MV_SSUM = slice(15, 16)
        MV_RS = slice(16, 17)
        MV_AWT = slice(17, 18)
        MV_CS = slice(18, 19)
        MV_WRS = slice(19, 20)
        MV_BS = slice(20, 21)

        P = Plan()
        hr = lambda dc, tc: f"h{dc}_{tc}"
        hq = lambda dc: [hr(dc, tc) for tc in range(NTC)]
        tsl = lambda tc: slice(tc * TW, (tc + 1) * TW)

        P.emit("sync", lambda e: e.dma_start(out=ident[:], in_=ident_d[:]),
               writes=["ident"], sem="DW", inc=16)
        P.emit("sync", lambda e: e.dma_start(out=lhsbf[:], in_=lhsbf_d[:]),
               writes=["lhsbf"], sem="DW2", inc=16)
        P.emit("sync", lambda e: e.dma_start(out=ones_col[:], in_=ones_col_d[:]),
               writes=["ones_col"], sem="DW", inc=16)
        P.emit("sync", lambda e: e.dma_start(out=onesb[:], in_=onesb_d[:]),
               writes=["onesb"], sem="DW2", inc=16)
        P.emit("sync", lambda e: e.dma_start(out=vecs[:], in_=vecs_d[:]),
               writes=["vecs"], sem="DW", inc=16)
        P.emit("sync", lambda e: e.dma_start(out=scrv[:, 0:D], in_=wout_d[:]),
               writes=["v0", "v1"], sem="DW", inc=16)
        P.emit("s", lambda e: e.copy(wouth[:], scrv[:, 0:D]),
               reads=["v0", "v1"], writes=["wouth"])
        P.emit("v", lambda e: e.tensor_tensor(woutl[:], scrv[:, 0:D], wouth[:],
                                              OP.subtract),
               reads=["v0", "v1", "wouth"], writes=["woutl"])
        P.emit("v", lambda e: e.tensor_reduce(mv[:, MV_WRS], scrv[:, 0:D], AX,
                                              OP.add),
               reads=["v0", "v1"], writes=["wrs"])
        P.emit("v", lambda e: e.memset(gmaxrow[0:1, :], 0.0), writes=["gmaxrow"])
        P.emit("sync", lambda e: e.dma_start(out=cc_in[:], in_=gmaxrow[0:1, :]),
               reads=["gmaxrow"], writes=["ccin"], sem="DC", inc=16)
        P.emit("gp", lambda e: e.collective_compute(
                   "AllReduce", OP.max, replica_groups=[list(range(ncores))],
                   ins=[cc_in], outs=[cc_out]),
               reads=["ccin"], writes=["ccout"], sem="SC", inc=1)
        P.emit("v", lambda e: e.memset(bdm[:], 0.0), writes=["bd"])

        def xbuf(nt):
            if nt < 8:
                i = nt % 3
                if i == 2:
                    return rowpool[:, D:2 * D], ["xrp"]
                return xstage[i][:, :], [f"xst{i}"]
            idx = (nt - 8) % 4
            if idx == 0:
                return scrq[:, 0:D], ["q0", "q1"]
            if idx == 1:
                return scrq[:, D:2 * D], ["q2", "q3"]
            i = idx - 2
            return xstage[i][:, :], [f"xst{i}"]

        def xload(nt):
            buf, keys = xbuf(nt)
            if nt < 8:
                eng, sem = "gp", ("DX", "DX2", "DR")[nt % 3]
            else:
                eng, sem = "sync", ("XA", "XB", "XC", "XD")[(nt - 8) % 4]
            P.emit(eng,
                   lambda e, nt=nt, buf=buf: e.dma_start(
                       out=buf, in_=x_d[nt * 128:(nt + 1) * 128, :]),
                   writes=keys, sem=sem, inc=16)
        xload(0)
        xload(1)

        def pass1_dc(dc):
            a = (dc % 2) * D
            keys = _qk("q", a, a + D)
            wpa = [f"pa{dc}_0", f"pa{dc}_1"]
            wpb = [f"pb{dc}_0", f"pb{dc}_1"]
            buf, rkeys = (scrke, ["e2", "e3"]) if dc % 2 == 0 \
                else (scrv, ["v2", "v3"])
            P.emit("sync",
                   lambda e, dc=dc, a=a: e.dma_start(
                       out=scrq[:, a:a + D], in_=wffn_d[dc * 128:(dc + 1) * 128, :]),
                   writes=keys, sem=("DW", "DW2", "WQ", "WQ2")[dc % 4], inc=16)
            P.emit("s", lambda e, dc=dc, a=a: e.copy(pA[dc][:, 0:D],
                                                     scrq[:, a:a + D]),
                   reads=keys, writes=wpa)
            P.emit("v",
                   lambda e, dc=dc, a=a: e.tensor_reduce(
                       mv[:, MV_AM][:, dc:dc + 1], scrq[:, a:a + D], AX, OP.add,
                       apply_absolute_value=True),
                   reads=keys, writes=[f"am{dc}"])
            P.emit("v",
                   lambda e, dc=dc, a=a, buf=buf: e.tensor_tensor(
                       buf[:, D:2 * D], scrq[:, a:a + D], pA[dc][:, 0:D],
                       OP.subtract),
                   reads=keys + wpa, writes=rkeys)
            P.emit("s", lambda e, dc=dc, buf=buf: e.copy(
                       wq[dc][:, :], buf[:, D:2 * D]),
                   reads=rkeys, writes=[f"wq{dc}"])
            P.emit("v",
                   lambda e, dc=dc, buf=buf: e.tensor_tensor(
                       pB[dc][:, 0:D], buf[:, D:2 * D], wq[dc][:, :],
                       OP.subtract),
                   reads=rkeys + [f"wq{dc}"], writes=wpb)

        def beta_tail():
            P.emit("v", lambda e: e.tensor_reduce(mv[:, MV_AWT], mv[:, MV_AM], AX,
                                                  OP.add),
                   reads=[f"am{dc}" for dc in range(NDC)], writes=["awt"])
            P.emit("t", lambda e: e.transpose(ps[6][0:1, 0:128], mv[:, MV_AWT],
                                              ident[:]),
                   reads=["awt", "ident"], writes=["ps6"])
            P.emit("v", lambda e: e.tensor_reduce(srow[0:1, 0:1],
                                                  ps[6][0:1, 0:128], AX, OP.add),
                   reads=["ps6"], writes=["beta_s"])
            P.emit("v", lambda e: e.tensor_scalar(srow[0:1, 1:2], srow[0:1, 0:1],
                                                  1.0 / (D * D), None, OP.mult),
                   reads=["beta_s"], writes=["beta"])
            P.emit("v", lambda e: e.tensor_scalar(srow[0:1, 2:3], srow[0:1, 1:2],
                                                  EPS, 0.5, OP.add, OP.mult),
                   reads=["beta"], writes=["thr_s"])
            P.emit("v", lambda e: e.tensor_scalar(srow[0:1, 3:4], srow[0:1, 1:2],
                                                  0.5, None, OP.mult),
                   reads=["beta"], writes=["sc_s"])
            P.emit("v", lambda e: e.tensor_reduce(mv[:, MV_BS],
                                                  vecs[:, V_BOUT], AX, OP.add),
                   reads=["vecs"], writes=["bs_p"])
            P.emit("t", lambda e: e.transpose(ps[6][0:1, 0:128], mv[:, MV_BS],
                                              ident[:]),
                   reads=["bs_p", "ident"], writes=["ps6"])
            P.emit("v", lambda e: e.tensor_reduce(srow[0:1, 4:5],
                                                  ps[6][0:1, 0:128], AX, OP.add),
                   reads=["ps6"], writes=["bs_s"])
            P.emit("v", lambda e: e.tensor_scalar(srow[0:1, 4:5], srow[0:1, 4:5],
                                                  INV_D, None, OP.mult),
                   reads=["bs_s"], writes=["bs_s"])
            P.emit("sync", lambda e: e.dma_start(out=bc_d[0:1, 0:3],
                                                 in_=srow[0:1, 2:5]),
                   reads=["thr_s", "sc_s", "bs_s"], writes=["bc_d"],
                   sem="DC", inc=16)
            P.emit("sync", lambda e: e.dma_start(
                       out=mv[:, MV_BS],
                       in_=bc_d[0:1, 2:3].broadcast_to((128, 1))),
                   reads=["bc_d"], writes=["bs"], sem="DC", inc=16)
            P.emit("sync", lambda e: e.dma_start(
                       out=mv[:, MV_THR],
                       in_=bc_d[0:1, 0:1].broadcast_to((128, 1))),
                   reads=["bc_d"], writes=["thr"], sem="DC", inc=16)
            P.emit("sync", lambda e: e.dma_start(
                       out=mv[:, MV_SC],
                       in_=bc_d[0:1, 1:2].broadcast_to((128, 1))),
                   reads=["bc_d"], writes=["sc"], sem="DC", inc=16)
            P.emit("v", lambda e: e.tensor_scalar(mv[:, MV_NTHR], mv[:, MV_THR],
                                                  -1.0, None, OP.mult),
                   reads=["thr"], writes=["nthr"])

        def ternarize(dc):
            if True:
                buf, rkeys = (scrke, ["e2", "e3"]) if dc % 2 == 0 \
                    else (scrv, ["v2", "v3"])
                wpa = [f"pa{dc}_0", f"pa{dc}_1"]
                wpb = [f"pb{dc}_0", f"pb{dc}_1"]
                P.emit("gp",
                       lambda e, dc=dc, buf=buf: e.tensor_tensor(
                           buf[:, D:2 * D], pA[dc][:, 0:D], wq[dc][:, :],
                           OP.add),
                       reads=wpa + [f"wq{dc}"], writes=rkeys)
                P.emit("gp",
                       lambda e, dc=dc, buf=buf: e.tensor_tensor(
                           buf[:, D:2 * D], buf[:, D:2 * D], pB[dc][:, 0:D],
                           OP.add),
                       reads=rkeys + wpb, writes=rkeys)
                P.emit("v",
                       lambda e, dc=dc, buf=buf: e.tensor_scalar(
                           wq[dc][:, :], buf[:, D:2 * D], mv[:, MV_THR], -1.0,
                           OP.is_gt, OP.add),
                       reads=rkeys + ["thr", f"wq{dc}"], writes=[f"wq{dc}"])
                P.emit("v",
                       lambda e, dc=dc, buf=buf: e.scalar_tensor_tensor(
                           wq[dc][:, :], buf[:, D:2 * D], mv[:, MV_NTHR],
                           wq[dc][:, :], OP.is_gt, OP.add),
                       reads=rkeys + ["nthr", f"wq{dc}"], writes=[f"wq{dc}"])

        def wqkv_bundle(dc):
            a = (dc % 2) * 384
            akeys = _qk("v", a, a + 384)
            P.emit("sync",
                   lambda e, dc=dc, a=a: e.dma_start(
                       out=scrv[:, a:a + 384],
                       in_=wqkv_d[dc * 128:(dc + 1) * 128, :]),
                   writes=akeys, sem="WQ" if dc % 2 == 0 else "WQ2", inc=16)
            P.emit("s",
                   lambda e, dc=dc, a=a: e.activation(
                       rowpool[:, 0:384], scrv[:, a:a + 384], ACT.Copy,
                       scale=vecs[:, V_GAT][:, dc:dc + 1]),
                   reads=akeys + ["vecs", "wgf"], writes=["wgf"])
            P.emit("t",
                   lambda e, dc=dc: e.matmul(ps[6][0:1, 0:384], ones_col[:, 0:1],
                                             rowpool[:, 0:384],
                                             start=(dc == 0), stop=(dc == NDC - 1)),
                   reads=["wgf", "ones_col"],
                   writes=["ps6"] if dc == NDC - 1 else [],
                   accum_writes=["ps6"] if dc != NDC - 1 else [])
            P.emit("s", lambda e, dc=dc: e.copy(wqgh[dc][:, :],
                                                rowpool[:, 0:384]),
                   reads=["wgf"], writes=[f"wqgh{dc}"])
            P.emit("v", lambda e, dc=dc: e.tensor_tensor(
                       wqgl[dc][:, :], rowpool[:, 0:384], wqgh[dc][:, :],
                       OP.subtract),
                   reads=["wgf", f"wqgh{dc}"], writes=[f"wqgl{dc}"])
            P.emit("t",
                   lambda e, dc=dc, a=a: e.matmul(
                       ps[7][0:1, 0:384], vecs[:, V_BAT][:, dc:dc + 1],
                       scrv[:, a:a + 384], start=(dc == 0), stop=(dc == NDC - 1)),
                   reads=akeys + ["vecs"],
                   writes=["ps7"] if dc == NDC - 1 else [],
                   accum_writes=["ps7"] if dc != NDC - 1 else [])

        def wqkv_evacs():
            P.emit("v", lambda e: e.tensor_copy(s1s2[0:1, :], ps[6][0:1, 0:384]),
                   reads=["ps6"], writes=["s1"])
            P.emit("v", lambda e: e.tensor_copy(scrke[0:1, 1536:1920],
                                                ps[7][0:1, 0:384]),
                   reads=["ps7"], writes=["e3"])
            P.emit("gp", lambda e: e.dma_start(out=s1s2[1:2, :],
                                               in_=scrke[0:1, 1536:1920]),
                   reads=["e3"], writes=["s2"], sem="DR", inc=16)

        for nt in range(NNT):
            buf, keys = xbuf(nt)
            P.emit("v",
                   lambda e, nt=nt, buf=buf: e.tensor_reduce(
                       stats[:, ST_S][:, nt:nt + 1], buf, AX, OP.add),
                   reads=keys, accum_writes=["st_s"])
            P.emit("s",
                   lambda e, nt=nt, buf=buf: e.activation(
                       scrke[:, 0:D], buf, ACT.Square,
                       accum_out=stats[:, ST_Q][:, nt:nt + 1]),
                   reads=keys, writes=["e0", "e1"], accum_writes=["st_q"])
            for dc in range(NDC):
                slot = (nt * NDC + dc) % 6
                P.emit("t",
                       lambda e, buf=buf, dc=dc, slot=slot: e.transpose(
                           ps[slot][:, 0:128],
                           buf[:, dc * 128:(dc + 1) * 128], ident[:]),
                       reads=keys + ["ident"], writes=[f"ps{slot}"])
                if dc % 2 == 0:
                    P.emit("s",
                           lambda e, dc=dc, nt=nt, slot=slot: e.copy(
                               h[dc][:, nt::16], ps[slot][:, 0:128]),
                           reads=[f"ps{slot}"], accum_writes=hq(dc))
                else:
                    P.emit("v",
                           lambda e, dc=dc, nt=nt, slot=slot: e.tensor_copy(
                               h[dc][:, nt::16], ps[slot][:, 0:128]),
                           reads=[f"ps{slot}"], accum_writes=hq(dc))
            if nt + 2 < 8:
                xload(nt + 2)
            if nt < NDC:
                pass1_dc(nt)
            if nt == 7:
                beta_tail()
            if nt >= 8:
                wqkv_bundle(nt - 8)
                ternarize(nt - 8)
            if 8 <= nt + 2 < NNT:
                xload(nt + 2)

        _dr_i = [0]

        def _dr():
            _dr_i[0] += 1
            return "DR" if _dr_i[0] % 2 == 0 else "DR2"

        def rowshuffle_out(pslot, field, fkey, rkey):
            P.emit("gp", lambda e: e.dma_start(
                       out=rowpool[pslot:pslot + 1, :].rearrange(
                           "o (p c) -> o p c", c=16),
                       in_=stats[:, field]),
                   reads=[fkey, "wgf", "xrp"], writes=[rkey], sem=_dr(), inc=16)

        def rowshuffle_bf(pslot, field, bfield, fkey, rkey):
            b0 = slice(16 * bfield, 16 * bfield + 16)
            b1 = slice(16 * bfield + 16, 16 * bfield + 32)
            P.emit("v", lambda e: e.tensor_copy(statsbf[:, b0], stats[:, field]),
                   reads=[fkey], writes=[f"sbf{bfield}"])
            P.emit("v", lambda e: e.tensor_tensor(
                       statsbf[:, b1], stats[:, field], statsbf[:, b0],
                       OP.subtract),
                   reads=[fkey, f"sbf{bfield}"], writes=[f"sbf{bfield}l"])
            P.emit("gp", lambda e: e.dma_start(
                       out=rowbf[pslot:pslot + 1, :].rearrange(
                           "o (p c) -> o p c", c=16),
                       in_=statsbf[:, b0]),
                   reads=[f"sbf{bfield}"], writes=[rkey], sem=_dr(), inc=16)
            P.emit("gp", lambda e: e.dma_start(
                       out=rowbf[pslot + 1:pslot + 2, :].rearrange(
                           "o (p c) -> o p c", c=16),
                       in_=statsbf[:, b1]),
                   reads=[f"sbf{bfield}l"], accum_writes=[rkey], sem=_dr(), inc=16)

        def stats_math_common(mbias=False):
            if mbias:
                P.emit("v", lambda e: e.tensor_scalar(
                           stats[:, ST_M], stats[:, ST_S], INV_D,
                           mv[:, MV_BS], OP.mult, OP.add),
                       reads=["st_s", "bs"], writes=["st_m"])
            else:
                P.emit("v", lambda e: e.tensor_scalar(
                           stats[:, ST_M], stats[:, ST_S], INV_D, None,
                           OP.mult),
                       reads=["st_s"], writes=["st_m"])
            P.emit("v", lambda e: e.tensor_tensor(stats[:, ST_T0], stats[:, ST_M],
                                                  stats[:, ST_M], OP.mult),
                   reads=["st_m"], writes=["st_nmrf"])
            P.emit("v", lambda e: e.scalar_tensor_tensor(
                       stats[:, ST_VAR], stats[:, ST_Q], INV_D, stats[:, ST_T0],
                       OP.mult, OP.subtract),
                   reads=["st_q", "st_nmrf"], writes=["st_var"])

        def stats_math_ffn(mbias=False):
            stats_math_common(mbias)
            P.emit("v", lambda e: e.tensor_scalar(
                       stats[:, ST_IRF], stats[:, ST_VAR], 1.0 + 1e-3, 1e-8,
                       OP.mult, OP.add),
                   reads=["st_var"], writes=["st_irf"])
            P.emit("s", lambda e: e.activation(stats[:, ST_IRF], stats[:, ST_IRF],
                                               ACT.Ln),
                   reads=["st_irf"], writes=["st_irf"])
            P.emit("s", lambda e: e.activation(stats[:, ST_RF], stats[:, ST_IRF],
                                               ACT.Exp, scale=-0.5),
                   reads=["st_irf"], writes=["st_rf"])
            P.emit("v", lambda e: e.scalar_tensor_tensor(
                       stats[:, ST_NMRF], stats[:, ST_M], -1.0, stats[:, ST_RF],
                       OP.mult, OP.mult),
                   reads=["st_m", "st_rf"], writes=["st_nmrf"])
            rowshuffle_bf(32, ST_RF, 0, "st_rf", "row32")
            rowshuffle_bf(64, ST_NMRF, 0, "st_nmrf", "row64")

        def fm_stats(pairs=False, sum_skip=False, acc_sq=False):
            for dc in range(NDC):
                bb, bkeys = (scrq, [f"q{t}" for t in range(NTC)]) if dc % 2 == 0 \
                    else (scrke, [f"e{t}" for t in range(NTC)])
                if acc_sq:
                    pass
                elif dc in (5, 7):
                    P.emit("v", lambda e, dc=dc, bb=bb: e.tensor_tensor(
                               bb[:], h[dc][:], h[dc][:], OP.mult),
                           reads=hq(dc), writes=bkeys)
                elif dc in (4, 6):
                    P.emit("gp", lambda e, dc=dc, bb=bb: e.tensor_tensor(
                               bb[:], h[dc][:], h[dc][:], OP.mult),
                           reads=hq(dc), writes=bkeys)
                else:
                    P.emit("s", lambda e, dc=dc, bb=bb: e.activation(
                               bb[:], h[dc][:], ACT.Square),
                           reads=hq(dc), writes=bkeys)
                for tc in range(NTC):
                    if sum_skip:
                        pass
                    elif pairs:
                        def summ(e, dc=dc, tc=tc):
                            e.matmul(ps[tc][0:1, :], onesb[:, 0:1],
                                     pA[dc][:, tsl(tc)],
                                     start=(dc == 0), stop=False)
                            return e.matmul(ps[tc][0:1, :], onesb[:, 0:1],
                                            pB[dc][:, tsl(tc)],
                                            start=False, stop=(dc == NDC - 1))
                        P.emit("t", summ,
                               reads=[f"pa{dc}_{tc}", f"pb{dc}_{tc}", "onesb"],
                               writes=[f"ps{tc}"] if dc == NDC - 1 else [],
                               accum_writes=[f"ps{tc}"] if dc != NDC - 1 else [])
                    else:
                        P.emit("t",
                               lambda e, dc=dc, tc=tc: e.matmul(
                                   ps[tc][0:1, :], ones_col[:, 0:1],
                                   h[dc][:, tsl(tc)],
                                   start=(dc == 0), stop=(dc == NDC - 1)),
                               reads=[hr(dc, tc), "ones_col"],
                               writes=[f"ps{tc}"] if dc == NDC - 1 else [],
                               accum_writes=[f"ps{tc}"] if dc != NDC - 1 else [])
                    if not acc_sq:
                        P.emit("t",
                               lambda e, dc=dc, tc=tc, bb=bb: e.matmul(
                                   ps[4 + tc][0:1, :], ones_col[:, 0:1],
                                   bb[:, tsl(tc)], start=(dc == 0),
                                   stop=(dc == NDC - 1)),
                               reads=[bkeys[tc], "ones_col"],
                               writes=[f"ps{4 + tc}"] if dc == NDC - 1 else [],
                               accum_writes=[f"ps{4 + tc}"]
                               if dc != NDC - 1 else [])
            if acc_sq:
                for tc in range(NTC):
                    P.emit("t", lambda e, tc=tc: e.matmul(
                               ps[4 + tc][0:1, :], ones_col[:, 0:1],
                               scrq[:, tsl(tc)], start=True, stop=True),
                           reads=[f"q{tc}", "ones_col"],
                           writes=[f"ps{4 + tc}"])
            for tc in range(NTC):
                if not sum_skip:
                    P.emit("v", lambda e, tc=tc: e.tensor_copy(
                               rowpool[0:1, tsl(tc)], ps[tc][0:1, :]),
                           reads=[f"ps{tc}"], accum_writes=["row0", "xrp"])
                P.emit("v", lambda e, tc=tc: e.tensor_copy(
                           rowpool[32:33, tsl(tc)], ps[4 + tc][0:1, :]),
                       reads=[f"ps{4 + tc}"], accum_writes=["row32", "xrp"])
            P.emit("gp", lambda e: e.dma_start(
                       out=stats[:, ST_S],
                       in_=rowpool[0:1, :].rearrange("o (p c) -> o p c", c=16)),
                   reads=["row0"], writes=["st_s"], sem="DR2", inc=16)
            P.emit("gp", lambda e: e.dma_start(
                       out=stats[:, ST_Q],
                       in_=rowpool[32:33, :].rearrange("o (p c) -> o p c", c=16)),
                   reads=["row32"], writes=["st_q"], sem="DR", inc=16)

        def ffn(first, mid_cb=None, out_cb=None, split=False):
            for tc in range(NTC):
                P.emit("t", lambda e, tc=tc: e.matmul(
                           ps[tc][:, :], lhsbf[32:34, :], rowbf[32:34, tsl(tc)],
                           start=True, stop=True),
                       reads=["row32", "lhsbf"], writes=[f"ps{tc}"])
                bslot = 4 + (tc % 2)
                P.emit("t", lambda e, tc=tc, bslot=bslot: e.matmul(
                           ps[bslot][:, :], lhsbf[64:66, :], rowbf[64:66, tsl(tc)],
                           start=True, stop=True),
                       reads=["row64", "lhsbf"], writes=[f"ps{bslot}"])
                P.emit("v", lambda e, tc=tc, bslot=bslot: e.tensor_copy(
                           xstage[tc // 2][:, (tc % 2) * TW:(tc % 2) * TW + TW],
                           ps[bslot][:, :]),
                       reads=[f"ps{bslot}"], accum_writes=[f"xst{tc // 2}"])
            def prod_front(dc):
                bb, bkeys = (scrke, [f"e{t}" for t in range(NTC)]) if dc % 2 == 0 \
                    else (scrv, [f"v{t}" for t in range(NTC)])
                for tc in range(NTC):
                    P.emit("v", lambda e, dc=dc, tc=tc, bb=bb: e.tensor_tensor(
                               bb[:, tsl(tc)], h[dc][:, tsl(tc)],
                               ps[tc][:, :], OP.mult),
                           reads=[hr(dc, tc), f"ps{tc}"], writes=[bkeys[tc]])
                    if tc % 2 == 1:
                        P.emit("gp",
                               lambda e, tc=tc, bb=bb: e.tensor_tensor(
                                   bb[:, (tc - 1) * TW:(tc + 1) * TW],
                                   bb[:, (tc - 1) * TW:(tc + 1) * TW],
                                   xstage[tc // 2][:], OP.add),
                               reads=[bkeys[tc - 1], bkeys[tc], f"xst{tc // 2}"],
                               writes=[bkeys[tc - 1], bkeys[tc]])
            def prod_lo(dc):
                bb, bkeys = (scrke, [f"e{t}" for t in range(NTC)]) if dc % 2 == 0 \
                    else (scrv, [f"v{t}" for t in range(NTC)])
                P.emit("v",
                       lambda e, dc=dc, bb=bb: e.tensor_tensor(
                           pB[dc][:], bb[:], pA[dc][:], OP.subtract),
                       reads=bkeys + [f"pa{dc}_{t}" for t in range(NTC)],
                       writes=[f"pb{dc}_{t}" for t in range(NTC)])

            def prod_back(dc, with_lo=True):
                bb, bkeys = (scrke, [f"e{t}" for t in range(NTC)]) if dc % 2 == 0 \
                    else (scrv, [f"v{t}" for t in range(NTC)])
                P.emit("s", lambda e, dc=dc, bb=bb: e.mul(
                           bb[:], bb[:], vecs[:, V_GFF][:, dc:dc + 1]),
                       reads=bkeys + ["vecs"], writes=bkeys)
                P.emit("s", lambda e, dc=dc, bb=bb: e.add(
                           bb[:], bb[:], vecs[:, V_BFF][:, dc:dc + 1]),
                       reads=bkeys + ["vecs"], writes=bkeys)
                P.emit("v", lambda e, dc=dc, bb=bb: e.tensor_reduce(
                           mv[:, MV_AM][:, dc:dc + 1], bb[:], AX, OP.max,
                           apply_absolute_value=True),
                       reads=bkeys, writes=[f"am{dc}"])
                P.emit("s", lambda e, dc=dc, bb=bb: e.copy(pA[dc][:], bb[:]),
                       reads=bkeys, writes=[f"pa{dc}_{t}" for t in range(NTC)])
                if with_lo:
                    prod_lo(dc)

            for dc in range(NDC):
                prod_front(dc)
                if dc > 0:
                    prod_back(dc - 1)
            prod_back(NDC - 1, with_lo=False)
            def xr_slot(dc):
                if dc < 4:
                    return scrke, (dc % 4) * TW, f"e{dc % 4}"
                return scrv, (dc % 4) * TW, f"v{dc % 4}"

            def xr_stage(tc):
                for dc in range(NDC):
                    buf, xa, xrn = xr_slot(dc)
                    P.emit("gp", lambda e, dc=dc, tc=tc, buf=buf, xa=xa:
                           e.tensor_tensor(
                               buf[:, xa:xa + TW], pA[dc][:, tsl(tc)],
                               pB[dc][:, tsl(tc)], OP.add),
                           reads=[f"pa{dc}_{tc}", f"pb{dc}_{tc}"], writes=[xrn])

            def clip_stage(tc):
                for dc in range(NDC):
                    buf, xa, xrn = xr_slot(dc)
                    P.emit("v", lambda e, buf=buf, xa=xa: e.tensor_scalar(
                               buf[:, xa:xa + TW], buf[:, xa:xa + TW],
                               mv[:, MV_NCBC], mv[:, MV_CBC], OP.max, OP.min),
                           reads=[xrn, "ncbc", "cbc"], writes=[xrn])
                for dc in range(NDC):
                    buf, xa, xrn = xr_slot(dc)
                    P.emit("s", lambda e, dc=dc, tc=tc, buf=buf, xa=xa: e.copy(
                               pA[dc][:, tsl(tc)], buf[:, xa:xa + TW]),
                           reads=[xrn], writes=[f"pa{dc}_{tc}"])
                for dc in range(NDC):
                    buf, xa, xrn = xr_slot(dc)
                    P.emit("v", lambda e, dc=dc, tc=tc, buf=buf, xa=xa:
                           e.tensor_tensor(
                               pB[dc][:, tsl(tc)], buf[:, xa:xa + TW],
                               pA[dc][:, tsl(tc)], OP.subtract),
                           reads=[xrn, f"pa{dc}_{tc}"], writes=[f"pb{dc}_{tc}"])

            P.emit("v", lambda e: e.tensor_reduce(mv[:, MV_LMAX], mv[:, MV_AM],
                                                  AX, OP.max),
                   reads=[f"am{dc}" for dc in range(NDC)], writes=["lmax"])
            P.emit("t", lambda e: e.transpose(ps[0][0:1, 0:128], mv[:, MV_LMAX],
                                              ident[:]),
                   reads=["lmax", "ident"], writes=["ps0"])
            P.emit("v", lambda e: e.tensor_reduce(gmaxrow[0:1, 0:1],
                                                  ps[0][0:1, 0:128], AX, OP.max),
                   reads=["ps0"], writes=["gmaxrow"])
            P.emit("sync", lambda e: e.dma_start(out=cc_in[:], in_=gmaxrow[0:1, :]),
                   reads=["gmaxrow"], writes=["ccin"], sem="DC", inc=16)
            P.emit("gp", lambda e: e.collective_compute(
                       "AllReduce", OP.max, replica_groups=[list(range(ncores))],
                       ins=[cc_in], outs=[cc_out]),
                   reads=["ccin"], writes=["ccout"], sem="SC", inc=1)
            prod_lo(NDC - 1)
            xr_stage(0)
            if mid_cb is not None:
                mid_cb()
            if debug:
                P.emit("sync", lambda e: e.dma_start(out=gmaxrow[0:1, :],
                                                     in_=cc_out[:]),
                       reads=["ccout"], writes=["gmaxrow"], sem="DC", inc=16)
            if debug and first:
                P.emit("sync", lambda e: e.dma_start(out=dbg["dbg_gamma"],
                                                     in_=gmaxrow[0:1, :]),
                       reads=["gmaxrow"], sem="DO", inc=16)
            P.emit("sync", lambda e: e.dma_start(
                       out=mv[:, MV_CBC],
                       in_=cc_out[0:1, 0:1].broadcast_to((128, 1))),
                   reads=["ccout"], writes=["cbc"], sem="DC", inc=16)
            P.emit("v", lambda e: e.tensor_scalar(mv[:, MV_NCBC], mv[:, MV_CBC],
                                                  -K1, None, OP.mult),
                   reads=["cbc"], writes=["ncbc"])
            P.emit("v", lambda e: e.tensor_scalar(mv[:, MV_CBC], mv[:, MV_CBC],
                                                  K1, None, OP.mult),
                   reads=["cbc"], writes=["cbc"])

            clip_stage(0)
            for tc in range(NTC):
                for uc in range(NDC):
                    def mmgroup(e, uc=uc, tc=tc):
                        n = 0
                        last = None
                        for dc in range(NDC):
                            for pa in (pA, pB):
                                last = e.matmul(
                                    ps[uc][:, :],
                                    wq[dc][:, uc * 128:(uc + 1) * 128],
                                    pa[dc][:, tsl(tc)],
                                    start=(n == 0), stop=(n == 15))
                                n += 1
                        return last
                    P.emit("t", mmgroup,
                           reads=[f"pa{dc}_{tc}" for dc in range(NDC)] +
                                 [f"pb{dc}_{tc}" for dc in range(NDC)] +
                                 [f"wq{dc}" for dc in range(NDC)],
                           writes=[f"ps{uc}"])
                if tc + 1 < NTC:
                    xr_stage(tc + 1)
                    clip_stage(tc + 1)
                for uc in range(NDC):
                    P.emit("v", lambda e, uc=uc, tc=tc:
                           e.scalar_tensor_tensor(
                               h[uc][:, tsl(tc)], ps[uc][:, :], mv[:, MV_SC],
                               h[uc][:, tsl(tc)], OP.mult, OP.add),
                           reads=[f"ps{uc}", "sc", hr(uc, tc)],
                           writes=[hr(uc, tc)])
                    if out_cb is not None:
                        out_cb(uc, tc)
                    if split:
                        P.emit("s", lambda e, uc=uc, tc=tc: e.copy(
                                   pA[uc][:, tsl(tc)], h[uc][:, tsl(tc)]),
                               reads=[hr(uc, tc)], writes=[f"pa{uc}_{tc}"])
                        P.emit("v", lambda e, uc=uc, tc=tc: e.tensor_tensor(
                                   pB[uc][:, tsl(tc)], h[uc][:, tsl(tc)],
                                   pA[uc][:, tsl(tc)], OP.subtract),
                               reads=[hr(uc, tc), f"pa{uc}_{tc}"],
                               writes=[f"pb{uc}_{tc}"])
                        if uc == 0:
                            P.emit("s", lambda e, tc=tc: e.activation(
                                       scrq[:, tsl(tc)], h[0][:, tsl(tc)],
                                       ACT.Square),
                                   reads=[hr(0, tc)], writes=[f"q{tc}"])
                        else:
                            w = uc % 4
                            tmp = xstage[w // 2][:, (w % 2) * TW:
                                                 (w % 2) * TW + TW]
                            teng = ("v" if uc in (5, 7) else
                                    "gp" if uc in (2, 4, 6) else "s")
                            if teng == "s":
                                P.emit("s", lambda e, uc=uc, tc=tc, tmp=tmp:
                                       e.activation(tmp, h[uc][:, tsl(tc)],
                                                    ACT.Square),
                                       reads=[hr(uc, tc)],
                                       writes=[f"sqt{w}", f"xst{w // 2}"])
                            else:
                                P.emit(teng, lambda e, uc=uc, tc=tc, tmp=tmp:
                                       e.tensor_tensor(tmp, h[uc][:, tsl(tc)],
                                                       h[uc][:, tsl(tc)],
                                                       OP.mult),
                                       reads=[hr(uc, tc)],
                                       writes=[f"sqt{w}", f"xst{w // 2}"])
                            P.emit("v", lambda e, tc=tc, tmp=tmp:
                                   e.tensor_tensor(scrq[:, tsl(tc)],
                                                   scrq[:, tsl(tc)], tmp,
                                                   OP.add),
                                   reads=[f"q{tc}", f"sqt{w}"],
                                   writes=[f"q{tc}"])

        def stats_math_attn():
            stats_math_common()
            P.emit("v", lambda e: e.tensor_scalar(
                       stats[:, ST_R2], stats[:, ST_VAR], 1.0 + EPS, 1e-10,
                       OP.mult, OP.add),
                   reads=["st_var"], writes=["st_r2"])
            P.emit("s", lambda e: e.activation(stats[:, ST_R2], stats[:, ST_R2],
                                               ACT.Ln),
                   reads=["st_r2"], writes=["st_r2"])
            P.emit("s", lambda e: e.activation(stats[:, ST_IRF], stats[:, ST_R2],
                                               ACT.Exp, scale=0.5),
                   reads=["st_r2"], writes=["st_irf"])
            P.emit("s", lambda e: e.activation(stats[:, ST_RF], stats[:, ST_R2],
                                               ACT.Exp, scale=-0.5),
                   reads=["st_r2"], writes=["st_rf"])
            P.emit("v", lambda e: e.tensor_scalar(stats[:, ST_R2],
                                                  stats[:, ST_VAR], EPS, None,
                                                  OP.add),
                   reads=["st_var"], writes=["st_r2"])
            P.emit("s", lambda e: e.activation(stats[:, ST_R2], stats[:, ST_R2],
                                               ACT.Ln),
                   reads=["st_r2"], writes=["st_r2"])
            P.emit("s", lambda e: e.activation(stats[:, ST_R2], stats[:, ST_R2],
                                               ACT.Exp, scale=-0.5),
                   reads=["st_r2"], writes=["st_r2"])
            P.emit("v", lambda e: e.scalar_tensor_tensor(
                       stats[:, ST_NMR2], stats[:, ST_M], -1.0, stats[:, ST_R2],
                       OP.mult, OP.mult),
                   reads=["st_m", "st_r2"], writes=["st_nmr2"])
            P.emit("v", lambda e: e.tensor_scalar(stats[:, ST_R2P1],
                                                  stats[:, ST_R2], 1.0, None,
                                                  OP.add),
                   reads=["st_r2"], writes=["st_r2p1"])
            P.emit("v", lambda e: e.tensor_scalar(stats[:, ST_NMRF],
                                                  stats[:, ST_M], -1.0, None,
                                                  OP.mult),
                   reads=["st_m"], writes=["st_nmrf"])
            rowshuffle_out(0, ST_NMRF, "st_nmrf", "row0")
            rowshuffle_out(1, ST_IRF, "st_irf", "row1")
            rowshuffle_bf(32, ST_RF, 0, "st_rf", "row32")

        def attn():
            qbuf = {0: (scrq, "q"), 1: (scrke, "e"), 2: (scrv, "v")}
            slot_i = [0]
            for tc in range(NTC):
                bslot = 6 + (tc % 2)
                P.emit("t", lambda e, tc=tc, bslot=bslot: e.matmul(
                           ps[bslot][:, :], lhsbf[32:34, :],
                           rowbf[32:34, tsl(tc)], start=True, stop=True),
                       reads=["row32", "lhsbf"], writes=[f"ps{bslot}"])
                P.emit("v", lambda e, tc=tc, bslot=bslot: e.tensor_copy(
                           xstage[tc % 2][:, 0:TW], ps[bslot][:, :]),
                       reads=[f"ps{bslot}"], writes=[f"xst{tc % 2}"])
                for uc in range(3):
                    slot = slot_i[0] % 6
                    slot_i[0] += 1

                    def qkvgroup(e, uc=uc, tc=tc, slot=slot):
                        n = 0
                        for wg, pa in ((wqgh, pA), (wqgh, pB), (wqgl, pA)):
                            for dc in range(NDC):
                                e.matmul(ps[slot][:, :],
                                         wg[dc][:, uc * 128:(uc + 1) * 128],
                                         pa[dc][:, tsl(tc)],
                                         start=(n == 0), stop=False)
                                n += 1
                        return e.matmul(ps[slot][:, :],
                                        s1s2[0:2, uc * 128:(uc + 1) * 128],
                                        rowpool[0:2, tsl(tc)],
                                        start=False, stop=True)
                    P.emit("t", qkvgroup,
                           reads=[f"pa{dc}_{tc}" for dc in range(NDC)] +
                                 [f"pb{dc}_{tc}" for dc in range(NDC)] +
                                 [f"wqgh{dc}" for dc in range(NDC)] +
                                 [f"wqgl{dc}" for dc in range(NDC)] +
                                 ["s1", "s2", "row0", "row1"],
                           writes=[f"ps{slot}"])
                    buf, pre = qbuf[uc]
                    P.emit("v", lambda e, buf=buf, tc=tc, slot=slot:
                           e.tensor_tensor(buf[:, tsl(tc)], ps[slot][:, :],
                                           xstage[tc % 2][:, 0:TW], OP.mult),
                           reads=[f"ps{slot}", f"xst{tc % 2}"],
                           writes=[f"{pre}{tc}"])
            if debug:
                for pre, nmq, buf in (("q", "q", scrq), ("e", "k", scrke),
                                      ("v", "v", scrv)):
                    P.emit("sync", lambda e, nmq=nmq, buf=buf: e.dma_start(
                               out=dbg[f"dbg_{nmq}"], in_=buf[:]),
                           reads=[f"{pre}{tc}" for tc in range(NTC)],
                           sem="DO", inc=16)
            P.emit("v", lambda e: e.tensor_reduce(mv[:, MV_KNEG], scrke[:], AX,
                                                  OP.max, negate=True),
                   reads=[f"e{tc}" for tc in range(NTC)], writes=["kneg"])
            P.emit("s", lambda e: e.activation(scrke[:], scrke[:], ACT.Exp,
                                               bias=mv[:, MV_KNEG],
                                               accum_out=mv[:, MV_SSUM]),
                   reads=[f"e{tc}" for tc in range(NTC)] + ["kneg"],
                   writes=[f"e{tc}" for tc in range(NTC)] + ["ssum"])
            P.emit("v", lambda e: e.reciprocal(mv[:, MV_RS], mv[:, MV_SSUM]),
                   reads=["ssum"], writes=["rs"])
            P.emit("v", lambda e: e.tensor_scalar(mv[:, MV_CS], mv[:, MV_RS],
                                                  HSCALE, None, OP.mult),
                   reads=["rs"], writes=["cs"])
            for j in range(NNT):
                sl = slice(j * 128, (j + 1) * 128)
                tcj = (j * 128) // TW
                s0 = j % 4
                s1_ = 4 + (j % 2)
                P.emit("t", lambda e, sl=sl, s0=s0: e.transpose(
                           ps[s0][:, 0:128], scrke[:, sl], ident[:]),
                       reads=[f"e{tcj}", "ident"], writes=[f"ps{s0}"])
                P.emit("v", lambda e, sl=sl, s0=s0: e.tensor_copy(
                           scrke[:, sl], ps[s0][:, 0:128]),
                       reads=[f"ps{s0}"], accum_writes=[f"e{tcj}"])
                P.emit("t", lambda e, sl=sl, s1_=s1_: e.transpose(
                           ps[s1_][:, 0:128], scrv[:, sl], ident[:]),
                       reads=[f"v{tcj}", "ident"], writes=[f"ps{s1_}"])
                P.emit("s", lambda e, sl=sl, s1_=s1_: e.copy(
                           scrv[:, sl], ps[s1_][:, 0:128]),
                       reads=[f"ps{s1_}"], accum_writes=[f"v{tcj}"])

            def ctxgroup(e):
                last = None
                for j in range(NNT):
                    sl = slice(j * 128, (j + 1) * 128)
                    last = e.matmul(ps[6][:, 0:128], scrke[:, sl], scrv[:, sl],
                                    start=(j == 0), stop=(j == NNT - 1))
                return last
            P.emit("t", ctxgroup,
                   reads=[f"e{tc}" for tc in range(NTC)] +
                         [f"v{tc}" for tc in range(NTC)],
                   writes=["ps6"])
            for hd in range(4):
                b = 32 * hd
                P.emit("v", lambda e, b=b: e.tensor_scalar(
                           bdm[b:b + 32, b:b + 32], ps[6][b:b + 32, b:b + 32],
                           mv[:, MV_CS][b:b + 32, 0:1], None, OP.mult),
                       reads=["ps6", "cs"], accum_writes=["bd"])
            for tc in range(NTC):
                slot = tc % 2
                P.emit("t", lambda e, tc=tc, slot=slot: e.matmul(
                           ps[slot][:, :], bdm[:], scrq[:, tsl(tc)],
                           start=True, stop=True),
                       reads=["bd", f"q{tc}"], writes=[f"ps{slot}"])
                P.emit("v", lambda e, tc=tc, slot=slot: e.tensor_copy(
                           scrq[:, tsl(tc)], ps[slot][:, :]),
                       reads=[f"ps{slot}"], writes=[f"q{tc}"])
            for tc in range(NTC):
                P.emit("t", lambda e, tc=tc: e.matmul(
                           ps[0][0:1, :], mv[:, MV_WRS], scrq[:, tsl(tc)],
                           start=True, stop=True),
                       reads=["wrs", f"q{tc}"], writes=["ps0"])
                P.emit("v", lambda e, tc=tc: e.scalar_tensor_tensor(
                           rowpool[0:1, tsl(tc)], rowpool[0:1, tsl(tc)],
                           -float(D), ps[0][0:1, :], OP.mult, OP.add),
                       reads=["row0", "ps0"], writes=["row0"])
            rowshuffle_bf(32, ST_NMR2, 0, "st_nmr2", "row32")
            rowshuffle_bf(64, ST_R2P1, 0, "st_r2p1", "row64")
            for tc in range(NTC):
                slot = 2 + (tc % 2)
                P.emit("t", lambda e, tc=tc, slot=slot: e.matmul(
                           ps[slot][:, :], lhsbf[64:66, :],
                           rowbf[64:66, tsl(tc)], start=True, stop=True),
                       reads=["row64", "lhsbf"], writes=[f"ps{slot}"])
                P.emit("v", lambda e, tc=tc, slot=slot: e.tensor_copy(
                           scrke[:, tsl(tc)], ps[slot][:, :]),
                       reads=[f"ps{slot}"], writes=[f"e{tc}"])
            P.emit("s", lambda e: e.copy(pA[0][:], scrq[:]),
                   reads=[f"q{t}" for t in range(NTC)],
                   writes=[f"pa0_{t}" for t in range(NTC)])
            P.emit("v", lambda e: e.tensor_tensor(pB[0][:], scrq[:], pA[0][:],
                                                  OP.subtract),
                   reads=[f"q{t}" for t in range(NTC)] +
                         [f"pa0_{t}" for t in range(NTC)],
                   writes=[f"pb0_{t}" for t in range(NTC)])
            slot_i = [0]
            for tc in range(NTC):
                for uc in range(NDC):
                    slot = 4 + (slot_i[0] % 4)
                    slot_i[0] += 1

                    def opgroup(e, uc=uc, tc=tc, slot=slot):
                        sl = slice(uc * 128, (uc + 1) * 128)
                        e.matmul(ps[slot][:, :], wouth[:, sl],
                                 pA[0][:, tsl(tc)], start=True, stop=False)
                        e.matmul(ps[slot][:, :], wouth[:, sl],
                                 pB[0][:, tsl(tc)], start=False, stop=False)
                        e.matmul(ps[slot][:, :], woutl[:, sl],
                                 pA[0][:, tsl(tc)], start=False, stop=False)
                        return e.matmul(ps[slot][:, :], lhsbf[32:34, 0:128],
                                        rowbf[32:34, tsl(tc)],
                                        start=False, stop=True)
                    P.emit("t", opgroup,
                           reads=["wouth", "woutl", f"pa0_{tc}", f"pb0_{tc}",
                                  "row32", "lhsbf"],
                           writes=[f"ps{slot}"])
                    va = (uc % 4) * TW
                    P.emit("v", lambda e, uc=uc, tc=tc, va=va: e.tensor_tensor(
                               scrv[:, va:va + TW], h[uc][:, tsl(tc)],
                               scrke[:, tsl(tc)], OP.mult),
                           reads=[hr(uc, tc), f"e{tc}"], writes=[f"v{uc % 4}"])
                    P.emit("v", lambda e, uc=uc, tc=tc, slot=slot, va=va:
                           e.scalar_tensor_tensor(
                               h[uc][:, tsl(tc)], ps[slot][:, :],
                               vecs[:, V_BOUT][:, uc:uc + 1], scrv[:, va:va + TW],
                               OP.add, OP.add),
                           reads=[f"ps{slot}", "vecs", f"v{uc % 4}"],
                           writes=[hr(uc, tc)])
                    if uc == 0:
                        P.emit("s", lambda e, tc=tc: e.activation(
                                   scrq[:, tsl(tc)], h[0][:, tsl(tc)],
                                   ACT.Square),
                               reads=[hr(0, tc)], writes=[f"q{tc}"])
                    else:
                        w = uc % 4
                        tmp = xstage[w // 2][:, (w % 2) * TW:(w % 2) * TW + TW]
                        if uc in (2, 4, 6):
                            P.emit("gp", lambda e, uc=uc, tc=tc, tmp=tmp:
                                   e.tensor_tensor(tmp, h[uc][:, tsl(tc)],
                                                   h[uc][:, tsl(tc)], OP.mult),
                                   reads=[hr(uc, tc)],
                                   writes=[f"sqt{w}", f"xst{w // 2}"])
                        else:
                            P.emit("s", lambda e, uc=uc, tc=tc, tmp=tmp:
                                   e.activation(tmp, h[uc][:, tsl(tc)],
                                                ACT.Square),
                                   reads=[hr(uc, tc)],
                                   writes=[f"sqt{w}", f"xst{w // 2}"])
                        P.emit("v", lambda e, tc=tc, tmp=tmp:
                               e.tensor_tensor(scrq[:, tsl(tc)],
                                               scrq[:, tsl(tc)], tmp, OP.add),
                               reads=[f"q{tc}", f"sqt{w}"], writes=[f"q{tc}"])

        stats_math_ffn()
        wqkv_evacs()
        ffn(first=True, split=True)
        if debug:
            for dc in range(NDC):
                P.emit("sync", lambda e, dc=dc: e.dma_start(
                           out=dbg["dbg_h1"][dc * 128:(dc + 1) * 128, :],
                           in_=h[dc][:]),
                       reads=hq(dc), sem="DO", inc=16)
        fm_stats(pairs=True, acc_sq=True)
        stats_math_attn()
        attn()
        if debug:
            for dc in range(NDC):
                P.emit("sync", lambda e, dc=dc: e.dma_start(
                           out=dbg["dbg_h2"][dc * 128:(dc + 1) * 128, :],
                           in_=h[dc][:]),
                       reads=hq(dc), sem="DO", inc=16)
        fm_stats(sum_skip=True, acc_sq=True)
        stats_math_ffn(mbias=True)

        Y_SEMS = ("DO", "DO2", "XA", "XB")
        _yn = [0]

        def ydma(uc, tc):
            sem = Y_SEMS[_yn[0] % 4]
            _yn[0] += 1
            P.emit("sync", lambda e, uc=uc, tc=tc: e.dma_start(
                       out=y_d[uc * 128:(uc + 1) * 128, tc * TW:(tc + 1) * TW],
                       in_=h[uc][:, tsl(tc)]),
                   reads=[hr(uc, tc)], writes=[f"y{uc}_{tc}"], sem=sem, inc=16)

        ffn(first=False, out_cb=ydma)

        final_y = {s: P.cnt[s] for s in Y_SEMS}

        def make_runner(qname):
            items = P.q[qname]

            def run(eng):
                for waits, fn, sem, inc in items:
                    for s_, v_ in waits:
                        eng.wait_ge(sems[s_], v_)
                    last = fn(eng)
                    if last is not None:
                        last.then_inc(sems[sem], inc)
                if qname == "sync":
                    for s_, v_ in final_y.items():
                        eng.wait_ge(sems[s_], v_)
            return run

        block.sync(make_runner("sync"))
        block.gpsimd(make_runner("gp"))
        block.vector(make_runner("v"))
        block.scalar(make_runner("s"))
        block.tensor(make_runner("t"))

    return nc


_CACHE = {}


def _get_nc(debug=False, ncores=8):
    key = (debug, ncores)
    if key not in _CACHE:
        _CACHE[key] = build(debug, ncores)
    return _CACHE[key]


def _pack_vec(v):
    return np.ascontiguousarray(np.asarray(v, np.float32).reshape(8, 128).T)


def make_in_maps(L0_attn_x, L0_attn_gamma, L0_attn_beta, L0_attn_w_qkv,
                 L0_attn_w_out, L0_attn_b_out, L0_ffn_weight, L0_ffn_ln_gamma,
                 L0_ffn_ln_beta):
    x = np.asarray(L0_attn_x, np.float32)
    vecs = np.concatenate([
        _pack_vec(L0_attn_b_out), _pack_vec(L0_ffn_ln_gamma),
        _pack_vec(L0_ffn_ln_beta),
        _pack_vec(np.asarray(L0_attn_gamma).reshape(-1)),
        _pack_vec(np.asarray(L0_attn_beta).reshape(-1))], axis=1)
    vecs = np.ascontiguousarray(vecs)
    wffn = np.ascontiguousarray(np.asarray(L0_ffn_weight, np.float32))
    wqkv = np.ascontiguousarray(np.asarray(L0_attn_w_qkv, np.float32))
    wout = np.ascontiguousarray(np.asarray(L0_attn_w_out, np.float32))
    return [{"x": np.ascontiguousarray(x[c]), "wffn": wffn, "wqkv": wqkv,
             "wout": wout, "vecs": vecs} for c in range(8)]


def kernel(L0_attn_x, L0_attn_gamma, L0_attn_beta, L0_attn_w_qkv, L0_attn_w_out,
           L0_attn_b_out, L0_ffn_weight, L0_ffn_ln_gamma, L0_ffn_ln_beta):
    nc = _get_nc(False)
    in_maps = make_in_maps(L0_attn_x, L0_attn_gamma, L0_attn_beta, L0_attn_w_qkv,
                           L0_attn_w_out, L0_attn_b_out, L0_ffn_weight,
                           L0_ffn_ln_gamma, L0_ffn_ln_beta)
    res = run_bass_kernel_spmd(nc, in_maps, core_ids=list(range(8)))
    return np.stack(
        [np.asarray(r["y"]).reshape(D, 128, 16).transpose(2, 1, 0)
         .reshape(T, D) for r in res.results], axis=0)
